# version 02
# baseline (speedup 1.0000x reference)
import math
import contextlib
import numpy as np
import concourse.bass as bass
import concourse.mybir as mybir
from concourse.bass_utils import run_bass_kernel_spmd

F32 = mybir.dt.float32
BF16 = mybir.dt.bfloat16
AF = mybir.ActivationFunctionType
ALU = mybir.AluOpType
AX = mybir.AxisListType

ENGS = ("sync", "gpsimd", "vector", "scalar", "tensor")
NEG = -1e30
D = 1024



class P:

    def __init__(self):
        self.streams = {e: [] for e in ENGS}
        self.keys = {}
        self.dmac = {e: 0 for e in ENGS}
        self.drained = {e: 0 for e in ENGS}
        self.nid = 0

    def _edges(self, e, cell, r, w):
        deps = []
        seen = set()

        def add(d):
            if d is None or id(d) in seen:
                return
            seen.add(id(d))
            deps.append(d)
        for k in r:
            st = self.keys.get(k)
            if st:
                add(st["w"])
        for k in w:
            st = self.keys.get(k)
            if st:
                add(st["w"])
                for rd in st["r"]:
                    add(rd)
        for d in deps:
            if d["e"] == e == "tensor" and d["kind"] == "op":
                continue
            self._wait_cell(e, d)
        for k in r:
            self.keys.setdefault(k, {"w": None, "r": []})["r"].append(cell)
        for k in w:
            self.keys[k] = {"w": cell, "r": []}

    def _wait_cell(self, e, d):
        if d["kind"] == "dma":
            qe = d["e"]
            if self.drained[qe] < d["idx"]:
                self.drained[qe] = self.dmac[qe]
                st = self.streams[qe]
                pos = len(st)
                while pos > 0 and st[pos - 1][0] != "dma":
                    pos -= 1
                st.insert(pos, ("dwait", qe, self.dmac[qe] * 16))
            if e != qe:
                self.streams[e].append(("dwait", qe, self.drained[qe] * 16))
        else:
            if d["inc"] == 0:
                d["inc"] = 1
            self.streams[e].append(("wait", d))

    def do(self, e, fn, r=(), w=()):
        self.nid += 1
        cell = {"kind": "op", "e": e, "fn": fn, "inc": 0, "id": self.nid}
        self._edges(e, cell, tuple(r), tuple(w))
        self.streams[e].append(("op", cell))
        return cell

    def dma(self, qe, out, in_, r=(), w=()):
        self.nid += 1
        cell = {"kind": "dma", "e": qe, "idx": self.dmac[qe] + 1,
                "id": self.nid}
        self._edges(qe, cell, tuple(r), tuple(w))
        self.dmac[qe] += 1
        self.streams[qe].append(("dma", out, in_))
        return cell

    def read_key(self, e_or_none, k):
        st = self.keys.get(k)
        if st and st["w"] is not None:
            self._wait_cell(e_or_none, st["w"])

    def barrier(self):
        cells = {}
        for e in ENGS:
            cell = None
            for item in reversed(self.streams[e]):
                if item[0] == "op":
                    cell = item[1]
                    break
            if cell is None:
                self.nid += 1
                cell = {"kind": "op", "e": e, "inc": 0, "id": self.nid,
                        "fn": lambda eng: eng.nop()}
                self.streams[e].append(("op", cell))
            cell["inc"] += 1
            cells[e] = cell
        for e in ENGS:
            if self.dmac[e] > self.drained[e]:
                self.drained[e] = self.dmac[e]
                self.streams[e].append(("dwait", e, self.dmac[e] * 16))
        for e in ENGS:
            for o in ENGS:
                if o != e:
                    self.streams[e].append(("wait", cells[o]))
            for o in ENGS:
                if o != e and self.dmac[o]:
                    self.streams[e].append(("dwait", o, self.dmac[o] * 16))
        self.keys = {}

    def emit(self, nc):
        for e in ENGS:
            v = 0
            for item in self.streams[e]:
                if item[0] == "op" and item[1]["inc"]:
                    v += item[1]["inc"]
                    item[1]["val"] = v
        with contextlib.ExitStack() as stk:
            S = {}
            for e in ENGS:
                S[e] = stk.enter_context(nc.semaphore("sem_" + e))
                S[e + "@dma"] = stk.enter_context(nc.semaphore("dma_" + e))
            blk = stk.enter_context(nc.Block())

            def section(e):
                def run(eng):
                    hw = {}
                    for item in self.streams[e]:
                        kind = item[0]
                        if kind == "op":
                            cell = item[1]
                            ins = cell["fn"](eng)
                            if cell["inc"]:
                                ins.then_inc(S[e], cell["inc"])
                        elif kind == "dma":
                            eng.dma_start(out=item[1], in_=item[2]).then_inc(
                                S[e + "@dma"], 16)
                        elif kind == "wait":
                            cell = item[1]
                            sname = cell["e"]
                            v = cell["val"]
                            if hw.get(sname, -1) < v:
                                hw[sname] = v
                                eng.wait_ge(S[sname], v)
                        elif kind == "dwait":
                            sname = item[1] + "@dma"
                            v = item[2]
                            if hw.get(sname, -1) < v:
                                hw[sname] = v
                                eng.wait_ge(S[sname], v)
                return run

            blk.sync(section("sync"))
            blk.gpsimd(section("gpsimd"))
            blk.vector(section("vector"))
            blk.scalar(section("scalar"))
            blk.tensor(section("tensor"))


class Ctx:
    _n = 0

    def sbt(self, name, shape, dtype):
        Ctx._n += 1
        return self.nc.sbuf_tensor(f"{name}_{Ctx._n}", shape, dtype)



def alloc_psum_pool(cx, stk):
    cx.ps = [stk.enter_context(cx.nc.psum_tensor(f"psb{i}", [128, 512], F32))
             for i in range(7)]
    cx.psbf = stk.enter_context(cx.nc.psum_tensor("psbf", [128, 1024], BF16))
    cx.PSK = [("psum", i) for i in range(7)] + [("psum", "bf")]


def emit_shared_consts(cx, p, stk):
    nc = cx.nc
    cx.ones_bf = stk.enter_context(cx.sbt("ones_bf", [128, 1], BF16))
    cx.ones_f = stk.enter_context(cx.sbt("ones_f", [128, 1], F32))
    cx.ones_row_f = stk.enter_context(cx.sbt("ones_row_f", [1, 128], F32))
    cx.ones_row_bf = stk.enter_context(
        nc.sbuf_tensor("ones_row_bf", [1, 128], BF16))
    cx.id_bf = stk.enter_context(cx.sbt("id_bf", [128, 128], BF16))
    cx.id_f = stk.enter_context(cx.sbt("id_f", [128, 128], F32))
    cx.cst = stk.enter_context(cx.sbt("cst", [128, 8], F32))
    cx.CVALS = {1e-5: 0, 1e-6: 1, 1.0: 2, 0.0: 3, 1e-8: 4}

    def cval(v, parts=128):
        i = cx.CVALS[v]
        return cx.cst[0:parts, i:i + 1]
    cx.cval = cval
    p.dma("sync", cx.id_f[:], cx.dr["c_ident"], w=[("c", "idf")])
    for v, i in cx.CVALS.items():
        p.do("vector", lambda e, v=v, i=i: e.memset(cx.cst[:, i:i + 1], v),
             w=[("c", "cst")])
    p.do("vector", lambda e: e.memset(cx.ones_f[:], 1.0), w=[("c", "o")])
    p.do("vector", lambda e: e.memset(cx.ones_row_f[:], 1.0), w=[("c", "orf")])
    p.do("vector", lambda e: e.memset(cx.ones_row_bf[:], 1.0), w=[("c", "orb")])
    p.do("vector", lambda e: e.memset(cx.ones_bf[:], 1.0), w=[("c", "ob")])
    p.do("vector", lambda e: e.tensor_copy(out=cx.id_bf[:], in_=cx.id_f[:]),
         r=[("c", "idf")], w=[("c", "idb")])
    p.barrier()


def _alloc_ln_scratch(cx, stk, NT):
    cx.ln_xbf = stk.enter_context(cx.sbt("lnxbf", [128, 2 * NT], BF16))
    cx.ln_x2bf = stk.enter_context(cx.sbt("lnx2", [128, 2 * NT], BF16))
    cx.ln_m = stk.enter_context(cx.sbt("lnm", [1, NT], F32))
    cx.ln_mm2 = stk.enter_context(cx.sbt("lnmm2", [1, NT], F32))
    cx.ln_t1 = stk.enter_context(cx.sbt("lnt1", [1, NT], F32))
    cx.ln_tmp = stk.enter_context(cx.sbt("lntmp", [128, NT], F32))


def emit_ln_chunk(p, cx, xck, xc, ln_bf, lnk, NT, affine=None):
    ps_s, ps_s2, ps_b1, ps_b2 = cx.ps[0], cx.ps[1], cx.ps[2], cx.ps[3]
    K = cx.PSK
    xbf, x2bf = cx.ln_xbf, cx.ln_x2bf
    for j in range(8):
        sl = slice(j * NT, (j + 1) * NT)
        ds = slice((j % 2) * NT, (j % 2 + 1) * NT)
        p.do("vector", lambda e, sl=sl, ds=ds: e.tensor_copy(
            out=xbf[:, ds], in_=xc[:, sl]),
            r=[(xck, j)], w=[("lnxbf", j % 2)])
        p.do("scalar", lambda e, sl=sl, ds=ds: e.activation(
            out=x2bf[:, ds], in_=xc[:, sl], func=AF.Square),
            r=[(xck, j)], w=[("lnx2", j % 2)])
        p.do("tensor", lambda e, ds=ds, j=j: e.matmul(
            ps_s[0:1, 0:NT], lhsT=cx.ones_bf[:], rhs=xbf[:, ds],
            start=(j == 0), stop=(j == 7)),
            r=[("lnxbf", j % 2)], w=[K[0]])
        p.do("tensor", lambda e, ds=ds, j=j: e.matmul(
            ps_s2[0:1, 0:NT], lhsT=cx.ones_bf[:], rhs=x2bf[:, ds],
            start=(j == 0), stop=(j == 7)),
            r=[("lnx2", j % 2)], w=[K[1]])
    m, t1, mm2, tmp = cx.ln_m, cx.ln_t1, cx.ln_mm2, cx.ln_tmp
    p.do("vector", lambda e: e.tensor_scalar(
        out=m[0:1, 0:NT], in0=ps_s[0:1, 0:NT], scalar1=1.0 / D, scalar2=None,
        op0=ALU.mult), r=[K[0]], w=["ln_m"])
    p.do("vector", lambda e: e.tensor_scalar(
        out=t1[0:1, 0:NT], in0=ps_s2[0:1, 0:NT], scalar1=1.0 / D, scalar2=None,
        op0=ALU.mult), r=[K[1]], w=["ln_t1"])
    p.do("vector", lambda e: e.tensor_mul(mm2[0:1, 0:NT], m[0:1, 0:NT],
                                          m[0:1, 0:NT]),
         r=["ln_m"], w=["ln_mm2"])
    p.do("vector", lambda e: e.tensor_sub(t1[0:1, 0:NT], t1[0:1, 0:NT],
                                          mm2[0:1, 0:NT]),
         r=["ln_t1", "ln_mm2"], w=["ln_t1"])
    p.do("scalar", lambda e: e.activation(out=mm2[0:1, 0:NT], in_=t1[0:1, 0:NT],
                                          func=AF.Sqrt, bias=cx.cval(1e-5, 1)),
         r=["ln_t1"], w=["ln_mm2"])
    p.do("vector", lambda e: e.reciprocal(out=t1[0:1, 0:NT],
                                          in_=mm2[0:1, 0:NT]),
         r=["ln_mm2"], w=["ln_t1"])
    rstd = t1
    p.do("vector", lambda e: e.scalar_tensor_tensor(
        out=m[0:1, 0:NT], in0=m[0:1, 0:NT], scalar=-1.0,
        in1=rstd[0:1, 0:NT], op0=ALU.mult, op1=ALU.mult),
        r=["ln_m", "ln_t1"], w=["ln_m"])
    offs = m
    p.do("tensor", lambda e: e.matmul(
        ps_b1[:, 0:NT], lhsT=cx.ones_row_f[:], rhs=rstd[0:1, 0:NT],
        start=True, stop=True), r=["ln_t1"], w=[K[2]])
    p.do("tensor", lambda e: e.matmul(
        ps_b2[:, 0:NT], lhsT=cx.ones_row_f[:], rhs=offs[0:1, 0:NT],
        start=True, stop=True), r=["ln_m"], w=[K[3]])
    for j in range(8):
        sl = slice(j * NT, (j + 1) * NT)
        p.do("vector", lambda e, sl=sl: e.tensor_mul(
            tmp[:, 0:NT], xc[:, sl], ps_b1[:, 0:NT]),
            r=[(xck, j), K[2]], w=["ln_tmp"])
        if affine is None:
            p.do("vector", lambda e, sl=sl: e.tensor_add(
                ln_bf[:, sl], tmp[:, 0:NT], ps_b2[:, 0:NT]),
                r=["ln_tmp", K[3]], w=[(lnk, j)])
        else:
            g_col, b_col = affine
            p.do("vector", lambda e, sl=sl: e.tensor_add(
                tmp[:, 0:NT], tmp[:, 0:NT], ps_b2[:, 0:NT]),
                r=["ln_tmp", K[3]], w=["ln_tmp"])
            p.do("vector", lambda e, sl=sl, j=j: e.tensor_scalar(
                out=ln_bf[:, sl], in0=tmp[:, 0:NT],
                scalar1=g_col[:, j:j + 1], scalar2=b_col[:, j:j + 1],
                op0=ALU.mult, op1=ALU.add),
                r=["ln_tmp", ("c", "aff")], w=[(lnk, j)])


def load_x_chunk(p, cx, x_dram, xsrck, c, xc, xck, NT):
    for j in range(8):
        p.dma("sync", xc[:, j * NT:(j + 1) * NT],
              x_dram[j * 128:(j + 1) * 128, c * NT:(c + 1) * NT],
              r=[(xsrck, "dram")], w=[(xck, j)])



def emit_quantize(p, cx, w_dram, wq_dram, KT, N, tag):
    nel = KT * 128 * N
    wt, acc, gcol = cx.qz_wt, cx.qz_acc, cx.qz_gcol
    ps = cx.ps[4]
    K4 = cx.PSK[4]
    for i in range(KT):
        p.dma("sync", wt[:, 0:N], w_dram[i * 128:(i + 1) * 128, :],
              w=[("qzwt",)])
        p.do("vector", lambda e, i=i: e.tensor_reduce(
            out=acc[:, i:i + 1], in_=wt[:, 0:N], axis=AX.X, op=ALU.add,
            apply_absolute_value=True), r=[("qzwt",)], w=[("qzacc", i)])
    p.do("vector", lambda e: e.tensor_reduce(
        out=acc[:, KT:KT + 1], in_=acc[:, 0:KT], axis=AX.X, op=ALU.add),
        r=[("qzacc", i) for i in range(KT)], w=[("qzacc", "t")])
    p.do("vector", lambda e: e.tensor_copy(out=cx.qz_accbf[:],
                                           in_=acc[:, KT:KT + 1]),
         r=[("qzacc", "t")], w=[("qzaccbf",)])
    p.do("tensor", lambda e: e.matmul(ps[0:1, 0:1], lhsT=cx.ones_bf[:],
                                      rhs=cx.qz_accbf[:], start=True, stop=True),
         r=[("qzaccbf",)], w=[K4])
    p.do("vector", lambda e: e.tensor_scalar(
        out=cx.qz_g[0:1, 0:1], in0=ps[0:1, 0:1], scalar1=1.0 / nel,
        scalar2=1e-5, op0=ALU.mult, op1=ALU.add), r=[K4], w=[("qzg", 0)])
    p.do("vector", lambda e: e.reciprocal(out=cx.qz_g[0:1, 1:2],
                                          in_=cx.qz_g[0:1, 0:1]),
         r=[("qzg", 0)], w=[("qzg", 1)])
    p.do("tensor", lambda e: e.matmul(ps[:, 0:1], lhsT=cx.ones_row_f[:],
                                      rhs=cx.qz_g[0:1, 1:2], start=True,
                                      stop=True), r=[("qzg", 1)], w=[K4])
    p.do("vector", lambda e: e.tensor_copy(out=gcol[:], in_=ps[:, 0:1]),
         r=[K4], w=[("qzgcol",)])
    MAGIC = 12582912.0
    for i in range(KT):
        p.dma("sync", wt[:, 0:N], w_dram[i * 128:(i + 1) * 128, :],
              w=[("qzwt",)])
        p.do("vector", lambda e: e.tensor_scalar(
            out=wt[:, 0:N], in0=wt[:, 0:N], scalar1=gcol[:, 0:1],
            scalar2=MAGIC, op0=ALU.mult, op1=ALU.add),
            r=[("qzwt",), ("qzgcol",)], w=[("qzwt",)])
        p.do("vector", lambda e: e.tensor_scalar(
            out=wt[:, 0:N], in0=wt[:, 0:N], scalar1=MAGIC, scalar2=None,
            op0=ALU.subtract), r=[("qzwt",)], w=[("qzwt",)])
        p.do("vector", lambda e: e.tensor_scalar(
            out=cx.qz_wq[:, 0:N], in0=wt[:, 0:N], scalar1=-1.0, scalar2=1.0,
            op0=ALU.max, op1=ALU.min), r=[("qzwt",)], w=[("qzwq",)])
        p.dma("scalar", wq_dram[i * 128:(i + 1) * 128, :], cx.qz_wq[:, 0:N],
              r=[("qzwq",)], w=[(tag, "dram")])


def emit_l1ffn(p, cx, x_in, xk, x_out, xok, first):
    nc = cx.nc
    NT, NCH = cx.NT, cx.NCH
    if first:
        with contextlib.ExitStack() as qstk:
            cx.qz_wt = qstk.enter_context(cx.sbt("qzwt", [128, 4096], F32))
            cx.qz_wq = qstk.enter_context(cx.sbt("qzwq", [128, 4096], BF16))
            cx.qz_acc = qstk.enter_context(cx.sbt("qzacc", [128, 40], F32))
            cx.qz_accbf = qstk.enter_context(cx.sbt("qzabf", [128, 1], BF16))
            cx.qz_gcol = qstk.enter_context(cx.sbt("qzg", [128, 1], F32))
            cx.qz_g = qstk.enter_context(cx.sbt("qzg2", [1, 2], F32))
            emit_quantize(p, cx, cx.dr["L1_ffn_W1T"], cx.dr["s_wq1"], 8, 4096,
                          "wq1")
            emit_quantize(p, cx, cx.dr["L1_ffn_W2T"], cx.dr["s_wq2"], 32, 1024,
                          "wq2")
            p.barrier()
    with contextlib.ExitStack() as stk:
        w1 = stk.enter_context(cx.sbt("l1w1", [128, 8 * 4096], BF16))
        w2 = stk.enter_context(cx.sbt("l1w2", [128, 32 * 1024], BF16))
        b1c = stk.enter_context(cx.sbt("l1b1", [128, 32], F32))
        b2c = stk.enter_context(cx.sbt("l1b2", [128, 8], F32))
        xc = stk.enter_context(cx.sbt("l1xc", [128, 8 * NT], F32))
        lnb = stk.enter_context(cx.sbt("l1ln", [128, 8 * NT], BF16))
        hbf = stk.enter_context(cx.sbt("l1h", [128, 32 * NT], BF16))
        yac = stk.enter_context(cx.sbt("l1y", [128, 2 * NT], F32))
        _alloc_ln_scratch(cx, stk, NT)
        for i in range(8):
            p.dma("sync", w1[:, i * 4096:(i + 1) * 4096],
                  cx.dr["s_wq1"][i * 128:(i + 1) * 128, :],
                  r=[("wq1", "dram")], w=[("w1", i)])
        for i in range(32):
            p.dma("sync", w2[:, i * 1024:(i + 1) * 1024],
                  cx.dr["s_wq2"][i * 128:(i + 1) * 128, :],
                  r=[("wq2", "dram")], w=[("w2", i)])
        p.dma("sync", b1c[:], cx.dr["L1_ffn_b1c"], w=[("b1c",)])
        p.dma("sync", b2c[:], cx.dr["L1_ffn_b2c"], w=[("b2c",)])
        for c in range(NCH):
            load_x_chunk(p, cx, x_in, xk, c, xc, "xc", NT)
            emit_ln_chunk(p, cx, "xc", xc, lnb, "ln", NT)
            for hh in range(32):
                ps = cx.ps[hh % 2]
                pk = cx.PSK[hh % 2]
                for k in range(8):
                    p.do("tensor", lambda e, hh=hh, k=k, ps=ps: e.matmul(
                        ps[:, 0:NT], lhsT=w1[:, k * 4096 + hh * 128:
                                             k * 4096 + (hh + 1) * 128],
                        rhs=lnb[:, k * NT:(k + 1) * NT],
                        start=(k == 0), stop=(k == 7)),
                        r=[("w1", k), ("ln", k)], w=[pk])
                p.do("scalar", lambda e, hh=hh, ps=ps: e.activation(
                    out=hbf[:, hh * NT:(hh + 1) * NT], in_=ps[:, 0:NT],
                    func=AF.Relu, bias=b1c[:, hh:hh + 1]),
                    r=[pk, ("b1c",)], w=[("h", hh)])
            for dd in range(8):
                ps = cx.ps[2 + dd % 2]
                pk = cx.PSK[2 + dd % 2]
                ysl = slice((dd % 2) * NT, (dd % 2 + 1) * NT)
                for k in range(32):
                    p.do("tensor", lambda e, dd=dd, k=k, ps=ps: e.matmul(
                        ps[:, 0:NT], lhsT=w2[:, k * 1024 + dd * 128:
                                             k * 1024 + (dd + 1) * 128],
                        rhs=hbf[:, k * NT:(k + 1) * NT],
                        start=(k == 0), stop=(k == 31)),
                        r=[("w2", k), ("h", k)], w=[pk])
                p.do("vector", lambda e, dd=dd, ps=ps, ysl=ysl: e.tensor_scalar(
                    out=yac[:, ysl], in0=ps[:, 0:NT],
                    scalar1=b2c[:, dd:dd + 1], scalar2=0.5,
                    op0=ALU.add, op1=ALU.mult),
                    r=[pk, ("b2c",)], w=[("y", dd % 2)])
                p.do("vector", lambda e, dd=dd, ysl=ysl: e.tensor_add(
                    yac[:, ysl], yac[:, ysl], xc[:, dd * NT:(dd + 1) * NT]),
                    r=[("y", dd % 2), ("xc", dd)], w=[("y", dd % 2)])
                p.dma("scalar", x_out[dd * 128:(dd + 1) * 128,
                                      c * NT:(c + 1) * NT], yac[:, ysl],
                      r=[("y", dd % 2)], w=[(xok, "dram")])
        p.barrier()




def _routing_top2(p, cx, pk, ps, E, tt, kind, ceT_sb, cek, rt):
    ce, sc1, sc2, eqs = rt
    ek = ("rt_ce",)
    K = pk
    if kind == "l2":
        gbb = cx.gbb
        p.do("vector", lambda e, gbb=gbb: e.tensor_add(
            eqs[:, 0:E], ps[:, 0:E], gbb[:, 0:E]),
             r=[K, ("gbb",)], w=[("rt_lg",)])
        lg, lgk = eqs[:, 0:E], ("rt_lg",)

        def LG(sl):
            return lg[:, sl]
    else:
        lg, lgk = ps, K

        def LG(sl):
            return lg[:, sl]
    p.do("vector", lambda e: e.tensor_reduce(
        out=sc1[:, 0:1], in_=LG(slice(0, E)), axis=AX.X, op=ALU.max),
        r=[lgk], w=[("rt_m1",)])
    p.do("vector", lambda e: e.tensor_scalar(
        out=ce[:, 0:E], in0=LG(slice(0, E)), scalar1=sc1[:, 0:1], scalar2=None,
        op0=ALU.is_ge), r=[lgk, ("rt_m1",)], w=[ek])
    p.do("vector", lambda e: e.scalar_tensor_tensor(
        out=eqs[:, E:2 * E], in0=ce[:, 0:E], scalar=NEG, in1=LG(slice(0, E)),
        op0=ALU.mult, op1=ALU.add), r=[ek, lgk], w=[("rt_x2",)])
    p.do("vector", lambda e: e.tensor_reduce(
        out=sc1[:, 1:2], in_=eqs[:, E:2 * E], axis=AX.X, op=ALU.max),
        r=[("rt_x2",)], w=[("rt_m2",)])
    p.do("vector", lambda e: e.tensor_scalar(
        out=eqs[:, E:2 * E], in0=eqs[:, E:2 * E], scalar1=sc1[:, 1:2],
        scalar2=None, op0=ALU.is_ge),
        r=[("rt_x2",), ("rt_m2",)], w=[("rt_eq2",)])
    if kind == "l0":
        p.do("vector", lambda e: e.tensor_sub(sc1[:, 2:3], sc1[:, 1:2],
                                              sc1[:, 0:1]),
             r=[("rt_m1",), ("rt_m2",)], w=[("rt_d",)])
        p.do("scalar", lambda e: e.activation(out=sc2[:, 0:1], in_=sc1[:, 2:3],
                                              func=AF.Exp),
             r=[("rt_d",)], w=[("rt_e",)])
        p.do("vector", lambda e: e.tensor_scalar(
            out=sc2[:, 1:2], in0=sc2[:, 0:1], scalar1=1.0, scalar2=None,
            op0=ALU.add), r=[("rt_e",)], w=[("rt_den",)])
        p.do("vector", lambda e: e.reciprocal(out=sc2[:, 2:3],
                                              in_=sc2[:, 1:2]),
             r=[("rt_den",)], w=[("rt_r",)])
        p.do("vector", lambda e: e.tensor_mul(sc2[:, 3:4], sc2[:, 0:1],
                                              sc2[:, 2:3]),
             r=[("rt_e",), ("rt_r",)], w=[("rt_w2",)])
        p.do("vector", lambda e: e.tensor_scalar(
            out=ce[:, 0:E], in0=ce[:, 0:E], scalar1=sc2[:, 2:3], scalar2=None,
            op0=ALU.mult), r=[ek, ("rt_r",)], w=[ek])
        p.do("vector", lambda e: e.scalar_tensor_tensor(
            out=ce[:, 0:E], in0=eqs[:, E:2 * E], scalar=sc2[:, 3:4],
            in1=ce[:, 0:E], op0=ALU.mult, op1=ALU.add),
            r=[("rt_eq2",), ("rt_w2",), ek], w=[ek])
    elif kind == "l2":
        p.do("scalar", lambda e: e.activation(out=sc2[:, 0:1], in_=sc1[:, 0:1],
                                              func=AF.Sigmoid),
             r=[("rt_m1",)], w=[("rt_s1",)])
        p.do("scalar", lambda e: e.activation(out=sc2[:, 1:2], in_=sc1[:, 1:2],
                                              func=AF.Sigmoid),
             r=[("rt_m2",)], w=[("rt_s2",)])
        p.do("vector", lambda e: e.tensor_scalar(
            out=sc2[:, 2:3], in0=sc2[:, 1:2], scalar1=sc2[:, 0:1],
            scalar2=1e-8, op0=ALU.add, op1=ALU.add),
            r=[("rt_s1",), ("rt_s2",)], w=[("rt_den",)])
        p.do("vector", lambda e: e.reciprocal(out=sc2[:, 2:3],
                                              in_=sc2[:, 2:3]),
             r=[("rt_den",)], w=[("rt_den",)])
        p.do("vector", lambda e: e.tensor_mul(sc1[:, 2:3], sc2[:, 0:1],
                                              sc2[:, 2:3]),
             r=[("rt_s1",), ("rt_den",)], w=[("rt_w1",)])
        p.do("vector", lambda e: e.tensor_mul(sc1[:, 3:4], sc2[:, 1:2],
                                              sc2[:, 2:3]),
             r=[("rt_s2",), ("rt_den",)], w=[("rt_w2",)])
        p.do("vector", lambda e: e.tensor_scalar(
            out=ce[:, 0:E], in0=ce[:, 0:E], scalar1=sc1[:, 2:3], scalar2=None,
            op0=ALU.mult), r=[ek, ("rt_w1",)], w=[ek])
        p.do("vector", lambda e: e.scalar_tensor_tensor(
            out=ce[:, 0:E], in0=eqs[:, E:2 * E], scalar=sc1[:, 3:4],
            in1=ce[:, 0:E], op0=ALU.mult, op1=ALU.add),
            r=[("rt_eq2",), ("rt_w2",), ek], w=[ek])
    elif kind == "l3":
        p.do("vector", lambda e: e.tensor_reduce(
            out=sc2[:, 0:1], in_=LG(slice(0, E)), axis=AX.X, op=ALU.max,
            negate=True), r=[lgk], w=[("rt_nm",)])
        p.do("scalar", lambda e: e.activation(
            out=eqs[:, 0:E], in_=LG(slice(0, E)), func=AF.Exp,
            bias=sc2[:, 0:1]), r=[lgk, ("rt_nm",)], w=[("rt_p",)])
        p.do("vector", lambda e: e.tensor_reduce(
            out=sc2[:, 1:2], in_=eqs[:, 0:E], axis=AX.X, op=ALU.add),
            r=[("rt_p",)], w=[("rt_s",)])
        p.do("vector", lambda e: e.reciprocal(out=sc2[:, 2:3],
                                              in_=sc2[:, 1:2]),
             r=[("rt_s",)], w=[("rt_s",)])
        p.do("vector", lambda e: e.tensor_add(
            ce[:, 0:E], ce[:, 0:E], eqs[:, E:2 * E]),
            r=[ek, ("rt_eq2",)], w=[ek])
        p.do("vector", lambda e: e.tensor_mul(
            ce[:, 0:E], ce[:, 0:E], eqs[:, 0:E]),
            r=[ek, ("rt_p",)], w=[ek])
        p.do("vector", lambda e: e.tensor_scalar(
            out=ce[:, 0:E], in0=ce[:, 0:E], scalar1=sc2[:, 2:3], scalar2=None,
            op0=ALU.mult), r=[ek, ("rt_s",)], w=[ek])
    p.do("tensor", lambda e: e.transpose(cx.ps[6][0:E, 0:128], ce[:, 0:E],
                                         cx.id_f[:]),
         r=[ek], w=[cx.PSK[6]])
    p.do("vector", lambda e, tt=tt: e.tensor_copy(
        out=ceT_sb[0:E, tt[1] * 128:(tt[1] + 1) * 128],
        in_=cx.ps[6][0:E, 0:128]),
        r=[cx.PSK[6]], w=[("ceTs",)])


def emit_moe(p, cx, x_in, xk, x_out, xok, kind):
    nc = cx.nc
    NT, TT = cx.NT, cx.TT
    NSC = TT // NT
    if kind == "l0":
        E, H, EALL = 8, 4096, 8
    elif kind == "l3":
        E, H, EALL = 8, 4096, 9
    else:
        E, H, EALL = 6, 1024, 7
    HT = H // 128
    sb = cx.sbt

    with contextlib.ExitStack() as stk:
        xc = stk.enter_context(sb("m_xc", [128, 8 * NT], F32))
        lnr = stk.enter_context(sb("m_lnr", [128, 8 * NT], BF16))
        ceT = stk.enter_context(sb("m_ceT", [8, NT], F32))
        yri = stk.enter_context(sb("m_yt", [128, 2 * NT], F32))
        rt_ce = stk.enter_context(sb("m_rce", [128, E], F32))
        rt_s1 = stk.enter_context(sb("m_rs1", [128, 4], F32))
        rt_s2 = stk.enter_context(sb("m_rs2", [128, 4], F32))
        rt_eq = stk.enter_context(sb("m_req", [128, 2 * E], F32))
        wr = stk.enter_context(sb("m_wr", [128, 8 * E], BF16))
        _alloc_ln_scratch(cx, stk, NT)
        rt = (rt_ce, rt_s1, rt_s2, rt_eq)
        if kind == "l0":
            b2t = stk.enter_context(sb("m_b2", [8, 1024], F32))
            p.dma("sync", b2t[:], cx.dr["L0_ffn_b2t"], w=[("b2t",)])
            for k in range(8):
                p.dma("gpsimd", wr[:, k * E:(k + 1) * E],
                      cx.dr["L0_ffn_Wr"][k * 128:(k + 1) * 128, :],
                      w=[("wr",)])
        elif kind == "l3":
            for k in range(8):
                p.dma("gpsimd", wr[:, k * E:(k + 1) * E],
                      cx.dr["L3_ffn_router_w"][k * 128:(k + 1) * 128, :],
                      w=[("wr",)])
        else:
            gb = stk.enter_context(sb("m_gb", [1, 8], F32))
            cx.gbb = gbbl = stk.enter_context(sb("m_gbb", [128, 8], F32))
            p.dma("sync", gb[0:1, 0:6], cx.dr["L2_ffn_gate_b2"],
                  w=[("gb",)])
            for k in range(8):
                p.dma("gpsimd", wr[:, k * E:(k + 1) * E],
                      cx.dr["L2_ffn_gate_wT"][k * 128:(k + 1) * 128, 0:E],
                      w=[("wr",)])
            p.do("tensor", lambda e: e.matmul(
                cx.ps[6][:, 0:6], lhsT=cx.ones_row_f[:], rhs=gb[0:1, 0:6],
                start=True, stop=True), r=[("gb",)], w=[cx.PSK[6]])
            p.do("vector", lambda e, gbbl=gbbl: e.tensor_copy(
                out=gbbl[:, 0:6], in_=cx.ps[6][:, 0:6]),
                 r=[cx.PSK[6]], w=[("gbb",)])
        for sc in range(NSC):
            load_x_chunk(p, cx, x_in, xk, sc, xc, "xc", NT)
            emit_ln_chunk(p, cx, "xc", xc, lnr, "lnbuf0", NT)
            for j in range(8):
                p.dma("scalar", cx.dr["s_ln"][j * 128:(j + 1) * 128,
                                              sc * NT:(sc + 1) * NT],
                      lnr[:, j * NT:(j + 1) * NT],
                      r=[("lnbuf0", j)], w=[("lndram", sc, j)])
            for tt in range(NT // 128):
                K = cx.PSK[4]
                for k in range(8):
                    p.do("tensor", lambda e, k=k, tt=tt: e.matmul(
                        cx.ps[4][:, 0:E],
                        lhsT=lnr[:, k * NT + tt * 128:k * NT + (tt + 1) * 128],
                        rhs=wr[:, k * E:(k + 1) * E],
                        start=(k == 0), stop=(k == 7)),
                        r=[("lnbuf0", k), ("wr",)], w=[K])
                _routing_top2(p, cx, K, cx.ps[4], E, (sc, tt), kind, ceT,
                              None, rt)
            p.dma("scalar", cx.dr["s_ceT"][0:E, sc * NT:(sc + 1) * NT],
                  ceT[0:E, 0:NT],
                  r=[("ceTs",)], w=[("ceTd", sc)])
            if kind == "l0":
                for dd in range(8):
                    p.do("tensor", lambda e, dd=dd, sc=sc: e.matmul(
                        cx.ps[5][:, 0:NT], lhsT=b2t[:, dd * 128:(dd + 1) * 128],
                        rhs=ceT[0:E, 0:NT],
                        start=True, stop=True),
                        r=[("b2t",), ("ceTs",)], w=[cx.PSK[5]])
                    p.do("vector", lambda e, yri=yri: e.tensor_copy(
                        out=yri[:, 0:NT], in_=cx.ps[5][:, 0:NT]),
                        r=[cx.PSK[5]], w=[("yri",)])
                    p.dma("scalar", cx.dr["s_y"][dd * 128:(dd + 1) * 128,
                                                 sc * NT:(sc + 1) * NT],
                          yri[:, 0:NT], r=[("yri",)], w=[("ydram", dd, sc)])
    p.barrier()

    def wsrc(kind_, e):
        if kind == "l0":
            if kind_ == 1:
                return cx.dr["L0_ffn_W1"][e]
            return cx.dr["L0_ffn_W2"][e]
        if kind == "l3":
            if e < 8:
                if kind_ == 1:
                    return cx.dr["L3_ffn_routed_up"][e]
                return cx.dr["L3_ffn_routed_down"][e]
            if kind_ == 1:
                return cx.dr["L3_ffn_shared_up"][0]
            return cx.dr["L3_ffn_shared_down"][0]
        if e < 6:
            if kind_ == 1:
                return cx.dr["L2_ffn_ew1T"][e]
            if kind_ == 3:
                return cx.dr["L2_ffn_ew3T"][e]
            return cx.dr["L2_ffn_ew2T"][e]
        if kind_ == 1:
            return cx.dr["L2_ffn_fc1T"]
        if kind_ == 3:
            return cx.dr["L2_ffn_fc2T"]
        return cx.dr["L2_ffn_fc3T"]

    with contextlib.ExitStack() as stk:
        w1 = stk.enter_context(sb("m_w1", [128, 8 * H], BF16))
        w3 = None
        if kind == "l2":
            w3 = stk.enter_context(sb("m_w3", [128, 8 * H], BF16))
        w2 = stk.enter_context(sb("m_w2", [128, HT * 1024], BF16))
        hbf = stk.enter_context(sb("m_h", [128, HT * NT], BF16))
        lnt = stk.enter_context(sb("m_ln", [128, 2 * 8 * NT], BF16))
        yt = stk.enter_context(sb("m_y", [128, 2 * NT], F32))
        xt2 = stk.enter_context(sb("m_x2", [128, 2 * NT], F32))
        bc = stk.enter_context(sb("m_bc", [128, 2 * NT], F32))
        cerow = stk.enter_context(sb("m_cerow", [1, NT], F32))
        if kind == "l0":
            b1c = stk.enter_context(sb("m_b1", [128, 8 * 32], F32))
            p.dma("sync", b1c[:], cx.dr["L0_ffn_b1c"], w=[("b1c",)])
        for e_i in range(EALL):
            gated = e_i < E
            for i in range(8):
                p.dma("gpsimd", w1[:, i * H:(i + 1) * H],
                      wsrc(1, e_i)[i * 128:(i + 1) * 128, :], w=[("w1", i)])
                if kind == "l2":
                    p.dma("gpsimd", w3[:, i * H:(i + 1) * H],
                          wsrc(3, e_i)[i * 128:(i + 1) * 128, :],
                          w=[("w3", i)])
            for i in range(HT):
                p.dma("gpsimd", w2[:, i * 1024:(i + 1) * 1024],
                      wsrc(2, e_i)[i * 128:(i + 1) * 128, :], w=[("w2", i)])
            for sc in range(NSC):
                lsl = sc % 2
                lno = lsl * 8 * NT
                for j in range(8):
                    p.dma("sync", lnt[:, lno + j * NT:lno + (j + 1) * NT],
                          cx.dr["s_ln"][j * 128:(j + 1) * 128,
                                        sc * NT:(sc + 1) * NT],
                          r=[("lndram", sc, j)],
                          w=[("lnbuf" + str(lsl), j)])
                for hh in range(HT):
                    pk = cx.PSK[hh % 2]
                    ps = cx.ps[hh % 2]
                    for k in range(8):
                        p.do("tensor", lambda e, hh=hh, k=k, ps=ps, lno=lno:
                             e.matmul(
                                 ps[:, 0:NT],
                                 lhsT=w1[:, k * H + hh * 128:
                                         k * H + (hh + 1) * 128],
                                 rhs=lnt[:, lno + k * NT:lno + (k + 1) * NT],
                                 start=(k == 0), stop=(k == 7)),
                             r=[("w1", k), ("lnbuf" + str(lsl), k)],
                             w=[pk])
                    if kind == "l2":
                        pk3 = cx.PSK[2 + hh % 2]
                        ps3 = cx.ps[2 + hh % 2]
                        for k in range(8):
                            p.do("tensor", lambda e, hh=hh, k=k, ps3=ps3,
                                 lno=lno: e.matmul(
                                     ps3[:, 0:NT],
                                     lhsT=w3[:, k * H + hh * 128:
                                             k * H + (hh + 1) * 128],
                                     rhs=lnt[:, lno + k * NT:
                                             lno + (k + 1) * NT],
                                     start=(k == 0), stop=(k == 7)),
                                 r=[("w3", k),
                                    ("lnbuf" + str(lsl), k)], w=[pk3])
                        p.do("scalar", lambda e, hh=hh, ps=ps: e.activation(
                            out=hbf[:, hh * NT:(hh + 1) * NT], in_=ps[:, 0:NT],
                            func=AF.Silu), r=[pk], w=[("hs", hh)])
                        p.do("vector", lambda e, hh=hh, ps3=ps3: e.tensor_mul(
                            hbf[:, hh * NT:(hh + 1) * NT],
                            hbf[:, hh * NT:(hh + 1) * NT], ps3[:, 0:NT]),
                            r=[("hs", hh), pk3], w=[("h", hh)])
                    elif kind == "l0":
                        p.do("scalar", lambda e, hh=hh, ps=ps, e_i=e_i:
                             e.activation(
                                 out=hbf[:, hh * NT:(hh + 1) * NT],
                                 in_=ps[:, 0:NT], func=AF.Gelu,
                                 bias=b1c[:, e_i * 32 + hh:e_i * 32 + hh + 1]),
                             r=[pk, ("b1c",)], w=[("h", hh)])
                    else:
                        p.do("scalar", lambda e, hh=hh, ps=ps: e.activation(
                            out=hbf[:, hh * NT:(hh + 1) * NT], in_=ps[:, 0:NT],
                            func=AF.Gelu), r=[pk], w=[("h", hh)])
                if gated:
                    p.dma("sync", cerow[0:1, 0:NT],
                          cx.dr["s_ceT"][e_i:e_i + 1, sc * NT:(sc + 1) * NT],
                          r=[("ceTd", sc)], w=[("cerow",)])
                    p.do("tensor", lambda e: e.matmul(
                        cx.ps[6][:, 0:NT], lhsT=cx.ones_row_f[:],
                        rhs=cerow[0:1, 0:NT], start=True, stop=True),
                        r=[("cerow",)], w=[cx.PSK[6]])
                    p.do("vector", lambda e: e.tensor_copy(
                        out=bc[:, 0:NT], in_=cx.ps[6][:, 0:NT]),
                        r=[cx.PSK[6]], w=[("bc", "b")])
                for dd in range(8):
                    pk = cx.PSK[4 + dd % 2]
                    ps = cx.ps[4 + dd % 2]
                    ysl = slice((dd % 2) * NT, (dd % 2 + 1) * NT)
                    for k in range(HT):
                        p.do("tensor", lambda e, dd=dd, k=k, ps=ps: e.matmul(
                            ps[:, 0:NT], lhsT=w2[:, k * 1024 + dd * 128:
                                                 k * 1024 + (dd + 1) * 128],
                            rhs=hbf[:, k * NT:(k + 1) * NT],
                            start=(k == 0), stop=(k == HT - 1)),
                            r=[("w2", k), ("h", k)], w=[pk])
                    last = (e_i == EALL - 1)
                    first = (e_i == 0 and kind != "l0")
                    if not first:
                        p.dma("sync", yt[:, ysl],
                              cx.dr["s_y"][dd * 128:(dd + 1) * 128,
                                           sc * NT:(sc + 1) * NT],
                              r=[("ydram", dd, sc)], w=[("yt", dd % 2)])
                    if gated:
                        p.do("vector", lambda e, ps=ps: e.tensor_mul(
                            bc[:, NT:2 * NT], ps[:, 0:NT], bc[:, 0:NT]),
                            r=[pk, ("bc", "b")], w=[("bcm",)])
                        src_k = ("bcm",)

                        def SRC():
                            return bc[:, NT:2 * NT]
                    else:
                        src_k = pk

                        def SRC(ps=ps):
                            return ps[:, 0:NT]
                    if first:
                        p.do("vector", lambda e, ysl=ysl, S=SRC: e.tensor_copy(
                            out=yt[:, ysl], in_=S()),
                            r=[src_k], w=[("yt", dd % 2)])
                    else:
                        p.do("vector", lambda e, ysl=ysl, S=SRC: e.tensor_add(
                            yt[:, ysl], yt[:, ysl], S()),
                            r=[src_k, ("yt", dd % 2)], w=[("yt", dd % 2)])
                    if not last:
                        p.dma("scalar", cx.dr["s_y"][dd * 128:(dd + 1) * 128,
                                                     sc * NT:(sc + 1) * NT],
                              yt[:, ysl], r=[("yt", dd % 2)],
                              w=[("ydram", dd, sc)])
                    else:
                        xsl = slice((dd % 2) * NT, (dd % 2 + 1) * NT)
                        p.dma("sync", xt2[:, xsl],
                              x_in[dd * 128:(dd + 1) * 128,
                                   sc * NT:(sc + 1) * NT],
                              r=[(xk, "dram")], w=[("xt2", dd % 2)])
                        if kind == "l3":
                            lno2 = (sc % 2) * 8 * NT
                            p.do("vector", lambda e, ysl=ysl, dd=dd, lno2=lno2:
                                 e.tensor_add(
                                     yt[:, ysl], yt[:, ysl],
                                     lnt[:, lno2 + dd * NT:
                                         lno2 + (dd + 1) * NT]),
                                 r=[("yt", dd % 2),
                                    ("lnbuf" + str(sc % 2), dd)],
                                 w=[("yt", dd % 2)])
                        p.do("vector", lambda e, ysl=ysl, xsl=xsl: (
                            e.scalar_tensor_tensor(
                                out=yt[:, ysl], in0=yt[:, ysl], scalar=0.5,
                                in1=xt2[:, xsl], op0=ALU.mult, op1=ALU.add)),
                            r=[("yt", dd % 2), ("xt2", dd % 2)],
                            w=[("yt", dd % 2)])
                        p.dma("scalar", x_out[dd * 128:(dd + 1) * 128,
                                              sc * NT:(sc + 1) * NT],
                              yt[:, ysl], r=[("yt", dd % 2)],
                              w=[(xok, "dram")])
        p.barrier()




def emit_rms_chunk(p, cx, xck, xc, out_bf, outk, NT, w_col, eps):
    ps_s2, ps_b1 = cx.ps[0], cx.ps[1]
    K = cx.PSK
    x2bf = cx.ln_x2bf
    for j in range(8):
        sl = slice(j * NT, (j + 1) * NT)
        ds = slice((j % 2) * NT, (j % 2 + 1) * NT)
        p.do("scalar", lambda e, sl=sl, ds=ds: e.activation(
            out=x2bf[:, ds], in_=xc[:, sl], func=AF.Square),
            r=[(xck, j)], w=[("lnx2", j % 2)])
        p.do("tensor", lambda e, ds=ds, j=j: e.matmul(
            ps_s2[0:1, 0:NT], lhsT=cx.ones_bf[:], rhs=x2bf[:, ds],
            start=(j == 0), stop=(j == 7)),
            r=[("lnx2", j % 2)], w=[K[0]])
    m, t1 = cx.ln_m, cx.ln_t1
    p.do("vector", lambda e, m=m: e.tensor_scalar(
        out=m[0:1, 0:NT], in0=ps_s2[0:1, 0:NT], scalar1=1.0 / D, scalar2=None,
        op0=ALU.mult), r=[K[0]], w=["ln_m"])
    p.do("scalar", lambda e, m=m, t1=t1: e.activation(
        out=t1[0:1, 0:NT], in_=m[0:1, 0:NT], func=AF.Sqrt,
        bias=cx.cval(eps, 1)), r=["ln_m"], w=["ln_t1"])
    p.do("vector", lambda e, m=m, t1=t1: e.reciprocal(
        out=m[0:1, 0:NT], in_=t1[0:1, 0:NT]), r=["ln_t1"], w=["ln_m"])
    p.do("tensor", lambda e, m=m: e.matmul(
        ps_b1[:, 0:NT], lhsT=cx.ones_row_f[:], rhs=m[0:1, 0:NT],
        start=True, stop=True), r=["ln_m"], w=[K[1]])
    tmp = cx.ln_tmp
    for j in range(8):
        sl = slice(j * NT, (j + 1) * NT)
        p.do("vector", lambda e, sl=sl, tmp=tmp: e.tensor_mul(
            tmp[:, 0:NT], xc[:, sl], ps_b1[:, 0:NT]),
            r=[(xck, j), K[1]], w=["ln_tmp"])
        p.do("vector", lambda e, sl=sl, j=j, tmp=tmp: e.tensor_scalar(
            out=out_bf[:, sl], in0=tmp[:, 0:NT], scalar1=w_col[:, j:j + 1],
            scalar2=None, op0=ALU.mult),
            r=["ln_tmp", ("c", "wcol")], w=[(outk, j)])


def _proj_fm(p, cx, w_sb, wk, src_sb, srck, dst, dstk, sc, NT, MT, kn=8,
             evac="bf"):
    for mi in range(MT):
        pk = cx.PSK[2 + mi % 2]
        ps = cx.ps[2 + mi % 2]
        for k in range(kn):
            p.do("tensor", lambda e, mi=mi, k=k, ps=ps: e.matmul(
                ps[:, 0:NT],
                lhsT=w_sb[:, (k * MT + mi) * 128:(k * MT + mi + 1) * 128],
                rhs=src_sb[:, k * NT:(k + 1) * NT],
                start=(k == 0), stop=(k == kn - 1)),
                r=[(wk, k), (srck, k)], w=[pk])
        evac(ps, pk, mi)


def emit_l2attn(p, cx, x_in, xk, x_out, xok):
    nc = cx.nc
    NT, TT, T = cx.NT, cx.TT, cx.T
    NSC = TT // NT
    NQC = T // NT
    NKT = T // 128
    sb = cx.sbt
    with contextlib.ExitStack() as stk:
        xc = stk.enter_context(sb("a2xc", [128, 8 * NT], F32))
        lnb = stk.enter_context(sb("a2ln", [128, 8 * NT], BF16))
        wq = stk.enter_context(sb("a2wq", [128, 8 * 8 * 128], BF16))
        wk_ = stk.enter_context(sb("a2wk", [128, 8 * 8 * 128], BF16))
        wv = stk.enter_context(sb("a2wv", [128, 8 * 1024], BF16))
        qf = stk.enter_context(sb("a2qf", [128, 2 * NT], BF16))
        rc = stk.enter_context(sb("a2rc", [128, NT], F32))
        rs = stk.enter_context(sb("a2rs", [128, NT], F32))
        swp = stk.enter_context(sb("a2swp", [128, 128], BF16))
        t1b = stk.enter_context(sb("a2t1", [128, NT], F32))
        t2b = stk.enter_context(sb("a2t2", [128, NT], F32))
        _alloc_ln_scratch(cx, stk, NT)
        p.dma("gpsimd", swp[:], cx.dr["c_swp"], w=[("swp",)])
        for k in range(8):
            p.dma("gpsimd", wq[:, k * 1024:(k + 1) * 1024],
                  cx.dr["L2_attn_w_qkv"][k * 128:(k + 1) * 128, 0:1024],
                  w=[("wq", k)])
            p.dma("gpsimd", wk_[:, k * 1024:(k + 1) * 1024],
                  cx.dr["L2_attn_w_qkv"][k * 128:(k + 1) * 128, 1024:2048],
                  w=[("wk", k)])
            p.dma("gpsimd", wv[:, k * 1024:(k + 1) * 1024],
                  cx.dr["L2_attn_w_qkv"][k * 128:(k + 1) * 128, 2048:3072],
                  w=[("wv",)])
        for sc in range(NSC):
            load_x_chunk(p, cx, x_in, xk, sc, xc, "xc", NT)
            emit_ln_chunk(p, cx, "xc", xc, lnb, "ln", NT)
            for j in range(8):
                p.dma("scalar", cx.dr["s_ln"][j * 128:(j + 1) * 128,
                                              sc * NT:(sc + 1) * NT],
                      lnb[:, j * NT:(j + 1) * NT],
                      r=[("ln", j)], w=[("lndram", sc, j)])

            def qk_evac(which, wbuf):
                scl = 0.125 if which == "q" else 1.0

                def ev(ps, pk, mi):
                    rope = (mi % 2 == 1)
                    dstd = cx.dr["s_p1" if which == "q" else "s_p2"]
                    if not rope:
                        p.do("vector", lambda e, ps=ps: e.tensor_scalar(
                            out=qf[:, 0:NT], in0=ps[:, 0:NT], scalar1=scl,
                            scalar2=None, op0=ALU.mult),
                            r=[pk], w=[("qf", 0)])
                        p.dma("scalar", dstd[mi * 128:(mi + 1) * 128,
                                            sc * NT:(sc + 1) * NT],
                              qf[:, 0:NT], r=[("qf", 0)],
                              w=[(which + "d", mi, sc)])
                    else:
                        p.do("vector", lambda e, ps=ps: e.tensor_copy(
                            out=qf[:, 0:NT], in_=ps[:, 0:NT]),
                            r=[pk], w=[("qf", 0)])
                        p.do("tensor", lambda e: e.matmul(
                            cx.ps[6][:, 0:NT], lhsT=swp[:],
                            rhs=qf[:, 0:NT], start=True, stop=True),
                            r=[("swp",), ("qf", 0)], w=[cx.PSK[6]])
                        p.do("vector", lambda e, sc=sc: e.tensor_mul(
                            t1b[:, 0:NT], qf[:, 0:NT],
                            rc[:, 0:NT]), r=[("qf", 0), ("rc",)],
                            w=[("t1b",)])
                        p.do("vector", lambda e, sc=sc: e.tensor_mul(
                            t2b[:, 0:NT], cx.ps[6][:, 0:NT], rs[:, 0:NT]),
                            r=[cx.PSK[6], ("rs",)], w=[("t2b",)])
                        p.do("vector", lambda e: e.scalar_tensor_tensor(
                            out=qf[:, NT:2 * NT], in0=t1b[:, 0:NT],
                            scalar=1.0, in1=t2b[:, 0:NT],
                            op0=ALU.mult, op1=ALU.add),
                            r=[("t1b",), ("t2b",)], w=[("qf", 1)])
                        if scl != 1.0:
                            p.do("vector", lambda e: e.tensor_scalar(
                                out=qf[:, NT:2 * NT], in0=qf[:, NT:2 * NT],
                                scalar1=scl, scalar2=None, op0=ALU.mult),
                                r=[("qf", 1)], w=[("qf", 1)])
                        p.dma("scalar", dstd[mi * 128:(mi + 1) * 128,
                                             sc * NT:(sc + 1) * NT],
                              qf[:, NT:2 * NT], r=[("qf", 1)],
                              w=[(which + "d", mi, sc)])
                return ev
            p.dma("sync", rc[:, 0:NT],
                  cx.dr["c_ropec"][:, sc * NT:(sc + 1) * NT], w=[("rc",)])
            p.dma("sync", rs[:, 0:NT],
                  cx.dr["c_ropes"][:, sc * NT:(sc + 1) * NT], w=[("rs",)])
            _proj_fm(p, cx, wq, "wq", lnb, "ln", None, None, sc, NT, 8,
                     evac=qk_evac("q", wq))
            _proj_fm(p, cx, wk_, "wk", lnb, "ln", None, None, sc, NT, 8,
                     evac=qk_evac("k", wk_))
            for tt in range(NT // 128):
                for nn in range(2):
                    pk = cx.PSK[4 + nn]
                    ps = cx.ps[4 + nn]
                    for k in range(8):
                        p.do("tensor", lambda e, tt=tt, nn=nn, k=k, ps=ps:
                             e.matmul(
                                 ps[:, 0:512],
                                 lhsT=lnb[:, k * NT + tt * 128:
                                          k * NT + (tt + 1) * 128],
                                 rhs=wv[:, k * 1024 + nn * 512:
                                        k * 1024 + (nn + 1) * 512],
                                 start=(k == 0), stop=(k == 7)),
                             r=[("ln", k), ("wv",)], w=[pk])
                    p.do("vector", lambda e, ps=ps: e.tensor_copy(
                        out=qf[:, 0:NT], in_=ps[:, 0:512]),
                        r=[pk], w=[("qf", 0)])
                    p.dma("scalar",
                          cx.dr["s_vtm"][sc * NT + tt * 128:
                                         sc * NT + (tt + 1) * 128,
                                         nn * 512:(nn + 1) * 512],
                          qf[:, 0:NT], r=[("qf", 0)], w=[("vd", sc, tt, nn)])
    p.barrier()
    with contextlib.ExitStack() as stk:
        qT = stk.enter_context(sb("a2qT", [64, T], BF16))
        kT = stk.enter_context(sb("a2kT", [64, T], BF16))
        v65 = stk.enter_context(sb("a2v65", [128, NKT * 65], BF16))
        q2 = stk.enter_context(sb("a2q2", [64, T], BF16))
        nrm = stk.enter_context(sb("a2nrm", [1, 2 * T + 8], F32))
        akc = stk.enter_context(sb("a2akc", [128, 16 * 16], F32))
        kposc = stk.enter_context(sb("a2kpos", [128, 16], F32))
        lxi = stk.enter_context(sb("a2lxi", [1, 16], F32))
        ebt = stk.enter_context(sb("a2ebt", [1, 16], F32))
        arow = stk.enter_context(sb("a2arow", [1, 16], F32))
        rsum = stk.enter_context(sb("a2rsum", [1, 512], F32))
        acb = stk.enter_context(sb("a2acb", [128, 16], F32))
        mbs = stk.enter_context(sb("a2mbs", [128, NT], F32))
        dmt = stk.enter_context(sb("a2dmt", [128, 4 * 512], F32))
        pT = stk.enter_context(sb("a2pT", [128, NKT * 512], BF16))
        tob = stk.enter_context(sb("a2to", [64, 512], F32))
        ofm = stk.enter_context(sb("a2ofm", [64, 512], BF16))
        p.dma("sync", lxi[:], cx.dr["L2_attn_log_xi2"], w=[("lxi",)])
        p.dma("sync", ebt[:], cx.dr["c_ebits"], w=[("ebits",)])
        p.dma("sync", kposc[:], cx.dr["c_kposc"], w=[("kposc",)])
        p.dma("sync", dmt[:], cx.dr["c_dm2t"], w=[("dmt",)])
        p.do("scalar", lambda e: e.activation(
            out=arow[0:1, 0:16], in_=lxi[0:1, 0:16], func=AF.Exp, scale=-1.0),
            r=[("lxi",)], w=[("arow", 0)])
        p.do("vector", lambda e: e.tensor_mul(
            arow[0:1, 0:16], arow[0:1, 0:16], ebt[0:1, 0:16]),
            r=[("arow", 0), ("ebits",)], w=[("arow", 0)])
        p.do("tensor", lambda e: e.matmul(
            cx.ps[6][:, 0:16], lhsT=cx.ones_row_f[:], rhs=arow[0:1, 0:16],
            start=True, stop=True), r=[("arow", 0)], w=[cx.PSK[6]])
        p.do("vector", lambda e: e.tensor_copy(out=acb[:], in_=cx.ps[6][:, 0:16]),
             r=[cx.PSK[6]], w=[("acb",)])
        for h in range(16):
            p.do("vector", lambda e, h=h: e.tensor_scalar(
                out=akc[:, h * 16:(h + 1) * 16], in0=kposc[:],
                scalar1=acb[:, h:h + 1], scalar2=None, op0=ALU.mult),
                r=[("kposc",), ("acb",)], w=[("akc",)])
        for b in range(2):
            for h in range(16):
                p.dma("sync", qT[:], cx.dr["s_p1"][h * 64:(h + 1) * 64,
                                                   b * T:(b + 1) * T],
                      r=[("qd", h // 2, "x")], w=[("qT",)])
                p.dma("sync", kT[:], cx.dr["s_p2"][h * 64:(h + 1) * 64,
                                                   b * T:(b + 1) * T],
                      r=[("kd", h // 2, "x")], w=[("kT",)])
                p.do("vector", lambda e: e.memset(v65[:], 1.0), w=[("v65",)])
                for kt in range(NKT):
                    p.dma("sync", v65[:, kt * 65:kt * 65 + 64],
                          cx.dr["s_vtm"][b * T + kt * 128:
                                         b * T + (kt + 1) * 128,
                                         h * 64:(h + 1) * 64],
                          r=[("vd", "x")], w=[("v65",)])
                p.do("scalar", lambda e: e.activation(
                    out=q2[:], in_=qT[:], func=AF.Square),
                    r=[("qT",)], w=[("q2",)])
                for c in range(T // 512):
                    p.do("tensor", lambda e, c=c: e.matmul(
                        cx.ps[6][0:1, 0:512],
                        lhsT=cx.ones_bf[0:64, :],
                        rhs=q2[:, c * 512:(c + 1) * 512],
                        start=True, stop=True), r=[("q2",)], w=[cx.PSK[6]])
                    p.do("scalar", lambda e, c=c: e.activation(
                        out=nrm[0:1, c * 512:(c + 1) * 512],
                        in_=cx.ps[6][0:1, 0:512], func=AF.Sqrt),
                        r=[cx.PSK[6]], w=[("nq",)])
                p.do("scalar", lambda e: e.activation(
                    out=q2[:], in_=kT[:], func=AF.Square),
                    r=[("kT",)], w=[("q2",)])
                for c in range(T // 512):
                    p.do("tensor", lambda e, c=c: e.matmul(
                        cx.ps[6][0:1, 0:512],
                        lhsT=cx.ones_bf[0:64, :],
                        rhs=q2[:, c * 512:(c + 1) * 512],
                        start=True, stop=True), r=[("q2",)], w=[cx.PSK[6]])
                    p.do("vector", lambda e, c=c: e.tensor_reduce(
                        out=nrm[0:1, 2 * T + 2 + c:2 * T + 3 + c],
                        in_=cx.ps[6][0:1, 0:512],
                        axis=AX.X, op=ALU.max), r=[cx.PSK[6]], w=[("mk2",)])
                p.do("vector", lambda e: e.tensor_reduce(
                    out=nrm[0:1, 2 * T:2 * T + 1],
                    in_=nrm[0:1, 2 * T + 2:2 * T + 2 + T // 512],
                    axis=AX.X, op=ALU.max), r=[("mk2",)], w=[("mk2m",)])
                p.do("scalar", lambda e: e.activation(
                    out=nrm[0:1, 2 * T + 1:2 * T + 2],
                    in_=nrm[0:1, 2 * T:2 * T + 1], func=AF.Sqrt),
                    r=[("mk2m",)], w=[("mk",)])
                p.do("vector", lambda e: e.tensor_scalar(
                    out=nrm[0:1, T:2 * T], in0=nrm[0:1, 0:T],
                    scalar1=nrm[0:1, 2 * T + 1:2 * T + 2], scalar2=None,
                    op0=ALU.mult),
                    r=[("nq",), ("mk",)], w=[("mrow",)])
                for qc in range(T // 512):
                    nkt = 4 * qc + 4
                    p.do("tensor", lambda e, qc=qc: e.matmul(
                        cx.ps[5][:, 0:512], lhsT=cx.ones_row_f[:],
                        rhs=nrm[0:1, T + qc * 512:T + (qc + 1) * 512],
                        start=True, stop=True), r=[("mrow",)], w=[cx.PSK[5]])
                    p.do("vector", lambda e: e.tensor_copy(
                        out=mbs[:, 0:512], in_=cx.ps[5][:, 0:512]),
                        r=[cx.PSK[5]], w=[("mbs",)])
                    for kt in range(nkt):
                        pk = cx.PSK[kt % 2]
                        ps = cx.ps[kt % 2]
                        p.do("tensor", lambda e, kt=kt, qc=qc, ps=ps: e.matmul(
                            ps[:, 0:512],
                            lhsT=kT[:, kt * 128:(kt + 1) * 128],
                            rhs=qT[:, qc * 512:(qc + 1) * 512],
                            start=True, stop=True),
                            r=[("kT",), ("qT",)], w=[pk])
                        p.do("vector", lambda e, kt=kt, h=h, ps=ps: (
                            e.scalar_tensor_tensor(
                                out=pT[:, kt * 512:(kt + 1) * 512],
                                in0=ps[:, 0:512],
                                scalar=akc[:, h * 16 + kt:h * 16 + kt + 1],
                                in1=mbs[:, 0:512],
                                op0=ALU.subtract, op1=ALU.subtract)),
                            r=[pk, ("akc",), ("mbs",)], w=[("pTf", kt)])
                        if kt >= 4 * qc:
                            v = kt - 4 * qc
                            p.do("vector", lambda e, kt=kt, v=v: e.tensor_add(
                                pT[:, kt * 512:(kt + 1) * 512],
                                pT[:, kt * 512:(kt + 1) * 512],
                                dmt[:, v * 512:(v + 1) * 512]),
                                r=[("pTf", kt), ("dmt",)], w=[("pTf", kt)])
                        p.do("scalar", lambda e, kt=kt: e.activation(
                            out=pT[:, kt * 512:(kt + 1) * 512],
                            in_=pT[:, kt * 512:(kt + 1) * 512], func=AF.Exp),
                            r=[("pTf", kt)], w=[("pTf", kt)])
                    for kt in range(nkt):
                        p.do("tensor", lambda e, kt=kt: e.matmul(
                            cx.ps[4][0:65, 0:512],
                            lhsT=v65[:, kt * 65:(kt + 1) * 65],
                            rhs=pT[:, kt * 512:(kt + 1) * 512],
                            start=(kt == 0), stop=(kt == nkt - 1)),
                            r=[("v65",), ("pTf", kt)], w=[cx.PSK[4]])
                    p.do("vector", lambda e: e.reciprocal(
                        out=rsum[0:1, 0:512], in_=cx.ps[4][64:65, 0:512]),
                        r=[cx.PSK[4]], w=[("rsum",)])
                    p.do("vector", lambda e: e.tensor_copy(
                        out=tob[:, 0:512], in_=cx.ps[4][0:64, 0:512]),
                        r=[cx.PSK[4]], w=[("tob",)])
                    p.do("tensor", lambda e: e.matmul(
                        cx.ps[5][0:64, 0:512], lhsT=cx.ones_row_f[0:1, 0:64],
                        rhs=rsum[0:1, 0:512], start=True, stop=True),
                        r=[("rsum",)], w=[cx.PSK[5]])
                    p.do("vector", lambda e: e.tensor_mul(
                        ofm[:, 0:512], tob[:, 0:512], cx.ps[5][0:64, 0:512]),
                        r=[("tob",), cx.PSK[5]], w=[("ofm",)])
                    p.dma("scalar", cx.dr["s_ofm"][h * 64:(h + 1) * 64,
                                                   b * T + qc * 512:
                                                   b * T + (qc + 1) * 512],
                          ofm[:, 0:512], r=[("ofm",)], w=[("od", "x")])
    p.barrier()
    with contextlib.ExitStack() as stk:
        wo = stk.enter_context(sb("a2wo", [128, 8 * 1024], BF16))
        osb = stk.enter_context(sb("a2osb", [128, 8 * NT], BF16))
        xt = stk.enter_context(sb("a2xt", [128, 2 * NT], F32))
        yt = stk.enter_context(sb("a2yt", [128, 2 * NT], F32))
        for k in range(8):
            p.dma("gpsimd", wo[:, k * 1024:(k + 1) * 1024],
                  cx.dr["L2_attn_w_o"][k * 128:(k + 1) * 128, :],
                  w=[("wo", k)])
        for sc in range(NSC):
            for j in range(8):
                p.dma("sync", osb[:, j * NT:(j + 1) * NT],
                      cx.dr["s_ofm"][j * 128:(j + 1) * 128,
                                     sc * NT:(sc + 1) * NT],
                      r=[("od", "x")], w=[("osb", j)])
            for dd in range(8):
                pk = cx.PSK[2 + dd % 2]
                ps = cx.ps[2 + dd % 2]
                ysl = slice((dd % 2) * NT, (dd % 2 + 1) * NT)
                for k in range(8):
                    p.do("tensor", lambda e, dd=dd, k=k, ps=ps: e.matmul(
                        ps[:, 0:NT], lhsT=wo[:, k * 1024 + dd * 128:
                                             k * 1024 + (dd + 1) * 128],
                        rhs=osb[:, k * NT:(k + 1) * NT],
                        start=(k == 0), stop=(k == 7)),
                        r=[("wo", k), ("osb", k)], w=[pk])
                p.dma("sync", xt[:, ysl],
                      x_in[dd * 128:(dd + 1) * 128, sc * NT:(sc + 1) * NT],
                      r=[(xk, "dram")], w=[("xt", dd % 2)])
                p.do("vector", lambda e, ps=ps, ysl=ysl: e.tensor_add(
                    yt[:, ysl], ps[:, 0:NT], xt[:, ysl]),
                    r=[pk, ("xt", dd % 2)], w=[("yt", dd % 2)])
                p.dma("scalar", x_out[dd * 128:(dd + 1) * 128,
                                      sc * NT:(sc + 1) * NT],
                      yt[:, ysl], r=[("yt", dd % 2)], w=[(xok, "dram")])
        p.barrier()


def emit_l1attn(p, cx, x_in, xk, x_out, xok):
    nc = cx.nc
    NT, TT, T = cx.NT, cx.TT, cx.T
    NSC = TT // NT
    sb = cx.sbt
    N = 16
    with contextlib.ExitStack() as stk:
        xc = stk.enter_context(sb("m1xc", [128, 8 * NT], F32))
        lnb = stk.enter_context(sb("m1ln", [128, 8 * NT], BF16))
        lnf = stk.enter_context(sb("m1lnf", [128, NT], BF16))
        win = stk.enter_context(sb("m1win", [128, 8 * 16 * 128], BF16))
        wx = stk.enter_context(sb("m1wx", [128, 8 * 9 * 128], BF16))
        wdt = stk.enter_context(sb("m1wdt", [128, 8 * 8 * 128], BF16))
        bdt = stk.enter_context(sb("m1bdt", [128, 8], F32))
        xsb = stk.enter_context(sb("m1xs", [128, 8 * NT], BF16))
        drb = stk.enter_context(sb("m1dr", [128, 8 * NT], BF16))
        tE = stk.enter_context(sb("m1te", [128, 2 * NT], F32))
        bcs = stk.enter_context(sb("m1bc", [32, TT], F32))
        _alloc_ln_scratch(cx, stk, NT)
        for k in range(8):
            p.dma("gpsimd", win[:, k * 2048:(k + 1) * 2048],
                  cx.dr["L1_attn_W_in"][k * 128:(k + 1) * 128, :],
                  w=[("win", k)])
            p.dma("gpsimd", wx[:, k * 1152:(k + 1) * 1152],
                  cx.dr["L1_attn_W_xp"][k * 128:(k + 1) * 128, :],
                  w=[("wx", k)])
            p.dma("gpsimd", wdt[:, k * 1024:(k + 1) * 1024],
                  cx.dr["L1_attn_W_dt"][k * 128:(k + 1) * 128, :],
                  w=[("wdt", k)])
        p.dma("sync", bdt[:], cx.dr["L1_attn_b_dtc"], w=[("bdt",)])
        for sc in range(NSC):
            load_x_chunk(p, cx, x_in, xk, sc, xc, "xc", NT)
            emit_ln_chunk(p, cx, "xc", xc, lnb, "ln", NT)
            for j in range(8):
                p.dma("scalar", cx.dr["s_ln"][j * 128:(j + 1) * 128,
                                              sc * NT:(sc + 1) * NT],
                      lnb[:, j * NT:(j + 1) * NT],
                      r=[("ln", j)], w=[("lndram", sc, j)])
            for mi in range(16):
                pk = cx.PSK[4 + mi % 2]
                ps = cx.ps[4 + mi % 2]
                for k in range(8):
                    p.do("tensor", lambda e, mi=mi, k=k, ps=ps: e.matmul(
                        ps[:, 0:NT],
                        lhsT=win[:, k * 2048 + mi * 128:
                                 k * 2048 + (mi + 1) * 128],
                        rhs=lnb[:, k * NT:(k + 1) * NT],
                        start=(k == 0), stop=(k == 7)),
                        r=[("win", k), ("ln", k)], w=[pk])
                if mi < 8:
                    p.do("vector", lambda e, mi=mi, ps=ps: e.tensor_copy(
                        out=xsb[:, mi * NT:(mi + 1) * NT], in_=ps[:, 0:NT]),
                        r=[pk], w=[("xs", mi)])
                    p.dma("scalar",
                          cx.dr["s_p1"][mi * 128:(mi + 1) * 128,
                                        sc * NT:(sc + 1) * NT],
                          xsb[:, mi * NT:(mi + 1) * NT],
                          r=[("xs", mi)], w=[("xsd", sc, mi)])
                else:
                    p.do("scalar", lambda e, mi=mi, ps=ps: e.activation(
                        out=lnf[:, 0:NT], in_=ps[:, 0:NT], func=AF.Silu),
                        r=[pk], w=[("dlt",)])
                    p.dma("scalar",
                          cx.dr["s_p2"][(mi - 8) * 128:(mi - 7) * 128,
                                        sc * NT:(sc + 1) * NT],
                          lnf[:, 0:NT], r=[("dlt",)], w=[("zd", sc, mi)])
            for mi in range(9):
                pk = cx.PSK[4 + mi % 2]
                ps = cx.ps[4 + mi % 2]
                for k in range(8):
                    p.do("tensor", lambda e, mi=mi, k=k, ps=ps: e.matmul(
                        ps[:, 0:NT],
                        lhsT=wx[:, k * 1152 + mi * 128:
                                k * 1152 + (mi + 1) * 128],
                        rhs=xsb[:, k * NT:(k + 1) * NT],
                        start=(k == 0), stop=(k == 7)),
                        r=[("wx", k), ("xs", k)], w=[pk])
                if mi < 8:
                    p.do("vector", lambda e, mi=mi, ps=ps: e.tensor_copy(
                        out=drb[:, mi * NT:(mi + 1) * NT], in_=ps[:, 0:NT]),
                        r=[pk], w=[("dr", mi)])
                else:
                    p.do("vector", lambda e, sc=sc, ps=ps: e.tensor_copy(
                        out=bcs[0:32, sc * NT:(sc + 1) * NT],
                        in_=ps[0:32, 0:NT]), r=[pk], w=[("bcs",)])
            for mi in range(8):
                pk = cx.PSK[4 + mi % 2]
                ps = cx.ps[4 + mi % 2]
                for k in range(8):
                    p.do("tensor", lambda e, mi=mi, k=k, ps=ps: e.matmul(
                        ps[:, 0:NT],
                        lhsT=wdt[:, k * 1024 + mi * 128:
                                 k * 1024 + (mi + 1) * 128],
                        rhs=drb[:, k * NT:(k + 1) * NT],
                        start=(k == 0), stop=(k == 7)),
                        r=[("wdt", k), ("dr", k)], w=[pk])
                p.do("vector", lambda e, mi=mi, ps=ps: e.tensor_scalar(
                    out=tE[:, 0:NT], in0=ps[:, 0:NT],
                    scalar1=bdt[:, mi:mi + 1], scalar2=None, op0=ALU.add),
                    r=[pk, ("bdt",)], w=[("spx",)])
                p.do("scalar", lambda e: e.activation(
                    out=tE[:, NT:2 * NT], in_=tE[:, 0:NT], func=AF.Abs),
                    r=[("spx",)], w=[("spa",)])
                p.do("scalar", lambda e: e.activation(
                    out=tE[:, NT:2 * NT], in_=tE[:, NT:2 * NT], func=AF.Exp,
                    scale=-1.0), r=[("spa",)], w=[("spa",)])
                p.do("scalar", lambda e: e.activation(
                    out=tE[:, NT:2 * NT], in_=tE[:, NT:2 * NT], func=AF.Ln,
                    bias=cx.cval(1.0)), r=[("spa",)], w=[("spa",)])
                p.do("vector", lambda e: e.tensor_scalar(
                    out=tE[:, 0:NT], in0=tE[:, 0:NT], scalar1=0.0,
                    scalar2=None, op0=ALU.max),
                    r=[("spx",)], w=[("spx",)])
                p.do("vector", lambda e: e.tensor_add(
                    tE[:, 0:NT], tE[:, 0:NT], tE[:, NT:2 * NT]),
                    r=[("spx",), ("spa",)], w=[("spx",)])
                p.do("vector", lambda e: e.tensor_copy(
                    out=lnf[:, 0:NT], in_=tE[:, 0:NT]),
                    r=[("spx",)], w=[("dlt",)])
                p.dma("scalar",
                      cx.dr["s_p3"][mi * 128:(mi + 1) * 128,
                                    sc * NT:(sc + 1) * NT],
                      lnf[:, 0:NT], r=[("dlt",)], w=[("dltd", sc, mi)])
        p.dma("scalar", cx.dr["s_bc"], bcs[0:32, :], r=[("bcs",)],
              w=[("bcd",)])
    p.barrier()
    with contextlib.ExitStack() as stk:
        dlt = stk.enter_context(sb("m1dl", [128, 4 * T], BF16))
        xss = stk.enter_context(sb("m1xss", [128, 4 * T], BF16))
        dxt = stk.enter_context(sb("m1dxt", [128, 4 * T], BF16))
        yal = stk.enter_context(sb("m1yal", [128, 4 * T], F32))
        Acol = stk.enter_context(sb("m1ac", [128, 8 * 16], F32))
        brow = stk.enter_context(sb("m1br", [1, T], F32))
        crow = stk.enter_context(sb("m1cr", [1, T], F32))
        Bb = stk.enter_context(sb("m1bb", [128, T], F32))
        Cb = stk.enter_context(sb("m1cb", [128, T], F32))
        dAa = stk.enter_context(sb("m1da", [128, 512], F32))
        dBx = stk.enter_context(sb("m1db", [128, 512], F32))
        hsc = stk.enter_context(sb("m1h", [128, 512 + 1], F32))
        zsl = stk.enter_context(sb("m1z", [128, NT], BF16))
        dpc = stk.enter_context(sb("m1dp", [128, 8], F32))
        p.dma("sync", Acol[:], cx.dr["L1_attn_A_logc"], w=[("alog",)])
        p.do("scalar", lambda e: e.activation(
            out=Acol[:], in_=Acol[:], func=AF.Exp), r=[("alog",)],
            w=[("alog",)])
        p.do("vector", lambda e: e.tensor_scalar(
            out=Acol[:], in0=Acol[:], scalar1=-1.0, scalar2=None,
            op0=ALU.mult), r=[("alog",)], w=[("alog",)])
        p.dma("sync", dpc[:], cx.dr["L1_attn_D_paramc"], w=[("dpc",)])
        for b in range(2):
            for hf in range(2):
                for jj in range(4):
                    j = hf * 4 + jj
                    p.dma("sync", dlt[:, jj * T:(jj + 1) * T],
                          cx.dr["s_p3"][j * 128:(j + 1) * 128,
                                        b * T:(b + 1) * T],
                          r=[("dltd", "x")], w=[("dltS", jj)])
                    p.dma("sync", xss[:, jj * T:(jj + 1) * T],
                          cx.dr["s_p1"][j * 128:(j + 1) * 128,
                                        b * T:(b + 1) * T],
                          r=[("xsd", "x")], w=[("xssS", jj)])
                    p.do("vector", lambda e, jj=jj: e.tensor_mul(
                        dxt[:, jj * T:(jj + 1) * T],
                        dlt[:, jj * T:(jj + 1) * T],
                        xss[:, jj * T:(jj + 1) * T]),
                        r=[("dltS", jj), ("xssS", jj)], w=[("dxt", jj)])
                    p.do("vector", lambda e, jj=jj: e.memset(
                        yal[:, jj * T:(jj + 1) * T], 0.0), w=[("yal", jj)])
                for n in range(N):
                    p.dma("sync", brow[:],
                          cx.dr["s_bc"][n:n + 1, b * T:(b + 1) * T],
                          r=[("bcd",)], w=[("brow",)])
                    p.dma("sync", crow[:],
                          cx.dr["s_bc"][16 + n:17 + n, b * T:(b + 1) * T],
                          r=[("bcd",)], w=[("crow",)])
                    for c in range(T // 512):
                        p.do("tensor", lambda e, c=c: e.matmul(
                            cx.ps[0][:, 0:512], lhsT=cx.ones_row_f[:],
                            rhs=brow[0:1, c * 512:(c + 1) * 512],
                            start=True, stop=True), r=[("brow",)],
                            w=[cx.PSK[0]])
                        p.do("vector", lambda e, c=c: e.tensor_copy(
                            out=Bb[:, c * 512:(c + 1) * 512],
                            in_=cx.ps[0][:, 0:512]), r=[cx.PSK[0]],
                            w=[("Bb",)])
                        p.do("tensor", lambda e, c=c: e.matmul(
                            cx.ps[1][:, 0:512], lhsT=cx.ones_row_f[:],
                            rhs=crow[0:1, c * 512:(c + 1) * 512],
                            start=True, stop=True), r=[("crow",)],
                            w=[cx.PSK[1]])
                        p.do("vector", lambda e, c=c: e.tensor_copy(
                            out=Cb[:, c * 512:(c + 1) * 512],
                            in_=cx.ps[1][:, 0:512]), r=[cx.PSK[1]],
                            w=[("Cb",)])
                    for jj in range(4):
                        j = hf * 4 + jj
                        for c in range(T // 512):
                            csl = slice(c * 512, (c + 1) * 512)
                            tsl = slice(jj * T + c * 512,
                                        jj * T + (c + 1) * 512)
                            p.do("vector", lambda e, tsl=tsl, j=j, n=n:
                                 e.tensor_scalar(
                                     out=dAa[:], in0=dlt[:, tsl],
                                     scalar1=Acol[:, j * 16 + n:
                                                  j * 16 + n + 1],
                                     scalar2=None, op0=ALU.mult),
                                 r=[("dltS", jj), ("alog",)], w=[("dAa",)])
                            p.do("scalar", lambda e: e.activation(
                                out=dAa[:], in_=dAa[:], func=AF.Exp),
                                r=[("dAa",)], w=[("dAa",)])
                            p.do("vector", lambda e, tsl=tsl, csl=csl:
                                 e.tensor_mul(
                                     dBx[:], dxt[:, tsl], Bb[:, csl]),
                                 r=[("dxt", jj), ("Bb",)], w=[("dBx",)])
                            init = 0.0 if c == 0 else hsc[:, 512:513]
                            p.do("vector", lambda e, init=init:
                                 e.tensor_tensor_scan(
                                     out=hsc[:, 0:512], data0=dAa[:],
                                     data1=dBx[:], initial=init,
                                     op0=ALU.mult, op1=ALU.add),
                                 r=[("dAa",), ("dBx",), ("hcar",)],
                                 w=[("hsc",)])
                            if c < T // 512 - 1:
                                p.do("vector", lambda e: e.tensor_copy(
                                    out=hsc[:, 512:513],
                                    in_=hsc[:, 511:512]),
                                    r=[("hsc",)], w=[("hcar",)])
                            p.do("vector", lambda e, csl=csl: e.tensor_mul(
                                dBx[:], hsc[:, 0:512], Cb[:, csl]),
                                r=[("hsc",), ("Cb",)], w=[("dBx",)])
                            p.do("vector", lambda e, tsl=tsl: e.tensor_add(
                                yal[:, tsl], yal[:, tsl], dBx[:]),
                                r=[("dBx",), ("yal", jj)], w=[("yal", jj)])
                for jj in range(4):
                    j = hf * 4 + jj
                    p.do("vector", lambda e, jj=jj, j=j:
                         e.scalar_tensor_tensor(
                             out=yal[:, jj * T:(jj + 1) * T],
                             in0=xss[:, jj * T:(jj + 1) * T],
                             scalar=dpc[:, j:j + 1],
                             in1=yal[:, jj * T:(jj + 1) * T],
                             op0=ALU.mult, op1=ALU.add),
                         r=[("xssS", jj), ("dpc",), ("yal", jj)],
                         w=[("yal", jj)])
                    for c in range(T // NT):
                        p.dma("sync", zsl[:, 0:NT],
                              cx.dr["s_p2"][j * 128:(j + 1) * 128,
                                            b * T + c * NT:
                                            b * T + (c + 1) * NT],
                              r=[("zd", "x")], w=[("zsl",)])
                        p.do("vector", lambda e, jj=jj, c=c: e.tensor_mul(
                            dlt[:, jj * T + c * NT:jj * T + (c + 1) * NT],
                            yal[:, jj * T + c * NT:jj * T + (c + 1) * NT],
                            zsl[:, 0:NT]),
                            r=[("yal", jj), ("zsl",)],
                            w=[("dltS", jj)])
                        p.dma("scalar",
                              cx.dr["s_p3"][j * 128:(j + 1) * 128,
                                            b * T + c * NT:
                                            b * T + (c + 1) * NT],
                              dlt[:, jj * T + c * NT:
                                  jj * T + (c + 1) * NT],
                              r=[("dltS", jj)], w=[("y2d", "x")])
    p.barrier()
    with contextlib.ExitStack() as stk:
        wout = stk.enter_context(sb("m1wo", [128, 8 * 1024], BF16))
        y2t = stk.enter_context(sb("m1y2t", [128, 8 * NT], BF16))
        lnoutf = stk.enter_context(sb("m1lno", [128, NT], BF16))
        resc = stk.enter_context(sb("m1res", [128, 8 * NT], F32))
        lnaf = stk.enter_context(sb("m1laf", [128, 8 * NT], BF16))
        gcol = stk.enter_context(sb("m1gc", [128, 8], F32))
        bcol = stk.enter_context(sb("m1bcl", [128, 8], F32))
        xt2 = stk.enter_context(sb("m1xt2", [128, 2 * NT], F32))
        _alloc_ln_scratch(cx, stk, NT)
        p.dma("sync", gcol[:], cx.dr["L1_attn_ln_gc"], w=[("c", "aff")])
        p.dma("sync", bcol[:], cx.dr["L1_attn_ln_bc"], w=[("c", "aff")])
        for k in range(8):
            p.dma("gpsimd", wout[:, k * 1024:(k + 1) * 1024],
                  cx.dr["L1_attn_W_out"][k * 128:(k + 1) * 128, :],
                  w=[("wout", k)])
        for sc in range(NSC):
            for j in range(8):
                p.dma("sync", y2t[:, j * NT:(j + 1) * NT],
                      cx.dr["s_p3"][j * 128:(j + 1) * 128,
                                    sc * NT:(sc + 1) * NT],
                      r=[("y2d", "x")], w=[("y2t", j)])
            for dd in range(8):
                pk = cx.PSK[4 + dd % 2]
                ps = cx.ps[4 + dd % 2]
                for k in range(8):
                    p.do("tensor", lambda e, dd=dd, k=k, ps=ps: e.matmul(
                        ps[:, 0:NT],
                        lhsT=wout[:, k * 1024 + dd * 128:
                                  k * 1024 + (dd + 1) * 128],
                        rhs=y2t[:, k * NT:(k + 1) * NT],
                        start=(k == 0), stop=(k == 7)),
                        r=[("wout", k), ("y2t", k)], w=[pk])
                p.dma("sync", lnoutf[:, 0:NT],
                      cx.dr["s_ln"][dd * 128:(dd + 1) * 128,
                                    sc * NT:(sc + 1) * NT],
                      r=[("lndram", "x")], w=[("lnof",)])
                p.do("vector", lambda e, dd=dd, ps=ps: e.tensor_add(
                    resc[:, dd * NT:(dd + 1) * NT], ps[:, 0:NT],
                    lnoutf[:, 0:NT]),
                    r=[pk, ("lnof",)], w=[("resc", dd)])
            emit_ln_chunk(p, cx, "resc", resc, lnaf, "lnaf", NT,
                          affine=(gcol, bcol))
            for dd in range(8):
                xsl = slice((dd % 2) * NT, (dd % 2 + 1) * NT)
                p.dma("sync", xt2[:, xsl],
                      x_in[dd * 128:(dd + 1) * 128, sc * NT:(sc + 1) * NT],
                      r=[(xk, "dram")], w=[("xt2", dd % 2)])
                p.do("vector", lambda e, dd=dd, xsl=xsl: e.tensor_add(
                    resc[:, dd * NT:(dd + 1) * NT],
                    lnaf[:, dd * NT:(dd + 1) * NT], xt2[:, xsl]),
                    r=[("lnaf", dd), ("xt2", dd % 2)], w=[("resc", dd)])
                p.dma("scalar", x_out[dd * 128:(dd + 1) * 128,
                                      sc * NT:(sc + 1) * NT],
                      resc[:, dd * NT:(dd + 1) * NT],
                      r=[("resc", dd)], w=[(xok, "dram")])
        p.barrier()




def emit_delta_core(p, cx, kind):
    nc = cx.nc
    T, TT = cx.T, cx.TT
    NCHK = T // 128
    sb = cx.sbt
    l0 = (kind == "l0")
    DV = 128 if l0 else 64
    pre = "L0_attn_" if l0 else "L3_attn_"
    gcolT, bcolT, ecolT, ncolT = cx.d_cols
    with contextlib.ExitStack() as stk:
        qr = stk.enter_context(sb("c_qr", [64, T], BF16))
        kr = stk.enter_context(sb("c_kr", [64, T], BF16))
        vr = stk.enter_context(sb("c_vr", [DV, T], BF16))
        qcv = stk.enter_context(sb("c_qcv", [64, T], F32))
        kcv = stk.enter_context(sb("c_kcv", [64, T], F32))
        vcv = stk.enter_context(sb("c_vcv", [DV, T], F32))
        qs = stk.enter_context(sb("c_qs", [64, T], BF16))
        ks = stk.enter_context(sb("c_ks", [64, T], BF16))
        vs = stk.enter_context(sb("c_vs", [DV, T], BF16))
        q2 = stk.enter_context(sb("c_q2", [64, T], BF16))
        nr = stk.enter_context(sb("c_nr", [1, T], F32))
        cqc = stk.enter_context(sb("c_cqc", [64, 64], F32))
        ckc = stk.enter_context(sb("c_ckc", [64, 64], F32))
        cvc = stk.enter_context(sb("c_cvc", [DV, 64], F32))
        grow = stk.enter_context(sb("c_grow", [1, T], F32))
        Es = stk.enter_context(sb("c_E", [128, 128], F32))
        Ei = stk.enter_context(sb("c_Ei", [128, 128], F32))
        mneg = stk.enter_context(sb("c_mneg", [128, 128], F32))
        m01s = stk.enter_context(sb("c_m01s", [128, 128], F32))
        m01i = stk.enter_context(sb("c_m01i", [128, 128], F32))
        mbd = stk.enter_context(sb("c_mbd", [128, 128], F32))
        mr = stk.enter_context(sb("c_mr", [128, 128], F32))
        tf = stk.enter_context(sb("c_tf", [128, 128], F32))
        MT0 = stk.enter_context(sb("c_MT0", [128, 128], F32))
        BDt = stk.enter_context(sb("c_BDt", [128, 128], F32))
        Rt = stk.enter_context(sb("c_Rt", [128, 128], F32))
        PB = [stk.enter_context(sb(f"c_PB{j}", [128, 128], F32))
              for j in range(3)]
        TBt = stk.enter_context(sb("c_TBt", [128, 128], F32))
        TBs = stk.enter_context(sb("c_TBs", [128, 128], F32))
        Nt = stk.enter_context(sb("c_Nt", [128, 128], F32))
        N2 = stk.enter_context(sb("c_N2", [128, 128], F32))
        N4 = stk.enter_context(sb("c_N4", [128, 128], F32))
        Mjs = stk.enter_context(sb("c_Mjs", [128, 128], F32))
        Wf = stk.enter_context(sb("c_W", [128, DV], F32))
        Ub = stk.enter_context(sb("c_Ub", [128, DV], BF16))
        U2b = stk.enter_context(sb("c_U2", [128, DV], BF16))
        bec = stk.enter_context(sb("c_bec", [128, 2], F32))
        Vt32 = stk.enter_context(sb("c_Vt", [128, DV], F32))
        Pt = None
        NNb = stk.enter_context(sb("c_NN", [128, 128], BF16))
        Ksb = stk.enter_context(sb("c_Ksb", [128, 64], BF16))
        S_f = stk.enter_context(sb("c_Sf", [64, DV], F32))
        S_b = stk.enter_context(sb("c_Sb", [64, DV], BF16))
        t_o = stk.enter_context(sb("c_to", [128, DV], F32))
        ot = stk.enter_context(sb("c_ot", [128, DV], F32))
        o2s = stk.enter_context(sb("c_o2s", [128, DV], BF16))
        racc = stk.enter_context(sb("c_racc", [128, 2], F32))
        wnb = stk.enter_context(sb("c_wnb", [128, DV], F32))
        gt = stk.enter_context(sb("c_gt", [128, DV], BF16))
        o5b = stk.enter_context(sb("c_o5", [128, DV], BF16))
        obx = stk.enter_context(sb("c_obx", [DV, 128], BF16))
        bc64 = stk.enter_context(sb("c_bc64", [64, 1], F32))
        nw1 = stk.enter_context(sb("c_nw1", [1, DV], F32))
        p.dma("sync", mneg[:], cx.dr["c_mneg"], w=[("mneg",)])
        p.dma("sync", m01s[:], cx.dr["c_m01s"], w=[("m01s",)])
        p.dma("sync", m01i[:], cx.dr["c_m01i"], w=[("m01i",)])
        p.dma("sync", mbd[:], cx.dr["c_mbd"], w=[("mbd",)])
        p.dma("sync", mr[:], cx.dr["c_mr"], w=[("mr",)])
        p.dma("sync", cqc[:], cx.dr[pre + ("conv_q_wc" if l0 else "cq_wc")],
              w=[("cqc",)])
        p.dma("sync", ckc[:], cx.dr[pre + ("conv_k_wc" if l0 else "ck_wc")],
              w=[("ckc",)])
        p.dma("sync", cvc[:], cx.dr[pre + ("conv_v_wc" if l0 else "cv_wc")],
              w=[("cvc",)])
        p.dma("sync", nw1[:], cx.dr[pre + ("o_norm_w1" if l0 else "nw1")],
              w=[("nw1",)])
        cbq = cbk = cbv = None
        if not l0:
            cbq = stk.enter_context(sb("c_cbq", [64, 16], F32))
            cbk = stk.enter_context(sb("c_cbk", [64, 16], F32))
            cbv = stk.enter_context(sb("c_cbv", [64, 16], F32))
            p.dma("sync", cbq[:], cx.dr["L3_attn_cq_bc"], w=[("cbq",)])
            p.dma("sync", cbk[:], cx.dr["L3_attn_ck_bc"], w=[("cbk",)])
            p.dma("sync", cbv[:], cx.dr["L3_attn_cv_bc"], w=[("cbv",)])
        p.do("tensor", lambda e: e.matmul(
            cx.ps[5][:, 0:DV], lhsT=cx.ones_row_f[:], rhs=nw1[0:1, 0:DV],
            start=True, stop=True), r=[("nw1",)], w=[cx.PSK[5]])
        p.do("vector", lambda e: e.tensor_copy(out=wnb[:], in_=cx.ps[5][:, 0:DV]),
             r=[cx.PSK[5]], w=[("wnb",)])
        rmse = 1e-6 if l0 else 1e-5

        def conv(dst, dstk, raw, rawk, P_, wc, wck, hh, bias, biask):
            c0 = hh * 4
            if bias is None:
                p.do("vector", lambda e: e.tensor_scalar(
                    out=dst[0:P_, :], in0=raw[0:P_, :],
                    scalar1=wc[0:P_, c0 + 3:c0 + 4], scalar2=None,
                    op0=ALU.mult), r=[rawk, wck], w=[dstk])
            else:
                p.do("vector", lambda e: e.tensor_scalar(
                    out=dst[0:P_, :], in0=raw[0:P_, :],
                    scalar1=wc[0:P_, c0 + 3:c0 + 4],
                    scalar2=bias[0:P_, hh:hh + 1],
                    op0=ALU.mult, op1=ALU.add), r=[rawk, wck, biask],
                    w=[dstk])
            for jt in range(3):
                s = 3 - jt
                p.do("vector", lambda e, jt=jt, s=s: e.scalar_tensor_tensor(
                    out=dst[0:P_, s:T], in0=raw[0:P_, 0:T - s],
                    scalar=wc[0:P_, c0 + jt:c0 + jt + 1],
                    in1=dst[0:P_, s:T], op0=ALU.mult, op1=ALU.add),
                    r=[rawk, wck, dstk], w=[dstk])
            p.do("scalar", lambda e: e.activation(
                out=dst[0:P_, :], in_=dst[0:P_, :], func=AF.Silu),
                r=[dstk], w=[dstk])

        def l2n(dst, dstk, src, srck, scale):
            p.do("scalar", lambda e: e.activation(
                out=q2[:], in_=src[0:64, :], func=AF.Square),
                r=[srck], w=[("q2",)])
            for c in range(T // 512):
                p.do("tensor", lambda e, c=c: e.matmul(
                    cx.ps[0][0:1, 0:512], lhsT=cx.ones_bf[0:64, :],
                    rhs=q2[:, c * 512:(c + 1) * 512],
                    start=True, stop=True), r=[("q2",)], w=[cx.PSK[0]])
                p.do("scalar", lambda e, c=c: e.activation(
                    out=nr[0:1, c * 512:(c + 1) * 512],
                    in_=cx.ps[0][0:1, 0:512], func=AF.Sqrt,
                    bias=cx.cval(1e-6, 1)), r=[cx.PSK[0]], w=[("nr",)])
            p.do("vector", lambda e: e.reciprocal(
                out=nr[0:1, 0:T], in_=nr[0:1, 0:T]), r=[("nr",)], w=[("nr",)])
            for c in range(T // 512):
                p.do("tensor", lambda e, c=c: e.matmul(
                    cx.ps[0][0:64, 0:512], lhsT=cx.ones_row_f[0:1, 0:64],
                    rhs=nr[0:1, c * 512:(c + 1) * 512],
                    start=True, stop=True), r=[("nr",)], w=[cx.PSK[0]])
                p.do("vector", lambda e, c=c: e.scalar_tensor_tensor(
                    out=dst[0:64, c * 512:(c + 1) * 512],
                    in0=src[0:64, c * 512:(c + 1) * 512], scalar=scale,
                    in1=cx.ps[0][0:64, 0:512], op0=ALU.mult, op1=ALU.mult),
                    r=[srck, cx.PSK[0]], w=[dstk])

        for b in range(2):
            for h in range(16):
                p.dma("sync", qr[:], cx.dr["s_p1"][h * 64:(h + 1) * 64,
                                                   b * T:(b + 1) * T],
                      r=[("s_p1", "x")], w=[("qr",)])
                p.dma("sync", kr[:], cx.dr["s_p2"][h * 64:(h + 1) * 64,
                                                   b * T:(b + 1) * T],
                      r=[("s_p2", "x")], w=[("kr",)])
                p.dma("sync", vr[:], cx.dr["s_p3"][h * DV:(h + 1) * DV,
                                                   b * T:(b + 1) * T],
                      r=[("s_p3", "x")], w=[("vr",)])
                if l0:
                    p.dma("sync", grow[:], cx.dr["s_g"][h:h + 1,
                                                        b * T:(b + 1) * T],
                          r=[("gd",)], w=[("grow",)])
                conv(qcv, ("qcv",), qr, ("qr",), 64, cqc, ("cqc",), h,
                     cbq, ("cbq",))
                conv(kcv, ("kcv",), kr, ("kr",), 64, ckc, ("ckc",), h,
                     cbk, ("cbk",))
                conv(vcv, ("vcv",), vr, ("vr",), DV, cvc, ("cvc",), h,
                     cbv, ("cbv",))
                l2n(qs, ("qs",), qcv, ("qcv",), 0.125)
                l2n(ks, ("ks",), kcv, ("kcv",), 1.0)
                p.do("vector", lambda e: e.tensor_copy(
                    out=vs[:], in_=vcv[:]), r=[("vcv",)], w=[("vs",)])
                p.do("vector", lambda e: e.memset(S_f[:], 0.0), w=[("Sf",)])
                p.do("vector", lambda e: e.memset(S_b[:], 0.0), w=[("Sb",)])
                cgb = b * NCHK
                for c in range(NCHK):
                    csl = slice(c * 128, (c + 1) * 128)
                    gsl = gcolT[:, (cgb + c) * 16 + h:(cgb + c) * 16 + h + 1]
                    bsl = bcolT[:, (cgb + c) * 16 + h:(cgb + c) * 16 + h + 1]
                    esl = ecolT[:, (cgb + c) * 16 + h:(cgb + c) * 16 + h + 1]
                    nsl = ncolT[:, (cgb + c) * 16 + h:(cgb + c) * 16 + h + 1]
                    if l0:
                        p.do("tensor", lambda e, csl=csl: e.matmul(
                            cx.ps[0][:, 0:128], lhsT=cx.ones_row_f[:],
                            rhs=grow[0:1, csl], start=True, stop=True),
                            r=[("grow",)], w=[cx.PSK[0]])
                        p.do("vector", lambda e, gsl=gsl: (
                            e.scalar_tensor_tensor(
                                out=Es[:], in0=cx.ps[0][:, 0:128],
                                scalar=gsl, in1=mneg[:],
                                op0=ALU.subtract, op1=ALU.add)),
                            r=[cx.PSK[0], ("gcolT",), ("mneg",)], w=[("Es",)])
                        p.do("scalar", lambda e: e.activation(
                            out=Es[:], in_=Es[:], func=AF.Exp),
                            r=[("Es",)], w=[("Es",)])
                        p.do("vector", lambda e: e.tensor_add(
                            Ei[:], Es[:], cx.id_f[:]),
                            r=[("Es",), ("c", "idf")], w=[("Ei",)])
                        p.do("scalar", lambda e: e.activation(
                            out=bc64[:], in_=cx.ps[0][0:64, 127:128],
                            func=AF.Exp), r=[cx.PSK[0]], w=[("bc64",)])
                    p.do("tensor", lambda e, csl=csl: e.matmul(
                        cx.ps[1][:, 0:128], lhsT=ks[:, csl], rhs=ks[:, csl],
                        start=True, stop=True), r=[("ks",)], w=[cx.PSK[1]])
                    if l0:
                        p.do("vector", lambda e: e.tensor_mul(
                            tf[:], cx.ps[1][:, 0:128], Es[:]),
                            r=[cx.PSK[1], ("Es",)], w=[("tf",)])
                        p.do("vector", lambda e, bsl=bsl: e.tensor_scalar(
                            out=MT0[:], in0=tf[:], scalar1=bsl,
                            scalar2=None, op0=ALU.mult),
                            r=[("tf",), ("bcolT",)], w=[("MT", 0)])
                    else:
                        p.do("vector", lambda e, bsl=bsl: (
                            e.scalar_tensor_tensor(
                                out=MT0[:], in0=cx.ps[1][:, 0:128],
                                scalar=bsl, in1=m01s[:],
                                op0=ALU.mult, op1=ALU.mult)),
                            r=[cx.PSK[1], ("bcolT",), ("m01s",)],
                            w=[("MT", 0)])
                    p.do("tensor", lambda e, csl=csl: e.transpose(
                        cx.psbf[:, 0:128][0:128, 0:DV], vs[:, csl],
                        cx.id_bf[0:DV, 0:DV]), r=[("vs",), ("c", "idb")],
                        w=[("psbf", 0)])
                    p.do("vector", lambda e: e.tensor_copy(
                        out=Vt32[:, 0:DV], in_=cx.psbf[:, 0:128][0:128, 0:DV]),
                        r=[("psbf", 0)], w=[("Vt",)])
                    p.do("tensor", lambda e, csl=csl: e.matmul(
                        cx.ps[2][:, 0:DV], lhsT=ks[:, csl], rhs=S_b[:],
                        start=True, stop=True), r=[("ks",), ("Sb",)],
                        w=[cx.PSK[2]])
                    if l0:
                        p.do("vector", lambda e, nsl=nsl: (
                            e.scalar_tensor_tensor(
                                out=Wf[:], in0=cx.ps[2][:, 0:DV],
                                scalar=nsl, in1=Vt32[:, 0:DV],
                                op0=ALU.mult, op1=ALU.add)),
                            r=[cx.PSK[2], ("ncolT",), ("Vt",)], w=[("Wf",)])
                    else:
                        p.do("vector", lambda e: e.scalar_tensor_tensor(
                            out=Wf[:], in0=cx.ps[2][:, 0:DV], scalar=-1.0,
                            in1=Vt32[:, 0:DV], op0=ALU.mult, op1=ALU.add),
                            r=[cx.PSK[2], ("Vt",)], w=[("Wf",)])
                    def tpose(dst, dstk, srck, src_t):
                        p.do("tensor", lambda e, src_t=src_t: e.transpose(
                            cx.ps[0][:, 0:128], src_t[:], cx.id_f[:]),
                            r=[srck, ("c", "idf")], w=[cx.PSK[0]])
                        p.do("vector", lambda e, dst=dst: e.tensor_copy(
                            out=dst[:], in_=cx.ps[0][:, 0:128]),
                            r=[cx.PSK[0]], w=[dstk])

                    def mmat(dst, dstk, lh, lhk, rh, rhk, addid=None):
                        p.do("tensor", lambda e, lh=lh, rh=rh: e.matmul(
                            cx.ps[3][:, 0:128], lhsT=lh[:], rhs=rh[:],
                            start=True, stop=True),
                            r=[lhk, rhk], w=[cx.PSK[3]])
                        if addid is None:
                            p.do("vector", lambda e, dst=dst: e.tensor_copy(
                                out=dst[:], in_=cx.ps[0 + 3][:, 0:128]),
                                r=[cx.PSK[3]], w=[dstk])
                        else:
                            p.do("vector", lambda e, dst=dst, a=addid:
                                 e.tensor_add(dst[:], a[:],
                                              cx.ps[3][:, 0:128]),
                                 r=[cx.PSK[3], ("xid",) if a is cx.id_f
                                    else ("tmpk",)], w=[dstk])
                    p.do("vector", lambda e: e.tensor_mul(
                        BDt[:], MT0[:], mbd[:]),
                        r=[("MT", 0), ("mbd",)], w=[("BDt",)])
                    p.do("vector", lambda e: e.tensor_mul(
                        Rt[:], MT0[:], mr[:]),
                        r=[("MT", 0), ("mr",)], w=[("Rt",)])
                    srcs = [(BDt, ("BDt",))]
                    for j in range(3):
                        sm, smk = srcs[-1]
                        tpose(Mjs, ("Mjs",), smk, sm)
                        mmat(PB[j], ("PB", j), Mjs, ("Mjs",), sm, smk)
                        srcs.append((PB[j], ("PB", j)))
                    p.do("vector", lambda e: e.tensor_sub(
                        TBt[:], cx.id_f[:], BDt[:]),
                        r=[("c", "idf"), ("BDt",)], w=[("TBt",)])
                    for j in range(3):
                        tpose(Mjs, ("Mjs",), ("PB", j), PB[j])
                        p.do("tensor", lambda e: e.matmul(
                            cx.ps[3][:, 0:128], lhsT=Mjs[:], rhs=TBt[:],
                            start=True, stop=True),
                            r=[("Mjs",), ("TBt",)], w=[cx.PSK[3]])
                        p.do("vector", lambda e: e.tensor_add(
                            TBt[:], TBt[:], cx.ps[3][:, 0:128]),
                            r=[("TBt",), cx.PSK[3]], w=[("TBt",)])
                    tpose(Mjs, ("Mjs",), ("Rt",), Rt)
                    mmat(Nt, ("Nt",), Mjs, ("Mjs",), TBt, ("TBt",))
                    tpose(Mjs, ("Mjs",), ("Nt",), Nt)
                    mmat(N2, ("N2",), Mjs, ("Mjs",), Nt, ("Nt",))
                    tpose(Mjs, ("Mjs",), ("N2",), N2)
                    mmat(N4, ("N4",), Mjs, ("Mjs",), N2, ("N2",))
                    p.do("tensor", lambda e: e.matmul(
                        cx.ps[3][:, 0:DV], lhsT=TBt[:], rhs=Wf[:],
                        start=True, stop=True),
                        r=[("TBt",), ("Wf",)], w=[cx.PSK[3]])
                    p.do("vector", lambda e: e.tensor_copy(
                        out=Wf[:], in_=cx.ps[3][:, 0:DV]),
                        r=[cx.PSK[3]], w=[("Wf",)])
                    for (mat, mk, sub) in ((N4, ("N4",), False),
                                           (N2, ("N2",), False),
                                           (Nt, ("Nt",), True)):
                        p.do("tensor", lambda e, mat=mat: e.matmul(
                            cx.ps[3][:, 0:DV], lhsT=mat[:], rhs=Wf[:],
                            start=True, stop=True),
                            r=[mk, ("Wf",)], w=[cx.PSK[3]])
                        if not sub:
                            p.do("vector", lambda e: e.tensor_add(
                                Wf[:], Wf[:], cx.ps[3][:, 0:DV]),
                                r=[("Wf",), cx.PSK[3]], w=[("Wf",)])
                        else:
                            p.do("vector", lambda e: e.tensor_sub(
                                Wf[:], Wf[:], cx.ps[3][:, 0:DV]),
                                r=[("Wf",), cx.PSK[3]], w=[("Wf",)])
                    p.do("vector", lambda e, bsl=bsl: e.tensor_scalar(
                        out=Ub[:], in0=Wf[:], scalar1=bsl, scalar2=None,
                        op0=ALU.mult), r=[("Wf",), ("bcolT",)], w=[("Ub",)])
                    if l0:
                        p.do("vector", lambda e, bsl=bsl: e.tensor_mul(
                            bec[:, 0:1], bsl, Ei[:, 127:128]),
                            r=[("bcolT",), ("Ei",)], w=[("becK",)])
                        p.do("vector", lambda e: e.tensor_scalar(
                            out=U2b[:], in0=Wf[:], scalar1=bec[:, 0:1],
                            scalar2=None, op0=ALU.mult),
                            r=[("Wf",), ("becK",)], w=[("U2b",)])
                    p.do("tensor", lambda e, csl=csl: e.matmul(
                        cx.ps[4][:, 0:128], lhsT=ks[:, csl], rhs=qs[:, csl],
                        start=True, stop=True), r=[("ks",), ("qs",)],
                        w=[cx.PSK[4]])
                    p.do("vector", lambda e: e.tensor_mul(
                        NNb[:], cx.ps[4][:, 0:128],
                        Ei[:] if l0 else m01i[:]),
                        r=[cx.PSK[4], ("Ei",) if l0 else ("m01i",)],
                        w=[("NNb",)])
                    if l0:
                        p.do("tensor", lambda e: e.matmul(
                            cx.ps[5][:, 0:DV], lhsT=NNb[:], rhs=Ub[:],
                            start=True, stop=True),
                            r=[("NNb",), ("Ub",)], w=[cx.PSK[5]])
                        p.do("tensor", lambda e, csl=csl: e.matmul(
                            cx.ps[6][:, 0:DV], lhsT=qs[:, csl], rhs=S_b[:],
                            start=True, stop=True),
                            r=[("qs",), ("Sb",)], w=[cx.PSK[6]])
                        p.do("vector", lambda e, esl=esl: e.tensor_scalar(
                            out=t_o[:], in0=cx.ps[6][:, 0:DV], scalar1=esl,
                            scalar2=None, op0=ALU.mult),
                            r=[cx.PSK[6], ("ecolT",)], w=[("t_o",)])
                        p.do("vector", lambda e: e.tensor_add(
                            ot[:], t_o[:], cx.ps[5][:, 0:DV]),
                            r=[("t_o",), cx.PSK[5]], w=[("ot",)])
                    else:
                        p.do("tensor", lambda e: e.matmul(
                            cx.ps[5][:, 0:DV], lhsT=NNb[:], rhs=Ub[:],
                            start=True, stop=False),
                            r=[("NNb",), ("Ub",)], w=[cx.PSK[5]])
                        p.do("tensor", lambda e, csl=csl: e.matmul(
                            cx.ps[5][:, 0:DV], lhsT=qs[:, csl], rhs=S_b[:],
                            start=False, stop=True),
                            r=[("qs",), ("Sb",)], w=[cx.PSK[5]])
                        p.do("vector", lambda e: e.tensor_copy(
                            out=ot[:], in_=cx.ps[5][:, 0:DV]),
                            r=[cx.PSK[5]], w=[("ot",)])
                    p.do("scalar", lambda e: e.activation(
                        out=o2s[:], in_=ot[:], func=AF.Square,
                        accum_out=racc[:, 0:1]), r=[("ot",)], w=[("racc",)])
                    p.do("vector", lambda e: e.tensor_scalar(
                        out=racc[:, 1:2], in0=racc[:, 0:1],
                        scalar1=1.0 / DV, scalar2=None, op0=ALU.mult),
                        r=[("racc",)], w=[("racc2",)])
                    p.do("scalar", lambda e: e.activation(
                        out=racc[:, 1:2], in_=racc[:, 1:2], func=AF.Sqrt,
                        bias=cx.cval(rmse)), r=[("racc2",)], w=[("racc2",)])
                    p.do("vector", lambda e: e.reciprocal(
                        out=racc[:, 1:2], in_=racc[:, 1:2]),
                        r=[("racc2",)], w=[("racc2",)])
                    p.do("vector", lambda e: e.tensor_scalar(
                        out=ot[:], in0=ot[:], scalar1=racc[:, 1:2],
                        scalar2=None, op0=ALU.mult),
                        r=[("ot",), ("racc2",)], w=[("ot",)])
                    p.do("vector", lambda e: e.tensor_mul(
                        ot[:], ot[:], wnb[:]), r=[("ot",), ("wnb",)],
                        w=[("ot",)])
                    p.dma("sync", gt[:, 0:DV],
                          cx.dr["s_gtm"][b * T + c * 128:b * T + (c + 1) * 128,
                                         h * DV:(h + 1) * DV],
                          r=[("gtm", "x")], w=[("gt",)])
                    p.do("vector", lambda e: e.tensor_mul(
                        o5b[:], ot[:], gt[:, 0:DV]),
                        r=[("ot",), ("gt",)], w=[("o5b",)])
                    p.do("tensor", lambda e: e.transpose(
                        cx.psbf[:, 384:512][0:DV, 0:128], o5b[:, 0:DV],
                        cx.id_bf[:]), r=[("o5b",), ("c", "idb")],
                        w=[("psbf", 3)])
                    p.do("vector", lambda e: e.tensor_copy(
                        out=obx[:], in_=cx.psbf[:, 384:512][0:DV, 0:128]),
                        r=[("psbf", 3)], w=[("obx",)])
                    p.dma("scalar",
                          cx.dr["s_ofm"][h * DV:(h + 1) * DV,
                                         b * T + c * 128:b * T + (c + 1) * 128],
                          obx[:], r=[("obx",)], w=[("od", "x")])
                    p.do("tensor", lambda e, csl=csl: e.transpose(
                        cx.psbf[:, 256:384][0:128, 0:64], ks[:, csl],
                        cx.id_bf[0:64, 0:64]), r=[("ks",), ("c", "idb")],
                        w=[("psbf", 2)])
                    p.do("vector", lambda e: e.tensor_copy(
                        out=Ksb[:, 0:64], in_=cx.psbf[:, 256:384][0:128, 0:64]),
                        r=[("psbf", 2)], w=[("Ksb",)])
                    p.do("tensor", lambda e: e.matmul(
                        cx.ps[1][0:64, 0:DV], lhsT=Ksb[:, 0:64],
                        rhs=U2b[:] if l0 else Ub[:],
                        start=True, stop=True),
                        r=[("Ksb",), ("U2b",) if l0 else ("Ub",)],
                        w=[cx.PSK[1]])
                    if l0:
                        p.do("vector", lambda e: e.scalar_tensor_tensor(
                            out=S_f[:], in0=S_f[:], scalar=bc64[:, 0:1],
                            in1=cx.ps[1][0:64, 0:DV], op0=ALU.mult,
                            op1=ALU.add),
                            r=[("Sf",), ("bc64",), cx.PSK[1]], w=[("Sf",)])
                    else:
                        p.do("vector", lambda e: e.tensor_add(
                            S_f[:], S_f[:], cx.ps[1][0:64, 0:DV]),
                            r=[("Sf",), cx.PSK[1]], w=[("Sf",)])
                    p.do("vector", lambda e: e.tensor_copy(
                        out=S_b[:], in_=S_f[:]), r=[("Sf",)], w=[("Sb",)])


def emit_delta(p, cx, x_in, xk, x_out, xok, kind):
    nc = cx.nc
    NT, TT, T = cx.NT, cx.TT, cx.T
    NSC = TT // NT
    NCHK = T // 128
    sb = cx.sbt
    l0 = (kind == "l0")
    DV = 128 if l0 else 64
    pre = "L0_attn_" if l0 else "L3_attn_"
    with contextlib.ExitStack() as stk:
        xc = stk.enter_context(sb("d_xc", [128, 8 * NT], F32))
        lnb = stk.enter_context(sb("d_ln", [128, 8 * NT], BF16))
        hb = stk.enter_context(sb("d_hb", [128, 8 * NT], BF16))
        wqb = stk.enter_context(sb("d_wq", [128, 8 * 1024], BF16))
        wkb = stk.enter_context(sb("d_wk", [128, 8 * 1024], BF16))
        wvb = stk.enter_context(sb("d_wv", [128, 8 * DV * 16], BF16))
        wgb = stk.enter_context(sb("d_wg", [128, 8 * DV * 16], BF16))
        wbb = stk.enter_context(sb("d_wb", [128, 8 * 16], BF16))
        wab = None
        ev = stk.enter_context(sb("d_ev", [128, NT], BF16))
        tE = stk.enter_context(sb("d_te", [128, 2 * NT], F32))
        beta_sb = stk.enter_context(sb("d_bs", [16, TT], F32))
        nw_col = stk.enter_context(sb("d_nw", [128, 8], F32))
        g_sb = None
        _alloc_ln_scratch(cx, stk, NT)
        if l0:
            wab = stk.enter_context(sb("d_wa", [128, 8 * 16], BF16))
            g_sb = stk.enter_context(sb("d_gs", [16, TT], F32))
            al1 = stk.enter_context(sb("d_al", [16, 2], F32))
            p.dma("sync", al1[:, 0:1], cx.dr["L0_attn_A_log1"],
                  w=[("al1",)])
            p.dma("sync", al1[:, 1:2], cx.dr["L0_attn_dt_bias1"],
                  w=[("dtb",)])
            p.do("scalar", lambda e: e.activation(
                out=al1[:, 0:1], in_=al1[:, 0:1], func=AF.Exp),
                r=[("al1",)], w=[("al1",)])
            p.do("vector", lambda e: e.tensor_scalar(
                out=al1[:, 0:1], in0=al1[:, 0:1], scalar1=-1.0, scalar2=None,
                op0=ALU.mult), r=[("al1",)], w=[("al1",)])
            p.dma("sync", nw_col[:], cx.dr["L0_attn_norm_wc"],
                  w=[("nwc",)])
        wnames = {"q": "q_w" if l0 else "Wq", "k": "k_w" if l0 else "Wk",
                  "v": "v_w" if l0 else "Wv", "g": "g_w" if l0 else "Wg",
                  "b": "b_w" if l0 else "Wb"}
        for k in range(8):
            p.dma("gpsimd", wqb[:, k * 1024:(k + 1) * 1024],
                  cx.dr[pre + wnames["q"]][k * 128:(k + 1) * 128, :],
                  w=[("wq", k)])
            p.dma("gpsimd", wkb[:, k * 1024:(k + 1) * 1024],
                  cx.dr[pre + wnames["k"]][k * 128:(k + 1) * 128, :],
                  w=[("wk", k)])
            p.dma("gpsimd", wvb[:, k * DV * 16:(k + 1) * DV * 16],
                  cx.dr[pre + wnames["v"]][k * 128:(k + 1) * 128, :],
                  w=[("wv", k)])
            p.dma("gpsimd", wgb[:, k * DV * 16:(k + 1) * DV * 16],
                  cx.dr[pre + wnames["g"]][k * 128:(k + 1) * 128, :],
                  w=[("wg", k)])
            p.dma("gpsimd", wbb[:, k * 16:(k + 1) * 16],
                  cx.dr[pre + wnames["b"]][k * 128:(k + 1) * 128, :],
                  w=[("wb", k)])
            if l0:
                p.dma("gpsimd", wab[:, k * 16:(k + 1) * 16],
                      cx.dr["L0_attn_a_w"][k * 128:(k + 1) * 128, :],
                      w=[("wa", k)])
        for sc in range(NSC):
            load_x_chunk(p, cx, x_in, xk, sc, xc, "xc", NT)
            emit_ln_chunk(p, cx, "xc", xc, lnb, "ln", NT)
            for j in range(8):
                p.dma("scalar", cx.dr["s_ln"][j * 128:(j + 1) * 128,
                                              sc * NT:(sc + 1) * NT],
                      lnb[:, j * NT:(j + 1) * NT],
                      r=[("ln", j)], w=[("lndram", sc, j)])
            if l0:
                for j in range(8):
                    p.do("vector", lambda e, j=j: e.tensor_copy(
                        out=xc[:, j * NT:(j + 1) * NT],
                        in_=lnb[:, j * NT:(j + 1) * NT]),
                        r=[("ln", j)], w=[("xc", j)])
                emit_rms_chunk(p, cx, "xc", xc, hb, "hb", NT, nw_col, 1e-6)
                srck = "hb"
                srcb = hb
            else:
                srck = "ln"
                srcb = lnb
            for (wname, wbuf, dst) in (("wq", wqb, "s_p1"),
                                       ("wk", wkb, "s_p2")):
                for mi in range(8):
                    pk = cx.PSK[4 + mi % 2]
                    ps = cx.ps[4 + mi % 2]
                    for k in range(8):
                        p.do("tensor", lambda e, mi=mi, k=k, ps=ps,
                             wbuf=wbuf: e.matmul(
                                 ps[:, 0:NT],
                                 lhsT=wbuf[:, k * 1024 + mi * 128:
                                           k * 1024 + (mi + 1) * 128],
                                 rhs=srcb[:, k * NT:(k + 1) * NT],
                                 start=(k == 0), stop=(k == 7)),
                             r=[(wname, k), (srck, k)], w=[pk])
                    p.do("vector", lambda e, ps=ps: e.tensor_copy(
                        out=ev[:, 0:NT], in_=ps[:, 0:NT]),
                        r=[pk], w=[("ev",)])
                    p.dma("scalar", cx.dr[dst][mi * 128:(mi + 1) * 128,
                                              sc * NT:(sc + 1) * NT],
                          ev[:, 0:NT], r=[("ev",)], w=[(dst, "x")])
            for mi in range(DV * 16 // 128):
                pk = cx.PSK[4 + mi % 2]
                ps = cx.ps[4 + mi % 2]
                for k in range(8):
                    p.do("tensor", lambda e, mi=mi, k=k, ps=ps: e.matmul(
                        ps[:, 0:NT],
                        lhsT=wvb[:, k * DV * 16 + mi * 128:
                                 k * DV * 16 + (mi + 1) * 128],
                        rhs=srcb[:, k * NT:(k + 1) * NT],
                        start=(k == 0), stop=(k == 7)),
                        r=[("wv", k), (srck, k)], w=[pk])
                p.do("vector", lambda e, ps=ps: e.tensor_copy(
                    out=ev[:, 0:NT], in_=ps[:, 0:NT]),
                    r=[pk], w=[("ev",)])
                p.dma("scalar", cx.dr["s_p3"][mi * 128:(mi + 1) * 128,
                                              sc * NT:(sc + 1) * NT],
                      ev[:, 0:NT], r=[("ev",)], w=[("s_p3", "x")])
            for tt in range(NT // 128):
                for nn in range(DV * 16 // 512):
                    pk = cx.PSK[4 + nn % 2]
                    ps = cx.ps[4 + nn % 2]
                    for k in range(8):
                        p.do("tensor", lambda e, tt=tt, nn=nn, k=k, ps=ps:
                             e.matmul(
                                 ps[:, 0:512],
                                 lhsT=srcb[:, k * NT + tt * 128:
                                           k * NT + (tt + 1) * 128],
                                 rhs=wgb[:, k * DV * 16 + nn * 512:
                                         k * DV * 16 + (nn + 1) * 512],
                                 start=(k == 0), stop=(k == 7)),
                             r=[(srck, k), ("wg", k)], w=[pk])
                    p.do("scalar", lambda e, ps=ps: e.activation(
                        out=ev[:, 0:512], in_=ps[:, 0:512], func=AF.Silu),
                        r=[pk], w=[("ev",)])
                    p.dma("scalar",
                          cx.dr["s_gtm"][sc * NT + tt * 128:
                                         sc * NT + (tt + 1) * 128,
                                         nn * 512:(nn + 1) * 512],
                          ev[:, 0:512], r=[("ev",)], w=[("gtm", "x")])
            for k in range(8):
                p.do("tensor", lambda e, k=k: e.matmul(
                    cx.ps[0][0:16, 0:NT], lhsT=wbb[:, k * 16:(k + 1) * 16],
                    rhs=srcb[:, k * NT:(k + 1) * NT],
                    start=(k == 0), stop=(k == 7)),
                    r=[("wb", k), (srck, k)], w=[cx.PSK[0]])
            p.do("scalar", lambda e, sc=sc: e.activation(
                out=beta_sb[0:16, sc * NT:(sc + 1) * NT],
                in_=cx.ps[0][0:16, 0:NT], func=AF.Sigmoid),
                r=[cx.PSK[0]], w=[("beta",)])
            if l0:
                for k in range(8):
                    p.do("tensor", lambda e, k=k: e.matmul(
                        cx.ps[1][0:16, 0:NT],
                        lhsT=wab[:, k * 16:(k + 1) * 16],
                        rhs=srcb[:, k * NT:(k + 1) * NT],
                        start=(k == 0), stop=(k == 7)),
                        r=[("wa", k), (srck, k)], w=[cx.PSK[1]])
                p.do("vector", lambda e: e.tensor_scalar(
                    out=tE[0:16, 0:NT], in0=cx.ps[1][0:16, 0:NT],
                    scalar1=al1[0:16, 1:2], scalar2=None, op0=ALU.add),
                    r=[cx.PSK[1], ("dtb",)], w=[("spx",)])
                p.do("scalar", lambda e: e.activation(
                    out=tE[0:16, NT:2 * NT], in_=tE[0:16, 0:NT],
                    func=AF.Abs), r=[("spx",)], w=[("spa",)])
                p.do("scalar", lambda e: e.activation(
                    out=tE[0:16, NT:2 * NT], in_=tE[0:16, NT:2 * NT],
                    func=AF.Exp, scale=-1.0), r=[("spa",)], w=[("spa",)])
                p.do("scalar", lambda e: e.activation(
                    out=tE[0:16, NT:2 * NT], in_=tE[0:16, NT:2 * NT],
                    func=AF.Ln, bias=cx.cval(1.0, 16)),
                    r=[("spa",)], w=[("spa",)])
                p.do("vector", lambda e: e.tensor_scalar(
                    out=tE[0:16, 0:NT], in0=tE[0:16, 0:NT], scalar1=0.0,
                    scalar2=None, op0=ALU.max), r=[("spx",)], w=[("spx",)])
                p.do("vector", lambda e: e.tensor_add(
                    tE[0:16, 0:NT], tE[0:16, 0:NT], tE[0:16, NT:2 * NT]),
                    r=[("spx",), ("spa",)], w=[("spx",)])
                p.do("vector", lambda e, sc=sc: e.tensor_scalar(
                    out=g_sb[0:16, sc * NT:(sc + 1) * NT],
                    in0=tE[0:16, 0:NT], scalar1=al1[0:16, 0:1],
                    scalar2=None, op0=ALU.mult),
                    r=[("spx",), ("al1",)], w=[("gsb",)])
        gcolT = stk.enter_context(sb("d_gct", [128, NCHK * 2 * 16], F32))
        bcolT = stk.enter_context(sb("d_bct", [128, NCHK * 2 * 16], F32))
        ecolT = stk.enter_context(sb("d_ect", [128, NCHK * 2 * 16], F32))
        ncolT = stk.enter_context(sb("d_nct", [128, NCHK * 2 * 16], F32))
        cx.d_cols = (gcolT, bcolT, ecolT, ncolT)
        if l0:
            csA = stk.enter_context(sb("d_csA", [16, TT], F32))
            cum = g_sb
            cur_key = ("gsb",)
            bufs = [csA, g_sb]
            for si, dlt_ in enumerate((1, 2, 4, 8, 16, 32, 64)):
                dst = bufs[si % 2]
                dk_ = ("cs", si)
                gv_in = cum[0:16, :].rearrange("p (c j) -> p c j", j=128)
                gv_out = dst[0:16, :].rearrange("p (c j) -> p c j", j=128)
                p.do("vector", lambda e, gi=gv_in, go=gv_out, d=dlt_:
                     e.tensor_add(go[:, :, d:], gi[:, :, d:],
                                  gi[:, :, 0:128 - d]),
                     r=[cur_key], w=[dk_])
                p.do("vector", lambda e, gi=gv_in, go=gv_out, d=dlt_:
                     e.tensor_copy(out=go[:, :, 0:d], in_=gi[:, :, 0:d]),
                     r=[cur_key], w=[dk_])
                cum = dst
                cur_key = dk_
            for cg in range(NCHK * 2):
                p.do("tensor", lambda e, cg=cg, cum=cum: e.transpose(
                    cx.ps[0][:, 0:16], cum[0:16, cg * 128:(cg + 1) * 128],
                    cx.id_f[0:16, 0:16]), r=[cur_key], w=[cx.PSK[0]])
                p.do("vector", lambda e, cg=cg: e.tensor_copy(
                    out=gcolT[:, cg * 16:(cg + 1) * 16],
                    in_=cx.ps[0][:, 0:16]), r=[cx.PSK[0]], w=[("gcolT",)])
            p.do("scalar", lambda e: e.activation(
                out=ecolT[:], in_=gcolT[:], func=AF.Exp),
                r=[("gcolT",)], w=[("ecolT",)])
            p.do("vector", lambda e: e.tensor_scalar(
                out=ncolT[:], in0=ecolT[:], scalar1=-1.0, scalar2=None,
                op0=ALU.mult), r=[("ecolT",)], w=[("ncolT",)])
            p.dma("scalar", cx.dr["s_g"], cum[0:16, :], r=[cur_key],
                  w=[("gd",)])
        for cg in range(NCHK * 2):
            p.do("tensor", lambda e, cg=cg: e.transpose(
                cx.ps[1][:, 0:16], beta_sb[0:16, cg * 128:(cg + 1) * 128],
                cx.id_f[0:16, 0:16]), r=[("beta",)], w=[cx.PSK[1]])
            p.do("vector", lambda e, cg=cg: e.tensor_copy(
                out=bcolT[:, cg * 16:(cg + 1) * 16],
                in_=cx.ps[1][:, 0:16]), r=[cx.PSK[1]], w=[("bcolT",)])
    p.barrier()
    emit_delta_core(p, cx, kind)
    p.barrier()
    with contextlib.ExitStack() as stk:
        KT = DV * 16 // 128
        wo = stk.enter_context(sb("d_wo", [128, KT * 1024], BF16))
        osb = stk.enter_context(sb("d_osb", [128, KT * NT], BF16))
        xt = stk.enter_context(sb("d_xt", [128, 2 * NT], F32))
        yt = stk.enter_context(sb("d_yt", [128, 2 * NT], F32))
        lnt = stk.enter_context(sb("d_lnt", [128, NT], BF16))
        won = "o_w" if l0 else "Wo"
        for k in range(KT):
            p.dma("gpsimd", wo[:, k * 1024:(k + 1) * 1024],
                  cx.dr[pre + won][k * 128:(k + 1) * 128, :],
                  w=[("wo", k)])
        for sc in range(NSC):
            for j in range(KT):
                p.dma("sync", osb[:, j * NT:(j + 1) * NT],
                      cx.dr["s_ofm"][j * 128:(j + 1) * 128,
                                     sc * NT:(sc + 1) * NT],
                      r=[("od", "x")], w=[("osb", j)])
            for dd in range(8):
                pk = cx.PSK[2 + dd % 2]
                ps = cx.ps[2 + dd % 2]
                ysl = slice((dd % 2) * NT, (dd % 2 + 1) * NT)
                for k in range(KT):
                    p.do("tensor", lambda e, dd=dd, k=k, ps=ps: e.matmul(
                        ps[:, 0:NT], lhsT=wo[:, k * 1024 + dd * 128:
                                             k * 1024 + (dd + 1) * 128],
                        rhs=osb[:, k * NT:(k + 1) * NT],
                        start=(k == 0), stop=(k == KT - 1)),
                        r=[("wo", k), ("osb", k)], w=[pk])
                if l0:
                    p.dma("sync", lnt[:, 0:NT],
                          cx.dr["s_ln"][dd * 128:(dd + 1) * 128,
                                        sc * NT:(sc + 1) * NT],
                          r=[("lndram", "x")], w=[("lnt",)])
                    p.do("vector", lambda e, ps=ps, ysl=ysl: e.tensor_add(
                        yt[:, ysl], ps[:, 0:NT], lnt[:, 0:NT]),
                        r=[pk, ("lnt",)], w=[("yt", dd % 2)])
                    p.dma("scalar", cx.dr["s_x1"][dd * 128:(dd + 1) * 128,
                                                  sc * NT:(sc + 1) * NT],
                          yt[:, ysl], r=[("yt", dd % 2)], w=[("x1d", "x")])
                else:
                    p.dma("sync", xt[:, ysl],
                          x_in[dd * 128:(dd + 1) * 128,
                               sc * NT:(sc + 1) * NT],
                          r=[(xk, "dram")], w=[("xt", dd % 2)])
                    p.do("vector", lambda e, ps=ps, ysl=ysl: e.tensor_add(
                        yt[:, ysl], ps[:, 0:NT], xt[:, ysl]),
                        r=[pk, ("xt", dd % 2)], w=[("yt", dd % 2)])
                    p.dma("scalar", x_out[dd * 128:(dd + 1) * 128,
                                          sc * NT:(sc + 1) * NT],
                          yt[:, ysl], r=[("yt", dd % 2)], w=[(xok, "dram")])
        p.barrier()
    if not l0:
        return
    with contextlib.ExitStack() as stk:
        HF = 2752
        HTS = [128] * 21 + [64]
        wga = stk.enter_context(sb("d_wga", [128, 8 * HF], BF16))
        wup = stk.enter_context(sb("d_wup", [128, 8 * HF], BF16))
        wdn = stk.enter_context(sb("d_wdn", [128, 22 * 1024], BF16))
        pn_col = stk.enter_context(sb("d_pn", [128, 8], F32))
        xd = stk.enter_context(sb("d_xc2", [128, 8 * NT], F32))
        h2b = stk.enter_context(sb("d_h2", [128, 8 * NT], BF16))
        hbuf = stk.enter_context(sb("d_hbuf", [128, 22 * NT], BF16))
        xt3 = stk.enter_context(sb("d_xt2", [128, 2 * NT], F32))
        yt3 = stk.enter_context(sb("d_yt2", [128, 2 * NT], F32))
        _alloc_ln_scratch(cx, stk, NT)
        p.dma("sync", pn_col[:], cx.dr["L0_attn_post_normc"], w=[("pnc",)])
        for k in range(8):
            p.dma("gpsimd", wga[:, k * HF:(k + 1) * HF],
                  cx.dr["L0_attn_gate_w"][k * 128:(k + 1) * 128, :],
                  w=[("wga", k)])
            p.dma("gpsimd", wup[:, k * HF:(k + 1) * HF],
                  cx.dr["L0_attn_up_w"][k * 128:(k + 1) * 128, :],
                  w=[("wup", k)])
        for k in range(22):
            ks_ = HTS[k]
            p.dma("gpsimd", wdn[:, k * 1024:k * 1024 + 1024][0:ks_, :],
                  cx.dr["L0_attn_down_w"][k * 128:k * 128 + ks_, :],
                  w=[("wdn", k)])
        for sc in range(NSC):
            for j in range(8):
                p.dma("sync", xd[:, j * NT:(j + 1) * NT],
                      cx.dr["s_x1"][j * 128:(j + 1) * 128,
                                    sc * NT:(sc + 1) * NT],
                      r=[("x1d", "x")], w=[("xc", j)])
            emit_rms_chunk(p, cx, "xc", xd, h2b, "h2", NT, pn_col, 1e-6)
            for hh in range(22):
                hs_ = HTS[hh]
                m0 = hh * 128
                pkA, psA = cx.PSK[2], cx.ps[2]
                pkB, psB = cx.PSK[3], cx.ps[3]
                for k in range(8):
                    p.do("tensor", lambda e, hh=hh, k=k, hs_=hs_, m0=m0:
                         e.matmul(
                             psA[:, 0:NT][0:hs_, :],
                             lhsT=wga[:, k * HF + m0:k * HF + m0 + hs_],
                             rhs=h2b[:, k * NT:(k + 1) * NT],
                             start=(k == 0), stop=(k == 7)),
                         r=[("wga", k), ("h2", k)], w=[pkA])
                for k in range(8):
                    p.do("tensor", lambda e, hh=hh, k=k, hs_=hs_, m0=m0:
                         e.matmul(
                             psB[:, 0:NT][0:hs_, :],
                             lhsT=wup[:, k * HF + m0:k * HF + m0 + hs_],
                             rhs=h2b[:, k * NT:(k + 1) * NT],
                             start=(k == 0), stop=(k == 7)),
                         r=[("wup", k), ("h2", k)], w=[pkB])
                p.do("scalar", lambda e, hh=hh, hs_=hs_: e.activation(
                    out=hbuf[:, hh * NT:(hh + 1) * NT][0:hs_, :],
                    in_=psA[:, 0:NT][0:hs_, :], func=AF.Silu),
                    r=[pkA], w=[("hsl", hh)])
                p.do("vector", lambda e, hh=hh, hs_=hs_: e.tensor_mul(
                    hbuf[:, hh * NT:(hh + 1) * NT][0:hs_, :],
                    hbuf[:, hh * NT:(hh + 1) * NT][0:hs_, :],
                    psB[:, 0:NT][0:hs_, :]),
                    r=[("hsl", hh), pkB], w=[("hb2", hh)])
            for dd in range(8):
                pk = cx.PSK[4 + dd % 2]
                ps = cx.ps[4 + dd % 2]
                ysl = slice((dd % 2) * NT, (dd % 2 + 1) * NT)
                for k in range(22):
                    ks_ = HTS[k]
                    p.do("tensor", lambda e, dd=dd, k=k, ks_=ks_, ps=ps:
                         e.matmul(
                             ps[:, 0:NT],
                             lhsT=wdn[:, k * 1024 + dd * 128:
                                      k * 1024 + (dd + 1) * 128][0:ks_, :],
                             rhs=hbuf[:, k * NT:(k + 1) * NT][0:ks_, :],
                             start=(k == 0), stop=(k == 21)),
                         r=[("wdn", k), ("hb2", k)], w=[pk])
                p.dma("sync", xt3[:, ysl],
                      x_in[dd * 128:(dd + 1) * 128, sc * NT:(sc + 1) * NT],
                      r=[(xk, "dram")], w=[("xt", dd % 2)])
                p.do("vector", lambda e, dd=dd, ps=ps, ysl=ysl: e.tensor_add(
                    yt3[:, ysl], ps[:, 0:NT], xd[:, dd * NT:(dd + 1) * NT]),
                    r=[pk, ("xc", dd)], w=[("yt", dd % 2)])
                p.do("vector", lambda e, ysl=ysl: e.tensor_add(
                    yt3[:, ysl], yt3[:, ysl], xt3[:, ysl]),
                    r=[("yt", dd % 2), ("xt", dd % 2)], w=[("yt", dd % 2)])
                p.dma("scalar", x_out[dd * 128:(dd + 1) * 128,
                                      sc * NT:(sc + 1) * NT],
                      yt3[:, ysl], r=[("yt", dd % 2)], w=[(xok, "dram")])
        p.barrier()



STAGES_ALL = ["l0f", "l0a", "l0f", "l1f", "l1a", "l1f",
              "l2f", "l2a", "l2f", "l3f", "l3a", "l3f"]
_STAGE_RANGE = None


def build_program(T, stages):
    nc = bass.Bass("TRN2", target_bir_lowering=False)
    cx = Ctx()
    cx.nc = nc
    cx.T = T
    cx.TT = 2 * T
    cx.NT = 512
    cx.NCH = cx.TT // cx.NT
    cx.dr = {}

    def din(name, shape):
        cx.dr[name] = nc.dram_tensor(name, shape, F32, kind="ExternalInput").ap()

    def dscr(name, shape, dtype=F32):
        cx.dr[name] = nc.dram_tensor(name, shape, dtype, kind="Internal").ap()

    din("x", [D, cx.TT])
    din("c_ident", [128, 128])
    cx.dr["out"] = nc.dram_tensor("out", [D, cx.TT], F32,
                                  kind="ExternalOutput").ap()
    dscr("s_xa", [D, cx.TT])
    dscr("s_xb", [D, cx.TT])
    if "l1f" in stages:
        din("L1_ffn_W1T", [1024, 4096])
        din("L1_ffn_W2T", [4096, 1024])
        din("L1_ffn_b1c", [128, 32])
        din("L1_ffn_b2c", [128, 8])
        dscr("s_wq1", [1024, 4096], BF16)
        dscr("s_wq2", [4096, 1024], BF16)
    if {"l0f", "l2f", "l3f"} & set(stages):
        dscr("s_ln", [D, cx.TT], BF16)
        dscr("s_y", [D, cx.TT], F32)
        dscr("s_ceT", [8, cx.TT], F32)
    if "l0f" in stages:
        din("L0_ffn_W1", [8, 1024, 4096])
        din("L0_ffn_W2", [8, 4096, 1024])
        din("L0_ffn_Wr", [1024, 8])
        din("L0_ffn_b1c", [128, 8 * 32])
        din("L0_ffn_b2t", [8, 1024])
    if "l3f" in stages:
        din("L3_ffn_routed_up", [8, 1024, 4096])
        din("L3_ffn_routed_down", [8, 4096, 1024])
        din("L3_ffn_shared_up", [1, 1024, 4096])
        din("L3_ffn_shared_down", [1, 4096, 1024])
        din("L3_ffn_router_w", [1024, 8])
    if {"l2a", "l1a", "l0a", "l3a"} & set(stages):
        dscr("s_p1", [2048, cx.TT], BF16)
        dscr("s_p2", [2048, cx.TT], BF16)
        dscr("s_p3", [2048, cx.TT], BF16)
        dscr("s_ofm", [2048, cx.TT], BF16)
        if "s_ln" not in cx.dr:
            dscr("s_ln", [D, cx.TT], BF16)
    if "l1a" in stages:
        din("L1_attn_W_in", [1024, 2048])
        din("L1_attn_W_xp", [1024, 1152])
        din("L1_attn_W_dt", [1024, 1024])
        din("L1_attn_b_dtc", [128, 8])
        din("L1_attn_A_logc", [128, 128])
        din("L1_attn_D_paramc", [128, 8])
        din("L1_attn_W_out", [1024, 1024])
        din("L1_attn_ln_gc", [128, 8])
        din("L1_attn_ln_bc", [128, 8])
        dscr("s_bc", [32, cx.TT])
    if {"l0a", "l3a"} & set(stages):
        din("c_mneg", [128, 128])
        din("c_m01s", [128, 128])
        din("c_m01i", [128, 128])
        din("c_mbd", [128, 128])
        din("c_mr", [128, 128])
        dscr("s_gtm", [cx.TT, 2048], BF16)
    if "l0a" in stages:
        for nm, shp in (("q_w", [1024, 1024]), ("k_w", [1024, 1024]),
                        ("v_w", [1024, 2048]), ("g_w", [1024, 2048]),
                        ("b_w", [1024, 16]), ("a_w", [1024, 16]),
                        ("A_log1", [16, 1]), ("dt_bias1", [16, 1]),
                        ("conv_q_wc", [64, 64]), ("conv_k_wc", [64, 64]),
                        ("conv_v_wc", [128, 64]), ("norm_wc", [128, 8]),
                        ("o_norm_w1", [1, 128]), ("o_w", [2048, 1024]),
                        ("post_normc", [128, 8]), ("gate_w", [1024, 2752]),
                        ("up_w", [1024, 2752]), ("down_w", [2752, 1024])):
            din("L0_attn_" + nm, shp)
        dscr("s_g", [16, cx.TT])
        dscr("s_x1", [D, cx.TT])
    if "l3a" in stages:
        for nm, shp in (("Wq", [1024, 1024]), ("Wk", [1024, 1024]),
                        ("Wv", [1024, 1024]), ("Wg", [1024, 1024]),
                        ("Wb", [1024, 16]), ("Wo", [1024, 1024]),
                        ("cq_wc", [64, 64]), ("ck_wc", [64, 64]),
                        ("cv_wc", [64, 64]), ("cq_bc", [64, 16]),
                        ("ck_bc", [64, 16]), ("cv_bc", [64, 16]),
                        ("nw1", [1, 64])):
            din("L3_attn_" + nm, shp)
    if "l2a" in stages:
        din("L2_attn_w_qkv", [1024, 3072])
        din("L2_attn_w_o", [1024, 1024])
        din("L2_attn_log_xi2", [1, 16])
        din("c_swp", [128, 128])
        din("c_ropec", [128, cx.TT])
        din("c_ropes", [128, cx.TT])
        din("c_kposc", [128, 16])
        din("c_dm2t", [128, 2048])
        din("c_ebits", [1, 16])
        dscr("s_vtm", [cx.TT, 1024], BF16)
    if "l2f" in stages:
        din("L2_ffn_ew1T", [6, 1024, 1024])
        din("L2_ffn_ew2T", [6, 1024, 1024])
        din("L2_ffn_ew3T", [6, 1024, 1024])
        din("L2_ffn_fc1T", [1024, 1024])
        din("L2_ffn_fc2T", [1024, 1024])
        din("L2_ffn_fc3T", [1024, 1024])
        din("L2_ffn_gate_wT", [1024, 8])
        din("L2_ffn_gate_b2", [1, 6])

    p = P()
    with contextlib.ExitStack() as stk:
        alloc_psum_pool(cx, stk)
        emit_shared_consts(cx, p, stk)
        bufs = ["s_xa", "s_xb"]
        cur, curk = cx.dr["x"], "x"
        seen_l1f = False
        for i, s in enumerate(stages):
            last = (i == len(stages) - 1)
            nxt = cx.dr["out"] if last else cx.dr[bufs[i % 2]]
            nxtk = "out" if last else bufs[i % 2]
            if s == "l1f":
                emit_l1ffn(p, cx, cur, curk, nxt, nxtk, first=not seen_l1f)
                seen_l1f = True
            elif s in ("l0f", "l2f", "l3f"):
                emit_moe(p, cx, cur, curk, nxt, nxtk, s[:2])
            elif s == "l2a":
                emit_l2attn(p, cx, cur, curk, nxt, nxtk)
            elif s == "l1a":
                emit_l1attn(p, cx, cur, curk, nxt, nxtk)
            elif s in ("l0a", "l3a"):
                emit_delta(p, cx, cur, curk, nxt, nxtk, s[:2])
            else:
                raise NotImplementedError(s)
            cur, curk = nxt, nxtk
        p.barrier()
    p.emit(nc)
    return nc, cx


def _host_consts(T=2048, stages=()):
    c = {"c_ident": np.eye(128, dtype=np.float32)}
    if {"l0a", "l3a"} & set(stages):
        r = np.arange(128)[:, None]
        s = np.arange(128)[None, :]
        c["c_mneg"] = np.where(r < s, 0.0, NEG).astype(np.float32)
        c["c_m01s"] = (r < s).astype(np.float32)
        c["c_m01i"] = (r <= s).astype(np.float32)
        c["c_mbd"] = (((r // 16) == (s // 16)) & (r < s)).astype(np.float32)
        c["c_mr"] = ((r // 16) < (s // 16)).astype(np.float32)
    if "l2a" in stages:
        swp = np.zeros((128, 128), np.float32)
        for i in range(64):
            swp[2 * i, 2 * i + 1] = 1.0
            swp[2 * i + 1, 2 * i] = 1.0
        c["c_swp"] = swp
        p = np.arange(128)
        i_of_p = (p % 64) // 2
        invf = 10000.0 ** (-(np.arange(0, 64, 2, dtype=np.float64) / 64))
        t = np.arange(T, dtype=np.float64)
        theta = t[None, :] * invf[i_of_p][:, None]
        cos = np.cos(theta).astype(np.float32)
        sin = np.sin(theta).astype(np.float32)
        sgn = np.where(p % 2 == 0, -1.0, 1.0).astype(np.float32)[:, None]
        c["c_ropec"] = np.ascontiguousarray(np.tile(cos, (1, 2)))
        c["c_ropes"] = np.ascontiguousarray(np.tile(sin * sgn, (1, 2)))
        kposc = (np.arange(16)[None, :] * 128 +
                 np.arange(128)[:, None]).astype(np.float32)
        c["c_kposc"] = np.ascontiguousarray(kposc)
        dm2t = np.zeros((128, 4, 512), np.float32)
        for v in range(4):
            pp = np.arange(128)[:, None]
            jj = np.arange(512)[None, :]
            dm2t[:, v, :] = np.where(128 * v + pp <= jj, 0.0, NEG)
        c["c_dm2t"] = np.ascontiguousarray(dm2t.reshape(128, 2048))
        c["c_ebits"] = (-(np.arange(16) % 2)).astype(np.float32).reshape(1, 16)
    return c


def _prep_weights(inputs, stages):
    A = lambda v: np.ascontiguousarray(np.asarray(v))
    w = {}
    if "l0f" in stages:
        w["L0_ffn_W1"] = A(inputs["L0_ffn_W1"])
        w["L0_ffn_W2"] = A(inputs["L0_ffn_W2"])
        w["L0_ffn_Wr"] = A(inputs["L0_ffn_Wr"])
        w["L0_ffn_b1c"] = A(np.asarray(inputs["L0_ffn_b1"]).reshape(
            8, 32, 128).transpose(2, 0, 1).reshape(128, 256))
        w["L0_ffn_b2t"] = A(inputs["L0_ffn_b2"])
    if "l3f" in stages:
        for k in ("routed_up", "routed_down", "shared_up", "shared_down",
                  "router_w"):
            w["L3_ffn_" + k] = A(inputs["L3_ffn_" + k])
    def ccol(v, P):
        v = np.asarray(v)
        nt = v.shape[0] // P
        return A(v.reshape(nt, P, v.shape[1]).transpose(1, 0, 2).reshape(
            P, nt * v.shape[1]))

    if "l0a" in stages:
        for nm in ("q_w", "k_w", "v_w", "g_w", "b_w", "a_w", "o_w",
                   "gate_w", "up_w", "down_w"):
            w["L0_attn_" + nm] = A(inputs["L0_attn_" + nm])
        w["L0_attn_A_log1"] = A(np.asarray(
            inputs["L0_attn_A_log"]).reshape(16, 1))
        w["L0_attn_dt_bias1"] = A(np.asarray(
            inputs["L0_attn_dt_bias"]).reshape(16, 1))
        w["L0_attn_conv_q_wc"] = ccol(inputs["L0_attn_conv_q_w"], 64)
        w["L0_attn_conv_k_wc"] = ccol(inputs["L0_attn_conv_k_w"], 64)
        w["L0_attn_conv_v_wc"] = ccol(inputs["L0_attn_conv_v_w"], 128)
        w["L0_attn_norm_wc"] = A(np.asarray(
            inputs["L0_attn_norm_w"]).reshape(8, 128).T)
        w["L0_attn_post_normc"] = A(np.asarray(
            inputs["L0_attn_post_norm_w"]).reshape(8, 128).T)
        w["L0_attn_o_norm_w1"] = A(np.asarray(
            inputs["L0_attn_o_norm_w"]).reshape(1, 128))
    if "l3a" in stages:
        for nm in ("Wq", "Wk", "Wv", "Wg", "Wb", "Wo"):
            w["L3_attn_" + nm] = A(inputs["L3_attn_" + nm])
        for nm in ("cq_w", "ck_w", "cv_w"):
            w["L3_attn_" + nm + "c"] = ccol(inputs["L3_attn_" + nm], 64)
        for nm in ("cq_b", "ck_b", "cv_b"):
            w["L3_attn_" + nm + "c"] = A(np.asarray(
                inputs["L3_attn_" + nm]).reshape(16, 64).T)
        w["L3_attn_nw1"] = A(np.asarray(inputs["L3_attn_nw"]).reshape(1, 64))
    if "l1a" in stages:
        w["L1_attn_W_in"] = A(inputs["L1_attn_W_in"])
        wx = np.asarray(inputs["L1_attn_W_x"])
        w["L1_attn_W_xp"] = A(np.concatenate(
            [wx, np.zeros((1024, 1152 - wx.shape[1]), np.float32)], axis=1))
        w["L1_attn_W_dt"] = A(inputs["L1_attn_W_dt"])
        w["L1_attn_b_dtc"] = A(
            np.asarray(inputs["L1_attn_b_dt"]).reshape(8, 128).T)
        w["L1_attn_A_logc"] = A(np.asarray(inputs["L1_attn_A_log"]).reshape(
            8, 128, 16).transpose(1, 0, 2).reshape(128, 128))
        w["L1_attn_D_paramc"] = A(
            np.asarray(inputs["L1_attn_D_param"]).reshape(8, 128).T)
        w["L1_attn_W_out"] = A(inputs["L1_attn_W_out"])
        w["L1_attn_ln_gc"] = A(
            np.asarray(inputs["L1_attn_ln_g"]).reshape(8, 128).T)
        w["L1_attn_ln_bc"] = A(
            np.asarray(inputs["L1_attn_ln_b"]).reshape(8, 128).T)
    if "l2a" in stages:
        w["L2_attn_w_qkv"] = A(inputs["L2_attn_w_qkv"])
        w["L2_attn_w_o"] = A(inputs["L2_attn_w_o"])
        w["L2_attn_log_xi2"] = A(
            np.asarray(inputs["L2_attn_log_xi"]).reshape(1, 16))
    if "l2f" in stages:
        for k in ("ew1", "ew2", "ew3"):
            w["L2_ffn_" + k + "T"] = A(
                np.asarray(inputs["L2_ffn_" + k]).transpose(0, 2, 1))
        for k in ("fc1", "fc2", "fc3"):
            w["L2_ffn_" + k + "T"] = A(np.asarray(inputs["L2_ffn_" + k]).T)
        w["L2_ffn_gate_wT"] = A(
            np.concatenate([np.asarray(inputs["L2_ffn_gate_w"]).T,
                            np.zeros((1024, 2), np.float32)], axis=1))
        w["L2_ffn_gate_b2"] = A(
            np.asarray(inputs["L2_ffn_gate_b"]).reshape(1, 6))
    if "l1f" in stages:
        w["L1_ffn_W1T"] = np.ascontiguousarray(inputs["L1_ffn_W1"].T)
        w["L1_ffn_W2T"] = np.ascontiguousarray(inputs["L1_ffn_W2"].T)
        w["L1_ffn_b1c"] = np.ascontiguousarray(
            np.asarray(inputs["L1_ffn_b1"]).reshape(32, 128).T)
        w["L1_ffn_b2c"] = np.ascontiguousarray(
            np.asarray(inputs["L1_ffn_b2"]).reshape(8, 128).T)
    return w


def kernel(**inputs):
    T = 2048
    x = np.asarray(inputs["L0_attn_hidden_states"])
    stages = STAGES_ALL if _STAGE_RANGE is None else \
        STAGES_ALL[_STAGE_RANGE[0]:_STAGE_RANGE[1]]
    nc, cx = build_program(T, stages)
    w = _prep_weights(inputs, stages)
    w.update(_host_consts(T, stages))
    in_maps = []
    for c in range(8):
        m = dict(w)
        xc = x[2 * c:2 * c + 2]
        m["x"] = np.ascontiguousarray(xc.transpose(2, 0, 1).reshape(D, 2 * T))
        in_maps.append(m)
    res = run_bass_kernel_spmd(nc, in_maps, core_ids=list(range(8)))
    outs = []
    for c in range(8):
        o = res.results[c]["out"]
        outs.append(o.reshape(D, 2, T).transpose(1, 2, 0))
    return np.concatenate(outs, axis=0).astype(np.float32)


# version 05
# speedup vs baseline: 1.0364x; 1.0364x over previous
import math
import contextlib
import numpy as np
import concourse.bass as bass
import concourse.mybir as mybir
from concourse.bass_utils import run_bass_kernel_spmd

F32 = mybir.dt.float32
BF16 = mybir.dt.bfloat16
AF = mybir.ActivationFunctionType
ALU = mybir.AluOpType
AX = mybir.AxisListType

ENGS = ("sync", "gpsimd", "vector", "scalar", "tensor")
NEG = -1e30
D = 1024



class P:

    def __init__(self):
        self.streams = {e: [] for e in ENGS}
        self.keys = {}
        self.dmac = {e: 0 for e in ENGS}
        self.drained = {e: 0 for e in ENGS}
        self.nid = 0

    def _edges(self, e, cell, r, w):
        deps = []
        seen = set()

        def add(d):
            if d is None or id(d) in seen:
                return
            seen.add(id(d))
            deps.append(d)
        for k in r:
            st = self.keys.get(k)
            if st:
                add(st["w"])
        for k in w:
            st = self.keys.get(k)
            if st:
                add(st["w"])
                for rd in st["r"]:
                    add(rd)
        for d in deps:
            if d["e"] == e == "tensor" and d["kind"] == "op":
                continue
            self._wait_cell(e, d)
        for k in r:
            self.keys.setdefault(k, {"w": None, "r": []})["r"].append(cell)
        for k in w:
            self.keys[k] = {"w": cell, "r": []}

    def _wait_cell(self, e, d):
        if d["kind"] == "dma":
            qe = d["e"]
            if self.drained[qe] < d["idx"]:
                self.drained[qe] = self.dmac[qe]
                st = self.streams[qe]
                pos = len(st)
                while pos > 0 and st[pos - 1][0] != "dma":
                    pos -= 1
                st.insert(pos, ("dwait", qe, self.dmac[qe] * 16))
            if e != qe:
                self.streams[e].append(("dwait", qe, self.drained[qe] * 16))
        else:
            if d["inc"] == 0:
                d["inc"] = 1
            self.streams[e].append(("wait", d))

    def do(self, e, fn, r=(), w=()):
        self.nid += 1
        cell = {"kind": "op", "e": e, "fn": fn, "inc": 0, "id": self.nid}
        self._edges(e, cell, tuple(r), tuple(w))
        self.streams[e].append(("op", cell))
        return cell

    def dma(self, qe, out, in_, r=(), w=()):
        self.nid += 1
        cell = {"kind": "dma", "e": qe, "idx": self.dmac[qe] + 1,
                "id": self.nid}
        self._edges(qe, cell, tuple(r), tuple(w))
        self.dmac[qe] += 1
        self.streams[qe].append(("dma", out, in_))
        return cell

    def read_key(self, e_or_none, k):
        st = self.keys.get(k)
        if st and st["w"] is not None:
            self._wait_cell(e_or_none, st["w"])

    def barrier(self):
        cells = {}
        for e in ENGS:
            cell = None
            for item in reversed(self.streams[e]):
                if item[0] == "op":
                    cell = item[1]
                    break
            if cell is None:
                self.nid += 1
                cell = {"kind": "op", "e": e, "inc": 0, "id": self.nid,
                        "fn": lambda eng: eng.nop()}
                self.streams[e].append(("op", cell))
            cell["inc"] += 1
            cells[e] = cell
        for e in ENGS:
            if self.dmac[e] > self.drained[e]:
                self.drained[e] = self.dmac[e]
                self.streams[e].append(("dwait", e, self.dmac[e] * 16))
        for e in ENGS:
            for o in ENGS:
                if o != e:
                    self.streams[e].append(("wait", cells[o]))
            for o in ENGS:
                if o != e and self.dmac[o]:
                    self.streams[e].append(("dwait", o, self.dmac[o] * 16))
        self.keys = {}

    def emit(self, nc):
        for e in ENGS:
            v = 0
            for item in self.streams[e]:
                if item[0] == "op" and item[1]["inc"]:
                    v += item[1]["inc"]
                    item[1]["val"] = v
        with contextlib.ExitStack() as stk:
            S = {}
            for e in ENGS:
                S[e] = stk.enter_context(nc.semaphore("sem_" + e))
                S[e + "@dma"] = stk.enter_context(nc.semaphore("dma_" + e))
            blk = stk.enter_context(nc.Block())

            def section(e):
                def run(eng):
                    hw = {}
                    for item in self.streams[e]:
                        kind = item[0]
                        if kind == "op":
                            cell = item[1]
                            ins = cell["fn"](eng)
                            if cell["inc"]:
                                ins.then_inc(S[e], cell["inc"])
                        elif kind == "dma":
                            eng.dma_start(out=item[1], in_=item[2]).then_inc(
                                S[e + "@dma"], 16)
                        elif kind == "wait":
                            cell = item[1]
                            sname = cell["e"]
                            v = cell["val"]
                            if hw.get(sname, -1) < v:
                                hw[sname] = v
                                eng.wait_ge(S[sname], v)
                        elif kind == "dwait":
                            sname = item[1] + "@dma"
                            v = item[2]
                            if hw.get(sname, -1) < v:
                                hw[sname] = v
                                eng.wait_ge(S[sname], v)
                return run

            blk.sync(section("sync"))
            blk.gpsimd(section("gpsimd"))
            blk.vector(section("vector"))
            blk.scalar(section("scalar"))
            blk.tensor(section("tensor"))


class Ctx:
    _n = 0

    def sbt(self, name, shape, dtype):
        Ctx._n += 1
        return self.nc.sbuf_tensor(f"{name}_{Ctx._n}", shape, dtype)



def alloc_psum_pool(cx, stk):
    cx.ps = [stk.enter_context(cx.nc.psum_tensor(f"psb{i}", [128, 512], F32))
             for i in range(7)]
    cx.psbf = stk.enter_context(cx.nc.psum_tensor("psbf", [128, 1024], BF16))
    cx.PSK = [("psum", i) for i in range(7)] + [("psum", "bf")]


def emit_shared_consts(cx, p, stk):
    nc = cx.nc
    cx.ones_bf = stk.enter_context(cx.sbt("ones_bf", [128, 1], BF16))
    cx.ones_f = stk.enter_context(cx.sbt("ones_f", [128, 1], F32))
    cx.ones_row_f = stk.enter_context(cx.sbt("ones_row_f", [1, 128], F32))
    cx.ones_row_bf = stk.enter_context(
        nc.sbuf_tensor("ones_row_bf", [1, 128], BF16))
    cx.id_bf = stk.enter_context(cx.sbt("id_bf", [128, 128], BF16))
    cx.id_f = stk.enter_context(cx.sbt("id_f", [128, 128], F32))
    cx.cst = stk.enter_context(cx.sbt("cst", [128, 8], F32))
    cx.CVALS = {1e-5: 0, 1e-6: 1, 1.0: 2, 0.0: 3, 1e-8: 4}

    def cval(v, parts=128):
        i = cx.CVALS[v]
        return cx.cst[0:parts, i:i + 1]
    cx.cval = cval
    p.dma("sync", cx.id_f[:], cx.dr["c_ident"], w=[("c", "idf")])
    for v, i in cx.CVALS.items():
        p.do("vector", lambda e, v=v, i=i: e.memset(cx.cst[:, i:i + 1], v),
             w=[("c", "cst")])
    p.do("vector", lambda e: e.memset(cx.ones_f[:], 1.0), w=[("c", "o")])
    p.do("vector", lambda e: e.memset(cx.ones_row_f[:], 1.0), w=[("c", "orf")])
    p.do("vector", lambda e: e.memset(cx.ones_row_bf[:], 1.0), w=[("c", "orb")])
    p.do("vector", lambda e: e.memset(cx.ones_bf[:], 1.0), w=[("c", "ob")])
    p.do("vector", lambda e: e.tensor_copy(out=cx.id_bf[:], in_=cx.id_f[:]),
         r=[("c", "idf")], w=[("c", "idb")])
    p.barrier()


def _alloc_ln_scratch(cx, stk, NT):
    cx.ln_xbf = stk.enter_context(cx.sbt("lnxbf", [128, 2 * NT], BF16))
    cx.ln_x2bf = stk.enter_context(cx.sbt("lnx2", [128, 2 * NT], BF16))
    cx.ln_m = stk.enter_context(cx.sbt("lnm", [1, NT], F32))
    cx.ln_mm2 = stk.enter_context(cx.sbt("lnmm2", [1, NT], F32))
    cx.ln_t1 = stk.enter_context(cx.sbt("lnt1", [1, NT], F32))
    cx.ln_tmp = stk.enter_context(cx.sbt("lntmp", [128, NT], F32))


def emit_ln_chunk(p, cx, xck, xc, ln_bf, lnk, NT, affine=None):
    ps_s, ps_s2, ps_b1, ps_b2 = cx.ps[0], cx.ps[1], cx.ps[2], cx.ps[3]
    K = cx.PSK
    xbf, x2bf = cx.ln_xbf, cx.ln_x2bf
    for j in range(8):
        sl = slice(j * NT, (j + 1) * NT)
        ds = slice((j % 2) * NT, (j % 2 + 1) * NT)
        p.do("vector", lambda e, sl=sl, ds=ds: e.tensor_copy(
            out=xbf[:, ds], in_=xc[:, sl]),
            r=[(xck, j)], w=[("lnxbf", j % 2)])
        p.do("scalar", lambda e, sl=sl, ds=ds: e.activation(
            out=x2bf[:, ds], in_=xc[:, sl], func=AF.Square),
            r=[(xck, j)], w=[("lnx2", j % 2)])
        p.do("tensor", lambda e, ds=ds, j=j: e.matmul(
            ps_s[0:1, 0:NT], lhsT=cx.ones_bf[:], rhs=xbf[:, ds],
            start=(j == 0), stop=(j == 7)),
            r=[("lnxbf", j % 2)], w=[K[0]])
        p.do("tensor", lambda e, ds=ds, j=j: e.matmul(
            ps_s2[0:1, 0:NT], lhsT=cx.ones_bf[:], rhs=x2bf[:, ds],
            start=(j == 0), stop=(j == 7)),
            r=[("lnx2", j % 2)], w=[K[1]])
    m, t1, mm2, tmp = cx.ln_m, cx.ln_t1, cx.ln_mm2, cx.ln_tmp
    p.do("vector", lambda e: e.tensor_scalar(
        out=m[0:1, 0:NT], in0=ps_s[0:1, 0:NT], scalar1=1.0 / D, scalar2=None,
        op0=ALU.mult), r=[K[0]], w=["ln_m"])
    p.do("vector", lambda e: e.tensor_scalar(
        out=t1[0:1, 0:NT], in0=ps_s2[0:1, 0:NT], scalar1=1.0 / D, scalar2=None,
        op0=ALU.mult), r=[K[1]], w=["ln_t1"])
    p.do("vector", lambda e: e.tensor_mul(mm2[0:1, 0:NT], m[0:1, 0:NT],
                                          m[0:1, 0:NT]),
         r=["ln_m"], w=["ln_mm2"])
    p.do("vector", lambda e: e.tensor_sub(t1[0:1, 0:NT], t1[0:1, 0:NT],
                                          mm2[0:1, 0:NT]),
         r=["ln_t1", "ln_mm2"], w=["ln_t1"])
    p.do("scalar", lambda e: e.activation(out=mm2[0:1, 0:NT], in_=t1[0:1, 0:NT],
                                          func=AF.Sqrt, bias=cx.cval(1e-5, 1)),
         r=["ln_t1"], w=["ln_mm2"])
    p.do("vector", lambda e: e.reciprocal(out=t1[0:1, 0:NT],
                                          in_=mm2[0:1, 0:NT]),
         r=["ln_mm2"], w=["ln_t1"])
    rstd = t1
    p.do("vector", lambda e: e.scalar_tensor_tensor(
        out=m[0:1, 0:NT], in0=m[0:1, 0:NT], scalar=-1.0,
        in1=rstd[0:1, 0:NT], op0=ALU.mult, op1=ALU.mult),
        r=["ln_m", "ln_t1"], w=["ln_m"])
    offs = m
    p.do("tensor", lambda e: e.matmul(
        ps_b1[:, 0:NT], lhsT=cx.ones_row_f[:], rhs=rstd[0:1, 0:NT],
        start=True, stop=True), r=["ln_t1"], w=[K[2]])
    p.do("tensor", lambda e: e.matmul(
        ps_b2[:, 0:NT], lhsT=cx.ones_row_f[:], rhs=offs[0:1, 0:NT],
        start=True, stop=True), r=["ln_m"], w=[K[3]])
    for j in range(8):
        sl = slice(j * NT, (j + 1) * NT)
        p.do("vector", lambda e, sl=sl: e.tensor_mul(
            tmp[:, 0:NT], xc[:, sl], ps_b1[:, 0:NT]),
            r=[(xck, j), K[2]], w=["ln_tmp"])
        if affine is None:
            p.do("vector", lambda e, sl=sl: e.tensor_add(
                ln_bf[:, sl], tmp[:, 0:NT], ps_b2[:, 0:NT]),
                r=["ln_tmp", K[3]], w=[(lnk, j)])
        else:
            g_col, b_col = affine
            p.do("vector", lambda e, sl=sl: e.tensor_add(
                tmp[:, 0:NT], tmp[:, 0:NT], ps_b2[:, 0:NT]),
                r=["ln_tmp", K[3]], w=["ln_tmp"])
            p.do("vector", lambda e, sl=sl, j=j: e.tensor_scalar(
                out=ln_bf[:, sl], in0=tmp[:, 0:NT],
                scalar1=g_col[:, j:j + 1], scalar2=b_col[:, j:j + 1],
                op0=ALU.mult, op1=ALU.add),
                r=["ln_tmp", ("c", "aff")], w=[(lnk, j)])


def load_x_chunk(p, cx, x_dram, xsrck, c, xc, xck, NT):
    dv = x_dram.rearrange("(j p) t -> p j t", p=128)
    sv = xc.rearrange("p (j t) -> p j t", t=NT)
    p.dma("sync", sv[:, 0:8, :], dv[:, :, c * NT:(c + 1) * NT],
          r=[(xsrck, "dram")], w=[(xck, j) for j in range(8)])



def emit_quantize(p, cx, w_dram, wq_dram, KT, N, tag):
    nel = KT * 128 * N
    wt, acc, gcol = cx.qz_wt, cx.qz_acc, cx.qz_gcol
    ps = cx.ps[4]
    K4 = cx.PSK[4]
    for i in range(KT):
        p.dma("sync", wt[:, 0:N], w_dram[i * 128:(i + 1) * 128, :],
              w=[("qzwt",)])
        p.do("vector", lambda e, i=i: e.tensor_reduce(
            out=acc[:, i:i + 1], in_=wt[:, 0:N], axis=AX.X, op=ALU.add,
            apply_absolute_value=True), r=[("qzwt",)], w=[("qzacc", i)])
    p.do("vector", lambda e: e.tensor_reduce(
        out=acc[:, KT:KT + 1], in_=acc[:, 0:KT], axis=AX.X, op=ALU.add),
        r=[("qzacc", i) for i in range(KT)], w=[("qzacc", "t")])
    p.do("vector", lambda e: e.tensor_copy(out=cx.qz_accbf[:],
                                           in_=acc[:, KT:KT + 1]),
         r=[("qzacc", "t")], w=[("qzaccbf",)])
    p.do("tensor", lambda e: e.matmul(ps[0:1, 0:1], lhsT=cx.ones_bf[:],
                                      rhs=cx.qz_accbf[:], start=True, stop=True),
         r=[("qzaccbf",)], w=[K4])
    p.do("vector", lambda e: e.tensor_scalar(
        out=cx.qz_g[0:1, 0:1], in0=ps[0:1, 0:1], scalar1=1.0 / nel,
        scalar2=1e-5, op0=ALU.mult, op1=ALU.add), r=[K4], w=[("qzg", 0)])
    p.do("vector", lambda e: e.reciprocal(out=cx.qz_g[0:1, 1:2],
                                          in_=cx.qz_g[0:1, 0:1]),
         r=[("qzg", 0)], w=[("qzg", 1)])
    p.do("tensor", lambda e: e.matmul(ps[:, 0:1], lhsT=cx.ones_row_f[:],
                                      rhs=cx.qz_g[0:1, 1:2], start=True,
                                      stop=True), r=[("qzg", 1)], w=[K4])
    p.do("vector", lambda e: e.tensor_copy(out=gcol[:], in_=ps[:, 0:1]),
         r=[K4], w=[("qzgcol",)])
    MAGIC = 12582912.0
    for i in range(KT):
        p.dma("sync", wt[:, 0:N], w_dram[i * 128:(i + 1) * 128, :],
              w=[("qzwt",)])
        p.do("vector", lambda e: e.tensor_scalar(
            out=wt[:, 0:N], in0=wt[:, 0:N], scalar1=gcol[:, 0:1],
            scalar2=MAGIC, op0=ALU.mult, op1=ALU.add),
            r=[("qzwt",), ("qzgcol",)], w=[("qzwt",)])
        p.do("vector", lambda e: e.tensor_scalar(
            out=wt[:, 0:N], in0=wt[:, 0:N], scalar1=MAGIC, scalar2=None,
            op0=ALU.subtract), r=[("qzwt",)], w=[("qzwt",)])
        p.do("vector", lambda e: e.tensor_scalar(
            out=cx.qz_wq[:, 0:N], in0=wt[:, 0:N], scalar1=-1.0, scalar2=1.0,
            op0=ALU.max, op1=ALU.min), r=[("qzwt",)], w=[("qzwq",)])
        p.dma("scalar", wq_dram[i * 128:(i + 1) * 128, :], cx.qz_wq[:, 0:N],
              r=[("qzwq",)], w=[(tag, "dram")])


def emit_l1ffn(p, cx, x_in, xk, x_out, xok, first):
    nc = cx.nc
    NT, NCH = cx.NT, cx.NCH
    if first:
        with contextlib.ExitStack() as qstk:
            cx.qz_wt = qstk.enter_context(cx.sbt("qzwt", [128, 4096], F32))
            cx.qz_wq = qstk.enter_context(cx.sbt("qzwq", [128, 4096], BF16))
            cx.qz_acc = qstk.enter_context(cx.sbt("qzacc", [128, 40], F32))
            cx.qz_accbf = qstk.enter_context(cx.sbt("qzabf", [128, 1], BF16))
            cx.qz_gcol = qstk.enter_context(cx.sbt("qzg", [128, 1], F32))
            cx.qz_g = qstk.enter_context(cx.sbt("qzg2", [1, 2], F32))
            emit_quantize(p, cx, cx.dr["L1_ffn_W1T"], cx.dr["s_wq1"], 8, 4096,
                          "wq1")
            emit_quantize(p, cx, cx.dr["L1_ffn_W2T"], cx.dr["s_wq2"], 32, 1024,
                          "wq2")
            p.barrier()
    with contextlib.ExitStack() as stk:
        w1 = stk.enter_context(cx.sbt("l1w1", [128, 8 * 4096], BF16))
        w2 = stk.enter_context(cx.sbt("l1w2", [128, 32 * 1024], BF16))
        b1c = stk.enter_context(cx.sbt("l1b1", [128, 32], F32))
        b2c = stk.enter_context(cx.sbt("l1b2", [128, 8], F32))
        xc = stk.enter_context(cx.sbt("l1xc", [128, 8 * NT], F32))
        lnb = stk.enter_context(cx.sbt("l1ln", [128, 8 * NT], BF16))
        hbf = stk.enter_context(cx.sbt("l1h", [128, 32 * NT], BF16))
        yac = stk.enter_context(cx.sbt("l1y", [128, 4 * NT], F32))
        _alloc_ln_scratch(cx, stk, NT)
        p.dma("sync", w1.rearrange("p (i n) -> p i n", n=4096),
              cx.dr["s_wq1"].rearrange("(i p) n -> p i n", p=128),
              r=[("wq1", "dram")], w=[("w1", i) for i in range(8)])
        p.dma("sync", w2.rearrange("p (i n) -> p i n", n=1024),
              cx.dr["s_wq2"].rearrange("(i p) n -> p i n", p=128),
              r=[("wq2", "dram")], w=[("w2", i) for i in range(32)])
        p.dma("sync", b1c[:], cx.dr["L1_ffn_b1c"], w=[("b1c",)])
        p.dma("sync", b2c[:], cx.dr["L1_ffn_b2c"], w=[("b2c",)])
        for c in range(NCH):
            load_x_chunk(p, cx, x_in, xk, c, xc, "xc", NT)
            emit_ln_chunk(p, cx, "xc", xc, lnb, "ln", NT)
            for hh in range(32):
                ps = cx.ps[hh % 2]
                pk = cx.PSK[hh % 2]
                for k in range(8):
                    p.do("tensor", lambda e, hh=hh, k=k, ps=ps: e.matmul(
                        ps[:, 0:NT], lhsT=w1[:, k * 4096 + hh * 128:
                                             k * 4096 + (hh + 1) * 128],
                        rhs=lnb[:, k * NT:(k + 1) * NT],
                        start=(k == 0), stop=(k == 7)),
                        r=[("w1", k), ("ln", k)], w=[pk])
                p.do("scalar", lambda e, hh=hh, ps=ps: e.activation(
                    out=hbf[:, hh * NT:(hh + 1) * NT], in_=ps[:, 0:NT],
                    func=AF.Relu, bias=b1c[:, hh:hh + 1]),
                    r=[pk, ("b1c",)], w=[("h", hh)])
            for dd in range(8):
                ps = cx.ps[2 + dd % 2]
                pk = cx.PSK[2 + dd % 2]
                ysl = slice((dd % 4) * NT, (dd % 4 + 1) * NT)
                for k in range(32):
                    p.do("tensor", lambda e, dd=dd, k=k, ps=ps: e.matmul(
                        ps[:, 0:NT], lhsT=w2[:, k * 1024 + dd * 128:
                                             k * 1024 + (dd + 1) * 128],
                        rhs=hbf[:, k * NT:(k + 1) * NT],
                        start=(k == 0), stop=(k == 31)),
                        r=[("w2", k), ("h", k)], w=[pk])
                p.do("vector", lambda e, dd=dd, ps=ps, ysl=ysl: e.tensor_scalar(
                    out=yac[:, ysl], in0=ps[:, 0:NT],
                    scalar1=b2c[:, dd:dd + 1], scalar2=0.5,
                    op0=ALU.add, op1=ALU.mult),
                    r=[pk, ("b2c",)], w=[("y", dd % 4)])
                p.do("vector", lambda e, dd=dd, ysl=ysl: e.tensor_add(
                    yac[:, ysl], yac[:, ysl], xc[:, dd * NT:(dd + 1) * NT]),
                    r=[("y", dd % 4), ("xc", dd)], w=[("y", dd % 4)])
                if dd % 4 == 3:
                    jb = dd - 3
                    p.dma("scalar",
                          x_out.rearrange("(j p) t -> p j t", p=128)[
                              :, jb // 4 * 4:jb // 4 * 4 + 4,
                              c * NT:(c + 1) * NT],
                          yac.rearrange("p (j t) -> p j t", t=NT),
                          r=[("y", q) for q in range(4)],
                          w=[(xok, "dram")])
        p.barrier()




def _routing_top2(p, cx, pk, ps, E, tt, kind, ceT_sb, cek, rt):
    ce, sc1, sc2, eqs = rt
    ek = ("rt_ce",)
    K = pk
    if kind == "l2":
        gbb = cx.gbb
        p.do("vector", lambda e, gbb=gbb: e.tensor_add(
            eqs[:, 0:E], ps[:, 0:E], gbb[:, 0:E]),
             r=[K, ("gbb",)], w=[("rt_lg",)])
        lg, lgk = eqs[:, 0:E], ("rt_lg",)

        def LG(sl):
            return lg[:, sl]
    else:
        lg, lgk = ps, K

        def LG(sl):
            return lg[:, sl]
    p.do("vector", lambda e: e.tensor_reduce(
        out=sc1[:, 0:1], in_=LG(slice(0, E)), axis=AX.X, op=ALU.max),
        r=[lgk], w=[("rt_m1",)])
    p.do("vector", lambda e: e.tensor_scalar(
        out=ce[:, 0:E], in0=LG(slice(0, E)), scalar1=sc1[:, 0:1], scalar2=None,
        op0=ALU.is_ge), r=[lgk, ("rt_m1",)], w=[ek])
    p.do("vector", lambda e: e.scalar_tensor_tensor(
        out=eqs[:, E:2 * E], in0=ce[:, 0:E], scalar=NEG, in1=LG(slice(0, E)),
        op0=ALU.mult, op1=ALU.add), r=[ek, lgk], w=[("rt_x2",)])
    p.do("vector", lambda e: e.tensor_reduce(
        out=sc1[:, 1:2], in_=eqs[:, E:2 * E], axis=AX.X, op=ALU.max),
        r=[("rt_x2",)], w=[("rt_m2",)])
    p.do("vector", lambda e: e.tensor_scalar(
        out=eqs[:, E:2 * E], in0=eqs[:, E:2 * E], scalar1=sc1[:, 1:2],
        scalar2=None, op0=ALU.is_ge),
        r=[("rt_x2",), ("rt_m2",)], w=[("rt_eq2",)])
    if kind == "l0":
        p.do("vector", lambda e: e.tensor_sub(sc1[:, 2:3], sc1[:, 1:2],
                                              sc1[:, 0:1]),
             r=[("rt_m1",), ("rt_m2",)], w=[("rt_d",)])
        p.do("scalar", lambda e: e.activation(out=sc2[:, 0:1], in_=sc1[:, 2:3],
                                              func=AF.Exp),
             r=[("rt_d",)], w=[("rt_e",)])
        p.do("vector", lambda e: e.tensor_scalar(
            out=sc2[:, 1:2], in0=sc2[:, 0:1], scalar1=1.0, scalar2=None,
            op0=ALU.add), r=[("rt_e",)], w=[("rt_den",)])
        p.do("vector", lambda e: e.reciprocal(out=sc2[:, 2:3],
                                              in_=sc2[:, 1:2]),
             r=[("rt_den",)], w=[("rt_r",)])
        p.do("vector", lambda e: e.tensor_mul(sc2[:, 3:4], sc2[:, 0:1],
                                              sc2[:, 2:3]),
             r=[("rt_e",), ("rt_r",)], w=[("rt_w2",)])
        p.do("vector", lambda e: e.tensor_scalar(
            out=ce[:, 0:E], in0=ce[:, 0:E], scalar1=sc2[:, 2:3], scalar2=None,
            op0=ALU.mult), r=[ek, ("rt_r",)], w=[ek])
        p.do("vector", lambda e: e.scalar_tensor_tensor(
            out=ce[:, 0:E], in0=eqs[:, E:2 * E], scalar=sc2[:, 3:4],
            in1=ce[:, 0:E], op0=ALU.mult, op1=ALU.add),
            r=[("rt_eq2",), ("rt_w2",), ek], w=[ek])
    elif kind == "l2":
        p.do("scalar", lambda e: e.activation(out=sc2[:, 0:1], in_=sc1[:, 0:1],
                                              func=AF.Sigmoid),
             r=[("rt_m1",)], w=[("rt_s1",)])
        p.do("scalar", lambda e: e.activation(out=sc2[:, 1:2], in_=sc1[:, 1:2],
                                              func=AF.Sigmoid),
             r=[("rt_m2",)], w=[("rt_s2",)])
        p.do("vector", lambda e: e.tensor_scalar(
            out=sc2[:, 2:3], in0=sc2[:, 1:2], scalar1=sc2[:, 0:1],
            scalar2=1e-8, op0=ALU.add, op1=ALU.add),
            r=[("rt_s1",), ("rt_s2",)], w=[("rt_den",)])
        p.do("vector", lambda e: e.reciprocal(out=sc2[:, 2:3],
                                              in_=sc2[:, 2:3]),
             r=[("rt_den",)], w=[("rt_den",)])
        p.do("vector", lambda e: e.tensor_mul(sc1[:, 2:3], sc2[:, 0:1],
                                              sc2[:, 2:3]),
             r=[("rt_s1",), ("rt_den",)], w=[("rt_w1",)])
        p.do("vector", lambda e: e.tensor_mul(sc1[:, 3:4], sc2[:, 1:2],
                                              sc2[:, 2:3]),
             r=[("rt_s2",), ("rt_den",)], w=[("rt_w2",)])
        p.do("vector", lambda e: e.tensor_scalar(
            out=ce[:, 0:E], in0=ce[:, 0:E], scalar1=sc1[:, 2:3], scalar2=None,
            op0=ALU.mult), r=[ek, ("rt_w1",)], w=[ek])
        p.do("vector", lambda e: e.scalar_tensor_tensor(
            out=ce[:, 0:E], in0=eqs[:, E:2 * E], scalar=sc1[:, 3:4],
            in1=ce[:, 0:E], op0=ALU.mult, op1=ALU.add),
            r=[("rt_eq2",), ("rt_w2",), ek], w=[ek])
    elif kind == "l3":
        p.do("vector", lambda e: e.tensor_reduce(
            out=sc2[:, 0:1], in_=LG(slice(0, E)), axis=AX.X, op=ALU.max,
            negate=True), r=[lgk], w=[("rt_nm",)])
        p.do("scalar", lambda e: e.activation(
            out=eqs[:, 0:E], in_=LG(slice(0, E)), func=AF.Exp,
            bias=sc2[:, 0:1]), r=[lgk, ("rt_nm",)], w=[("rt_p",)])
        p.do("vector", lambda e: e.tensor_reduce(
            out=sc2[:, 1:2], in_=eqs[:, 0:E], axis=AX.X, op=ALU.add),
            r=[("rt_p",)], w=[("rt_s",)])
        p.do("vector", lambda e: e.reciprocal(out=sc2[:, 2:3],
                                              in_=sc2[:, 1:2]),
             r=[("rt_s",)], w=[("rt_s",)])
        p.do("vector", lambda e: e.tensor_add(
            ce[:, 0:E], ce[:, 0:E], eqs[:, E:2 * E]),
            r=[ek, ("rt_eq2",)], w=[ek])
        p.do("vector", lambda e: e.tensor_mul(
            ce[:, 0:E], ce[:, 0:E], eqs[:, 0:E]),
            r=[ek, ("rt_p",)], w=[ek])
        p.do("vector", lambda e: e.tensor_scalar(
            out=ce[:, 0:E], in0=ce[:, 0:E], scalar1=sc2[:, 2:3], scalar2=None,
            op0=ALU.mult), r=[ek, ("rt_s",)], w=[ek])
    p.do("tensor", lambda e: e.transpose(cx.ps[6][0:E, 0:128], ce[:, 0:E],
                                         cx.id_f[:]),
         r=[ek], w=[cx.PSK[6]])
    p.do("vector", lambda e, tt=tt: e.tensor_copy(
        out=ceT_sb[0:E, tt[1] * 128:(tt[1] + 1) * 128],
        in_=cx.ps[6][0:E, 0:128]),
        r=[cx.PSK[6]], w=[("ceTs",)])


def emit_moe(p, cx, x_in, xk, x_out, xok, kind):
    nc = cx.nc
    NT, TT = cx.NT, cx.TT
    NSC = TT // NT
    if kind == "l0":
        E, H, EALL = 8, 4096, 8
    elif kind == "l3":
        E, H, EALL = 8, 4096, 9
    else:
        E, H, EALL = 6, 1024, 7
    HT = H // 128
    sb = cx.sbt

    with contextlib.ExitStack() as stk:
        xc = stk.enter_context(sb("m_xc", [128, 8 * NT], F32))
        lnr = stk.enter_context(sb("m_lnr", [128, 8 * NT], BF16))
        ceT = stk.enter_context(sb("m_ceT", [8, NT], F32))
        yri = stk.enter_context(sb("m_yt", [128, 8 * NT], F32))
        rt_ce = stk.enter_context(sb("m_rce", [128, E], F32))
        rt_s1 = stk.enter_context(sb("m_rs1", [128, 4], F32))
        rt_s2 = stk.enter_context(sb("m_rs2", [128, 4], F32))
        rt_eq = stk.enter_context(sb("m_req", [128, 2 * E], F32))
        wr = stk.enter_context(sb("m_wr", [128, 8 * E], BF16))
        _alloc_ln_scratch(cx, stk, NT)
        rt = (rt_ce, rt_s1, rt_s2, rt_eq)
        if kind == "l0":
            b2t = stk.enter_context(sb("m_b2", [8, 1024], F32))
            p.dma("sync", b2t[:], cx.dr["L0_ffn_b2t"], w=[("b2t",)])
            for k in range(8):
                p.dma("gpsimd", wr[:, k * E:(k + 1) * E],
                      cx.dr["L0_ffn_Wr"][k * 128:(k + 1) * 128, :],
                      w=[("wr",)])
        elif kind == "l3":
            for k in range(8):
                p.dma("gpsimd", wr[:, k * E:(k + 1) * E],
                      cx.dr["L3_ffn_router_w"][k * 128:(k + 1) * 128, :],
                      w=[("wr",)])
        else:
            gb = stk.enter_context(sb("m_gb", [1, 8], F32))
            cx.gbb = gbbl = stk.enter_context(sb("m_gbb", [128, 8], F32))
            p.dma("sync", gb[0:1, 0:6], cx.dr["L2_ffn_gate_b2"],
                  w=[("gb",)])
            for k in range(8):
                p.dma("gpsimd", wr[:, k * E:(k + 1) * E],
                      cx.dr["L2_ffn_gate_wT"][k * 128:(k + 1) * 128, 0:E],
                      w=[("wr",)])
            p.do("tensor", lambda e: e.matmul(
                cx.ps[6][:, 0:6], lhsT=cx.ones_row_f[:], rhs=gb[0:1, 0:6],
                start=True, stop=True), r=[("gb",)], w=[cx.PSK[6]])
            p.do("vector", lambda e, gbbl=gbbl: e.tensor_copy(
                out=gbbl[:, 0:6], in_=cx.ps[6][:, 0:6]),
                 r=[cx.PSK[6]], w=[("gbb",)])
        for sc in range(NSC):
            load_x_chunk(p, cx, x_in, xk, sc, xc, "xc", NT)
            emit_ln_chunk(p, cx, "xc", xc, lnr, "lnbuf0", NT)
            p.dma("scalar",
                  cx.dr["s_ln"].rearrange("(j p) t -> p j t", p=128)[
                      :, :, sc * NT:(sc + 1) * NT],
                  lnr.rearrange("p (j t) -> p j t", t=NT),
                  r=[("lnbuf0", j) for j in range(8)],
                  w=[("lndram", sc, j) for j in range(8)])
            for tt in range(NT // 128):
                K = cx.PSK[4]
                for k in range(8):
                    p.do("tensor", lambda e, k=k, tt=tt: e.matmul(
                        cx.ps[4][:, 0:E],
                        lhsT=lnr[:, k * NT + tt * 128:k * NT + (tt + 1) * 128],
                        rhs=wr[:, k * E:(k + 1) * E],
                        start=(k == 0), stop=(k == 7)),
                        r=[("lnbuf0", k), ("wr",)], w=[K])
                _routing_top2(p, cx, K, cx.ps[4], E, (sc, tt), kind, ceT,
                              None, rt)
            p.dma("scalar", cx.dr["s_ceT"][0:E, sc * NT:(sc + 1) * NT],
                  ceT[0:E, 0:NT],
                  r=[("ceTs",)], w=[("ceTd", sc)])
            if kind == "l0":
                for dd in range(8):
                    p.do("tensor", lambda e, dd=dd, sc=sc: e.matmul(
                        cx.ps[5][:, 0:NT], lhsT=b2t[:, dd * 128:(dd + 1) * 128],
                        rhs=ceT[0:E, 0:NT],
                        start=True, stop=True),
                        r=[("b2t",), ("ceTs",)], w=[cx.PSK[5]])
                    p.do("vector", lambda e, yri=yri, dd=dd: e.tensor_copy(
                        out=yri[:, dd * NT:(dd + 1) * NT],
                        in_=cx.ps[5][:, 0:NT]),
                        r=[cx.PSK[5]], w=[("yri", dd)])
                p.dma("scalar",
                      cx.dr["s_y"].rearrange("(j p) t -> p j t", p=128)[
                          :, :, sc * NT:(sc + 1) * NT],
                      yri.rearrange("p (j t) -> p j t", t=NT),
                      r=[("yri", dd) for dd in range(8)],
                      w=[("ydram", dd, sc) for dd in range(8)])
    p.barrier()

    def wsrc(kind_, e):
        if kind == "l0":
            if kind_ == 1:
                return cx.dr["L0_ffn_W1"][e]
            return cx.dr["L0_ffn_W2"][e]
        if kind == "l3":
            if e < 8:
                if kind_ == 1:
                    return cx.dr["L3_ffn_routed_up"][e]
                return cx.dr["L3_ffn_routed_down"][e]
            if kind_ == 1:
                return cx.dr["L3_ffn_shared_up"][0]
            return cx.dr["L3_ffn_shared_down"][0]
        if e < 6:
            if kind_ == 1:
                return cx.dr["L2_ffn_ew1T"][e]
            if kind_ == 3:
                return cx.dr["L2_ffn_ew3T"][e]
            return cx.dr["L2_ffn_ew2T"][e]
        if kind_ == 1:
            return cx.dr["L2_ffn_fc1T"]
        if kind_ == 3:
            return cx.dr["L2_ffn_fc2T"]
        return cx.dr["L2_ffn_fc3T"]

    with contextlib.ExitStack() as stk:
        w1 = stk.enter_context(sb("m_w1", [128, 8 * H], BF16))
        w3 = None
        if kind == "l2":
            w3 = stk.enter_context(sb("m_w3", [128, 8 * H], BF16))
        w2 = stk.enter_context(sb("m_w2", [128, HT * 1024], BF16))
        hbf = stk.enter_context(sb("m_h", [128, HT * NT], BF16))
        lnt = stk.enter_context(sb("m_ln", [128, 8 * NT], BF16))
        yt = stk.enter_context(sb("m_y", [128, 8 * NT], F32))
        xt2 = stk.enter_context(sb("m_x2", [128, 8 * NT], F32))
        bc = stk.enter_context(sb("m_bc", [128, 2 * NT], F32))
        cerow = bc[0:1, NT:2 * NT]
        if kind == "l0":
            b1c = stk.enter_context(sb("m_b1", [128, 8 * 32], F32))
            p.dma("sync", b1c[:], cx.dr["L0_ffn_b1c"], w=[("b1c",)])
        for e_i in range(EALL):
            gated = e_i < E
            p.dma("gpsimd", w1.rearrange("p (i n) -> p i n", n=H),
                  wsrc(1, e_i).rearrange("(i p) n -> p i n", p=128),
                  w=[("w1", i) for i in range(8)])
            if kind == "l2":
                p.dma("gpsimd", w3.rearrange("p (i n) -> p i n", n=H),
                      wsrc(3, e_i).rearrange("(i p) n -> p i n", p=128),
                      w=[("w3", i) for i in range(8)])
            p.dma("gpsimd", w2.rearrange("p (i n) -> p i n", n=1024),
                  wsrc(2, e_i).rearrange("(i p) n -> p i n", p=128),
                  w=[("w2", i) for i in range(HT)])
            for sc in range(NSC):
                lsl = 0
                lno = 0
                p.dma("sync",
                      lnt.rearrange("p (j t) -> p j t", t=NT),
                      cx.dr["s_ln"].rearrange("(j p) t -> p j t", p=128)[
                          :, :, sc * NT:(sc + 1) * NT],
                      r=[("lndram", sc, j) for j in range(8)],
                      w=[("lnbuf0", j) for j in range(8)])
                for hh in range(HT):
                    pk = cx.PSK[hh % 2]
                    ps = cx.ps[hh % 2]
                    for k in range(8):
                        p.do("tensor", lambda e, hh=hh, k=k, ps=ps, lno=lno:
                             e.matmul(
                                 ps[:, 0:NT],
                                 lhsT=w1[:, k * H + hh * 128:
                                         k * H + (hh + 1) * 128],
                                 rhs=lnt[:, lno + k * NT:lno + (k + 1) * NT],
                                 start=(k == 0), stop=(k == 7)),
                             r=[("w1", k), ("lnbuf" + str(lsl), k)],
                             w=[pk])
                    if kind == "l2":
                        pk3 = cx.PSK[2 + hh % 2]
                        ps3 = cx.ps[2 + hh % 2]
                        for k in range(8):
                            p.do("tensor", lambda e, hh=hh, k=k, ps3=ps3,
                                 lno=lno: e.matmul(
                                     ps3[:, 0:NT],
                                     lhsT=w3[:, k * H + hh * 128:
                                             k * H + (hh + 1) * 128],
                                     rhs=lnt[:, lno + k * NT:
                                             lno + (k + 1) * NT],
                                     start=(k == 0), stop=(k == 7)),
                                 r=[("w3", k),
                                    ("lnbuf" + str(lsl), k)], w=[pk3])
                        p.do("scalar", lambda e, hh=hh, ps=ps: e.activation(
                            out=hbf[:, hh * NT:(hh + 1) * NT], in_=ps[:, 0:NT],
                            func=AF.Silu), r=[pk], w=[("hs", hh)])
                        p.do("vector", lambda e, hh=hh, ps3=ps3: e.tensor_mul(
                            hbf[:, hh * NT:(hh + 1) * NT],
                            hbf[:, hh * NT:(hh + 1) * NT], ps3[:, 0:NT]),
                            r=[("hs", hh), pk3], w=[("h", hh)])
                    elif kind == "l0":
                        p.do("scalar", lambda e, hh=hh, ps=ps, e_i=e_i:
                             e.activation(
                                 out=hbf[:, hh * NT:(hh + 1) * NT],
                                 in_=ps[:, 0:NT], func=AF.Gelu,
                                 bias=b1c[:, e_i * 32 + hh:e_i * 32 + hh + 1]),
                             r=[pk, ("b1c",)], w=[("h", hh)])
                    else:
                        p.do("scalar", lambda e, hh=hh, ps=ps: e.activation(
                            out=hbf[:, hh * NT:(hh + 1) * NT], in_=ps[:, 0:NT],
                            func=AF.Gelu), r=[pk], w=[("h", hh)])
                if gated:
                    p.dma("sync", cerow[0:1, 0:NT],
                          cx.dr["s_ceT"][e_i:e_i + 1, sc * NT:(sc + 1) * NT],
                          r=[("ceTd", sc)], w=[("bcm",)])
                    p.do("tensor", lambda e: e.matmul(
                        cx.ps[6][:, 0:NT], lhsT=cx.ones_row_f[:],
                        rhs=cerow[0:1, 0:NT], start=True, stop=True),
                        r=[("bcm",)], w=[cx.PSK[6]])
                    p.do("vector", lambda e: e.tensor_copy(
                        out=bc[:, 0:NT], in_=cx.ps[6][:, 0:NT]),
                        r=[cx.PSK[6]], w=[("bc", "b")])
                last = (e_i == EALL - 1)
                first = (e_i == 0 and kind != "l0")
                if not first:
                    p.dma("sync", yt.rearrange("p (j t) -> p j t", t=NT),
                          cx.dr["s_y"].rearrange("(j p) t -> p j t", p=128)[
                              :, :, sc * NT:(sc + 1) * NT],
                          r=[("ydram", dd, sc) for dd in range(8)],
                          w=[("yt", dd) for dd in range(8)])
                if last:
                    p.dma("sync", xt2.rearrange("p (j t) -> p j t", t=NT),
                          x_in.rearrange("(j p) t -> p j t", p=128)[
                              :, :, sc * NT:(sc + 1) * NT],
                          r=[(xk, "dram")],
                          w=[("xt2", dd) for dd in range(8)])
                for dd in range(8):
                    pk = cx.PSK[4 + dd % 2]
                    ps = cx.ps[4 + dd % 2]
                    ysl = slice(dd * NT, (dd + 1) * NT)
                    for k in range(HT):
                        p.do("tensor", lambda e, dd=dd, k=k, ps=ps: e.matmul(
                            ps[:, 0:NT], lhsT=w2[:, k * 1024 + dd * 128:
                                                 k * 1024 + (dd + 1) * 128],
                            rhs=hbf[:, k * NT:(k + 1) * NT],
                            start=(k == 0), stop=(k == HT - 1)),
                            r=[("w2", k), ("h", k)], w=[pk])
                    if gated:
                        p.do("vector", lambda e, ps=ps: e.tensor_mul(
                            bc[:, NT:2 * NT], ps[:, 0:NT], bc[:, 0:NT]),
                            r=[pk, ("bc", "b")], w=[("bcm",)])
                        src_k = ("bcm",)

                        def SRC():
                            return bc[:, NT:2 * NT]
                    else:
                        src_k = pk

                        def SRC(ps=ps):
                            return ps[:, 0:NT]
                    if first:
                        p.do("vector", lambda e, ysl=ysl, S=SRC: e.tensor_copy(
                            out=yt[:, ysl], in_=S()),
                            r=[src_k], w=[("yt", dd)])
                    else:
                        p.do("vector", lambda e, ysl=ysl, S=SRC: e.tensor_add(
                            yt[:, ysl], yt[:, ysl], S()),
                            r=[src_k, ("yt", dd)], w=[("yt", dd)])
                    if last:
                        if kind == "l3":
                            lno2 = 0
                            p.do("vector", lambda e, ysl=ysl, dd=dd, lno2=lno2:
                                 e.tensor_add(
                                     yt[:, ysl], yt[:, ysl],
                                     lnt[:, lno2 + dd * NT:
                                         lno2 + (dd + 1) * NT]),
                                 r=[("yt", dd), ("lnbuf0", dd)],
                                 w=[("yt", dd)])
                        p.do("vector", lambda e, ysl=ysl: (
                            e.scalar_tensor_tensor(
                                out=yt[:, ysl], in0=yt[:, ysl], scalar=0.5,
                                in1=xt2[:, ysl], op0=ALU.mult, op1=ALU.add)),
                            r=[("yt", dd), ("xt2", dd)],
                            w=[("yt", dd)])
                if not last:
                    p.dma("scalar",
                          cx.dr["s_y"].rearrange("(j p) t -> p j t", p=128)[
                              :, :, sc * NT:(sc + 1) * NT],
                          yt.rearrange("p (j t) -> p j t", t=NT),
                          r=[("yt", dd) for dd in range(8)],
                          w=[("ydram", dd, sc) for dd in range(8)])
                else:
                    p.dma("scalar",
                          x_out.rearrange("(j p) t -> p j t", p=128)[
                              :, :, sc * NT:(sc + 1) * NT],
                          yt.rearrange("p (j t) -> p j t", t=NT),
                          r=[("yt", dd) for dd in range(8)],
                          w=[(xok, "dram")])
        p.barrier()




def emit_rms_chunk(p, cx, xck, xc, out_bf, outk, NT, w_col, eps):
    ps_s2, ps_b1 = cx.ps[0], cx.ps[1]
    K = cx.PSK
    x2bf = cx.ln_x2bf
    for j in range(8):
        sl = slice(j * NT, (j + 1) * NT)
        ds = slice((j % 2) * NT, (j % 2 + 1) * NT)
        p.do("scalar", lambda e, sl=sl, ds=ds: e.activation(
            out=x2bf[:, ds], in_=xc[:, sl], func=AF.Square),
            r=[(xck, j)], w=[("lnx2", j % 2)])
        p.do("tensor", lambda e, ds=ds, j=j: e.matmul(
            ps_s2[0:1, 0:NT], lhsT=cx.ones_bf[:], rhs=x2bf[:, ds],
            start=(j == 0), stop=(j == 7)),
            r=[("lnx2", j % 2)], w=[K[0]])
    m, t1 = cx.ln_m, cx.ln_t1
    p.do("vector", lambda e, m=m: e.tensor_scalar(
        out=m[0:1, 0:NT], in0=ps_s2[0:1, 0:NT], scalar1=1.0 / D, scalar2=None,
        op0=ALU.mult), r=[K[0]], w=["ln_m"])
    p.do("scalar", lambda e, m=m, t1=t1: e.activation(
        out=t1[0:1, 0:NT], in_=m[0:1, 0:NT], func=AF.Sqrt,
        bias=cx.cval(eps, 1)), r=["ln_m"], w=["ln_t1"])
    p.do("vector", lambda e, m=m, t1=t1: e.reciprocal(
        out=m[0:1, 0:NT], in_=t1[0:1, 0:NT]), r=["ln_t1"], w=["ln_m"])
    p.do("tensor", lambda e, m=m: e.matmul(
        ps_b1[:, 0:NT], lhsT=cx.ones_row_f[:], rhs=m[0:1, 0:NT],
        start=True, stop=True), r=["ln_m"], w=[K[1]])
    tmp = cx.ln_tmp
    for j in range(8):
        sl = slice(j * NT, (j + 1) * NT)
        p.do("vector", lambda e, sl=sl, tmp=tmp: e.tensor_mul(
            tmp[:, 0:NT], xc[:, sl], ps_b1[:, 0:NT]),
            r=[(xck, j), K[1]], w=["ln_tmp"])
        p.do("vector", lambda e, sl=sl, j=j, tmp=tmp: e.tensor_scalar(
            out=out_bf[:, sl], in0=tmp[:, 0:NT], scalar1=w_col[:, j:j + 1],
            scalar2=None, op0=ALU.mult),
            r=["ln_tmp", ("c", "wcol")], w=[(outk, j)])


def _proj_fm(p, cx, w_sb, wk, src_sb, srck, dst, dstk, sc, NT, MT, kn=8,
             evac="bf"):
    for mi in range(MT):
        pk = cx.PSK[2 + mi % 2]
        ps = cx.ps[2 + mi % 2]
        for k in range(kn):
            p.do("tensor", lambda e, mi=mi, k=k, ps=ps: e.matmul(
                ps[:, 0:NT],
                lhsT=w_sb[:, (k * MT + mi) * 128:(k * MT + mi + 1) * 128],
                rhs=src_sb[:, k * NT:(k + 1) * NT],
                start=(k == 0), stop=(k == kn - 1)),
                r=[(wk, k), (srck, k)], w=[pk])
        evac(ps, pk, mi)


def emit_l2attn(p, cx, x_in, xk, x_out, xok):
    nc = cx.nc
    NT, TT, T = cx.NT, cx.TT, cx.T
    NSC = TT // NT
    NQC = T // NT
    NKT = T // 128
    sb = cx.sbt
    with contextlib.ExitStack() as stk:
        xc = stk.enter_context(sb("a2xc", [128, 8 * NT], F32))
        lnb = stk.enter_context(sb("a2ln", [128, 8 * NT], BF16))
        wq = stk.enter_context(sb("a2wq", [128, 8 * 8 * 128], BF16))
        wk_ = stk.enter_context(sb("a2wk", [128, 8 * 8 * 128], BF16))
        wv = stk.enter_context(sb("a2wv", [128, 8 * 1024], BF16))
        qf = stk.enter_context(sb("a2qf", [128, 2 * NT], BF16))
        qst = stk.enter_context(sb("a2qst", [128, 8 * NT], BF16))
        kst = stk.enter_context(sb("a2kst", [128, 8 * NT], BF16))
        vst = stk.enter_context(sb("a2vst", [128, 2 * NT], BF16))
        rc = stk.enter_context(sb("a2rc", [128, NT], F32))
        rs = stk.enter_context(sb("a2rs", [128, NT], F32))
        swp = stk.enter_context(sb("a2swp", [128, 128], BF16))
        t1b = stk.enter_context(sb("a2t1", [128, NT], F32))
        t2b = stk.enter_context(sb("a2t2", [128, NT], F32))
        _alloc_ln_scratch(cx, stk, NT)
        p.dma("gpsimd", swp[:], cx.dr["c_swp"], w=[("swp",)])
        for k in range(8):
            p.dma("gpsimd", wq[:, k * 1024:(k + 1) * 1024],
                  cx.dr["L2_attn_w_qkv"][k * 128:(k + 1) * 128, 0:1024],
                  w=[("wq", k)])
            p.dma("gpsimd", wk_[:, k * 1024:(k + 1) * 1024],
                  cx.dr["L2_attn_w_qkv"][k * 128:(k + 1) * 128, 1024:2048],
                  w=[("wk", k)])
            p.dma("gpsimd", wv[:, k * 1024:(k + 1) * 1024],
                  cx.dr["L2_attn_w_qkv"][k * 128:(k + 1) * 128, 2048:3072],
                  w=[("wv",)])
        for sc in range(NSC):
            load_x_chunk(p, cx, x_in, xk, sc, xc, "xc", NT)
            emit_ln_chunk(p, cx, "xc", xc, lnb, "ln", NT)
            for j in range(8):
                p.dma("scalar", cx.dr["s_ln"][j * 128:(j + 1) * 128,
                                              sc * NT:(sc + 1) * NT],
                      lnb[:, j * NT:(j + 1) * NT],
                      r=[("ln", j)], w=[("lndram", sc, j)])

            def qk_evac(which, wbuf):
                scl = 0.125 if which == "q" else 1.0
                stg = qst if which == "q" else kst
                stk_ = "qst" if which == "q" else "kst"

                def ev(ps, pk, mi):
                    rope = (mi % 2 == 1)
                    osl = slice(mi * NT, (mi + 1) * NT)
                    if not rope:
                        p.do("vector", lambda e, ps=ps, osl=osl:
                             e.tensor_scalar(
                                 out=stg[:, osl], in0=ps[:, 0:NT],
                                 scalar1=scl, scalar2=None, op0=ALU.mult),
                             r=[pk], w=[(stk_, mi)])
                    else:
                        p.do("vector", lambda e, ps=ps: e.tensor_copy(
                            out=qf[:, 0:NT], in_=ps[:, 0:NT]),
                            r=[pk], w=[("qf", 0)])
                        p.do("tensor", lambda e: e.matmul(
                            cx.ps[6][:, 0:NT], lhsT=swp[:],
                            rhs=qf[:, 0:NT], start=True, stop=True),
                            r=[("swp",), ("qf", 0)], w=[cx.PSK[6]])
                        p.do("vector", lambda e, sc=sc: e.tensor_mul(
                            t1b[:, 0:NT], qf[:, 0:NT],
                            rc[:, 0:NT]), r=[("qf", 0), ("rc",)],
                            w=[("t1b",)])
                        p.do("vector", lambda e, sc=sc: e.tensor_mul(
                            t2b[:, 0:NT], cx.ps[6][:, 0:NT], rs[:, 0:NT]),
                            r=[cx.PSK[6], ("rs",)], w=[("t2b",)])
                        p.do("vector", lambda e, osl=osl:
                             e.scalar_tensor_tensor(
                                 out=stg[:, osl], in0=t1b[:, 0:NT],
                                 scalar=1.0, in1=t2b[:, 0:NT],
                                 op0=ALU.mult, op1=ALU.add),
                             r=[("t1b",), ("t2b",)], w=[(stk_, mi)])
                        if scl != 1.0:
                            p.do("vector", lambda e, osl=osl: e.tensor_scalar(
                                out=stg[:, osl], in0=stg[:, osl],
                                scalar1=scl, scalar2=None, op0=ALU.mult),
                                r=[(stk_, mi)], w=[(stk_, mi)])
                    if mi == 7:
                        dstd = cx.dr["s_p1" if which == "q" else "s_p2"]
                        p.dma("scalar",
                              dstd.rearrange("(j p) t -> p j t", p=128)[
                                  :, 0:8, sc * NT:(sc + 1) * NT],
                              stg.rearrange("p (j t) -> p j t", t=NT),
                              r=[(stk_, q) for q in range(8)],
                              w=[(which + "d", mi, sc)])
                return ev
            p.dma("sync", rc[:, 0:NT],
                  cx.dr["c_ropec"][:, sc * NT:(sc + 1) * NT], w=[("rc",)])
            p.dma("sync", rs[:, 0:NT],
                  cx.dr["c_ropes"][:, sc * NT:(sc + 1) * NT], w=[("rs",)])
            _proj_fm(p, cx, wq, "wq", lnb, "ln", None, None, sc, NT, 8,
                     evac=qk_evac("q", wq))
            _proj_fm(p, cx, wk_, "wk", lnb, "ln", None, None, sc, NT, 8,
                     evac=qk_evac("k", wk_))
            for tt in range(NT // 128):
                for nn in range(2):
                    pk = cx.PSK[4 + nn]
                    ps = cx.ps[4 + nn]
                    for k in range(8):
                        p.do("tensor", lambda e, tt=tt, nn=nn, k=k, ps=ps:
                             e.matmul(
                                 ps[:, 0:512],
                                 lhsT=lnb[:, k * NT + tt * 128:
                                          k * NT + (tt + 1) * 128],
                                 rhs=wv[:, k * 1024 + nn * 512:
                                        k * 1024 + (nn + 1) * 512],
                                 start=(k == 0), stop=(k == 7)),
                             r=[("ln", k), ("wv",)], w=[pk])
                    p.do("vector", lambda e, ps=ps, nn=nn: e.tensor_copy(
                        out=vst[:, nn * 512:(nn + 1) * 512], in_=ps[:, 0:512]),
                        r=[pk], w=[("vst", nn)])
                p.dma("scalar",
                      cx.dr["s_vtm"][sc * NT + tt * 128:
                                     sc * NT + (tt + 1) * 128, 0:1024],
                      vst[:, 0:1024],
                      r=[("vst", nn) for nn in range(2)],
                      w=[("vd", sc, tt)])
    p.barrier()
    with contextlib.ExitStack() as stk:
        qT = stk.enter_context(sb("a2qT", [64, T], BF16))
        kT = stk.enter_context(sb("a2kT", [64, T], BF16))
        v65 = stk.enter_context(sb("a2v65", [128, NKT * 65], BF16))
        q2 = stk.enter_context(sb("a2q2", [64, T], BF16))
        nrm = stk.enter_context(sb("a2nrm", [1, 2 * T + 8], F32))
        akc = stk.enter_context(sb("a2akc", [128, 16 * 16], F32))
        kposc = stk.enter_context(sb("a2kpos", [128, 16], F32))
        lxi = stk.enter_context(sb("a2lxi", [1, 16], F32))
        ebt = stk.enter_context(sb("a2ebt", [1, 16], F32))
        arow = stk.enter_context(sb("a2arow", [1, 16], F32))
        rsum = stk.enter_context(sb("a2rsum", [1, 512], F32))
        acb = stk.enter_context(sb("a2acb", [128, 16], F32))
        mbs = stk.enter_context(sb("a2mbs", [128, NT], F32))
        dmt = stk.enter_context(sb("a2dmt", [128, 4 * 512], F32))
        pT = stk.enter_context(sb("a2pT", [128, NKT * 512], BF16))
        tob = stk.enter_context(sb("a2to", [64, 512], F32))
        ofm = stk.enter_context(sb("a2ofm", [64, T], BF16))
        p.dma("sync", lxi[:], cx.dr["L2_attn_log_xi2"], w=[("lxi",)])
        p.dma("sync", ebt[:], cx.dr["c_ebits"], w=[("ebits",)])
        p.dma("sync", kposc[:], cx.dr["c_kposc"], w=[("kposc",)])
        p.dma("sync", dmt[:], cx.dr["c_dm2t"], w=[("dmt",)])
        p.do("scalar", lambda e: e.activation(
            out=arow[0:1, 0:16], in_=lxi[0:1, 0:16], func=AF.Exp, scale=-1.0),
            r=[("lxi",)], w=[("arow", 0)])
        p.do("vector", lambda e: e.tensor_mul(
            arow[0:1, 0:16], arow[0:1, 0:16], ebt[0:1, 0:16]),
            r=[("arow", 0), ("ebits",)], w=[("arow", 0)])
        p.do("tensor", lambda e: e.matmul(
            cx.ps[6][:, 0:16], lhsT=cx.ones_row_f[:], rhs=arow[0:1, 0:16],
            start=True, stop=True), r=[("arow", 0)], w=[cx.PSK[6]])
        p.do("vector", lambda e: e.tensor_copy(out=acb[:], in_=cx.ps[6][:, 0:16]),
             r=[cx.PSK[6]], w=[("acb",)])
        for h in range(16):
            p.do("vector", lambda e, h=h: e.tensor_scalar(
                out=akc[:, h * 16:(h + 1) * 16], in0=kposc[:],
                scalar1=acb[:, h:h + 1], scalar2=None, op0=ALU.mult),
                r=[("kposc",), ("acb",)], w=[("akc",)])
        for b in range(2):
            for h in range(16):
                p.dma("sync", qT[:], cx.dr["s_p1"][h * 64:(h + 1) * 64,
                                                   b * T:(b + 1) * T],
                      r=[("qd", h // 2, "x")], w=[("qT",)])
                p.dma("sync", kT[:], cx.dr["s_p2"][h * 64:(h + 1) * 64,
                                                   b * T:(b + 1) * T],
                      r=[("kd", h // 2, "x")], w=[("kT",)])
                p.do("vector", lambda e: e.memset(v65[:], 1.0), w=[("v65",)])
                p.dma("sync",
                      v65.rearrange("p (c v) -> p c v", v=65)[:, :, 0:64],
                      cx.dr["s_vtm"].rearrange("(c p) v -> p c v", p=128)[
                          :, b * NKT:(b + 1) * NKT, h * 64:(h + 1) * 64],
                      r=[("vd", "x")], w=[("v65",)])
                p.do("scalar", lambda e: e.activation(
                    out=q2[:], in_=qT[:], func=AF.Square),
                    r=[("qT",)], w=[("q2",)])
                for c in range(T // 512):
                    p.do("tensor", lambda e, c=c: e.matmul(
                        cx.ps[6][0:1, 0:512],
                        lhsT=cx.ones_bf[0:64, :],
                        rhs=q2[:, c * 512:(c + 1) * 512],
                        start=True, stop=True), r=[("q2",)], w=[cx.PSK[6]])
                    p.do("scalar", lambda e, c=c: e.activation(
                        out=nrm[0:1, c * 512:(c + 1) * 512],
                        in_=cx.ps[6][0:1, 0:512], func=AF.Sqrt),
                        r=[cx.PSK[6]], w=[("nq",)])
                p.do("scalar", lambda e: e.activation(
                    out=q2[:], in_=kT[:], func=AF.Square),
                    r=[("kT",)], w=[("q2",)])
                for c in range(T // 512):
                    p.do("tensor", lambda e, c=c: e.matmul(
                        cx.ps[6][0:1, 0:512],
                        lhsT=cx.ones_bf[0:64, :],
                        rhs=q2[:, c * 512:(c + 1) * 512],
                        start=True, stop=True), r=[("q2",)], w=[cx.PSK[6]])
                    p.do("vector", lambda e, c=c: e.tensor_reduce(
                        out=nrm[0:1, 2 * T + 2 + c:2 * T + 3 + c],
                        in_=cx.ps[6][0:1, 0:512],
                        axis=AX.X, op=ALU.max), r=[cx.PSK[6]], w=[("mk2",)])
                p.do("vector", lambda e: e.tensor_reduce(
                    out=nrm[0:1, 2 * T:2 * T + 1],
                    in_=nrm[0:1, 2 * T + 2:2 * T + 2 + T // 512],
                    axis=AX.X, op=ALU.max), r=[("mk2",)], w=[("mk2m",)])
                p.do("scalar", lambda e: e.activation(
                    out=nrm[0:1, 2 * T + 1:2 * T + 2],
                    in_=nrm[0:1, 2 * T:2 * T + 1], func=AF.Sqrt),
                    r=[("mk2m",)], w=[("mk",)])
                p.do("vector", lambda e: e.tensor_scalar(
                    out=nrm[0:1, T:2 * T], in0=nrm[0:1, 0:T],
                    scalar1=nrm[0:1, 2 * T + 1:2 * T + 2], scalar2=None,
                    op0=ALU.mult),
                    r=[("nq",), ("mk",)], w=[("mrow",)])
                for qc in range(T // 512):
                    nkt = 4 * qc + 4
                    p.do("tensor", lambda e, qc=qc: e.matmul(
                        cx.ps[5][:, 0:512], lhsT=cx.ones_row_f[:],
                        rhs=nrm[0:1, T + qc * 512:T + (qc + 1) * 512],
                        start=True, stop=True), r=[("mrow",)], w=[cx.PSK[5]])
                    p.do("vector", lambda e: e.tensor_copy(
                        out=mbs[:, 0:512], in_=cx.ps[5][:, 0:512]),
                        r=[cx.PSK[5]], w=[("mbs",)])
                    for kt in range(nkt):
                        pk = cx.PSK[kt % 2]
                        ps = cx.ps[kt % 2]
                        p.do("tensor", lambda e, kt=kt, qc=qc, ps=ps: e.matmul(
                            ps[:, 0:512],
                            lhsT=kT[:, kt * 128:(kt + 1) * 128],
                            rhs=qT[:, qc * 512:(qc + 1) * 512],
                            start=True, stop=True),
                            r=[("kT",), ("qT",)], w=[pk])
                        p.do("vector", lambda e, kt=kt, h=h, ps=ps: (
                            e.scalar_tensor_tensor(
                                out=pT[:, kt * 512:(kt + 1) * 512],
                                in0=ps[:, 0:512],
                                scalar=akc[:, h * 16 + kt:h * 16 + kt + 1],
                                in1=mbs[:, 0:512],
                                op0=ALU.subtract, op1=ALU.subtract)),
                            r=[pk, ("akc",), ("mbs",)], w=[("pTf", kt)])
                        if kt >= 4 * qc:
                            v = kt - 4 * qc
                            p.do("vector", lambda e, kt=kt, v=v: e.tensor_add(
                                pT[:, kt * 512:(kt + 1) * 512],
                                pT[:, kt * 512:(kt + 1) * 512],
                                dmt[:, v * 512:(v + 1) * 512]),
                                r=[("pTf", kt), ("dmt",)], w=[("pTf", kt)])
                        p.do("scalar", lambda e, kt=kt: e.activation(
                            out=pT[:, kt * 512:(kt + 1) * 512],
                            in_=pT[:, kt * 512:(kt + 1) * 512], func=AF.Exp),
                            r=[("pTf", kt)], w=[("pTf", kt)])
                    for kt in range(nkt):
                        p.do("tensor", lambda e, kt=kt: e.matmul(
                            cx.ps[4][0:65, 0:512],
                            lhsT=v65[:, kt * 65:(kt + 1) * 65],
                            rhs=pT[:, kt * 512:(kt + 1) * 512],
                            start=(kt == 0), stop=(kt == nkt - 1)),
                            r=[("v65",), ("pTf", kt)], w=[cx.PSK[4]])
                    p.do("vector", lambda e: e.reciprocal(
                        out=rsum[0:1, 0:512], in_=cx.ps[4][64:65, 0:512]),
                        r=[cx.PSK[4]], w=[("rsum",)])
                    p.do("vector", lambda e: e.tensor_copy(
                        out=tob[:, 0:512], in_=cx.ps[4][0:64, 0:512]),
                        r=[cx.PSK[4]], w=[("tob",)])
                    p.do("tensor", lambda e: e.matmul(
                        cx.ps[5][0:64, 0:512], lhsT=cx.ones_row_f[0:1, 0:64],
                        rhs=rsum[0:1, 0:512], start=True, stop=True),
                        r=[("rsum",)], w=[cx.PSK[5]])
                    p.do("vector", lambda e, qc=qc: e.tensor_mul(
                        ofm[:, qc * 512:(qc + 1) * 512], tob[:, 0:512],
                        cx.ps[5][0:64, 0:512]),
                        r=[("tob",), cx.PSK[5]], w=[("ofm", qc)])
                if True:
                    p.dma("scalar", cx.dr["s_ofm"][h * 64:(h + 1) * 64,
                                                   b * T:(b + 1) * T],
                          ofm[:, 0:T],
                          r=[("ofm", q) for q in range(T // 512)],
                          w=[("od", "x")])
    p.barrier()
    with contextlib.ExitStack() as stk:
        wo = stk.enter_context(sb("a2wo", [128, 8 * 1024], BF16))
        osb = stk.enter_context(sb("a2osb", [128, 8 * NT], BF16))
        xt = stk.enter_context(sb("a2xt", [128, 2 * NT], F32))
        yt = stk.enter_context(sb("a2yt", [128, 2 * NT], F32))
        for k in range(8):
            p.dma("gpsimd", wo[:, k * 1024:(k + 1) * 1024],
                  cx.dr["L2_attn_w_o"][k * 128:(k + 1) * 128, :],
                  w=[("wo", k)])
        for sc in range(NSC):
            for j in range(8):
                p.dma("sync", osb[:, j * NT:(j + 1) * NT],
                      cx.dr["s_ofm"][j * 128:(j + 1) * 128,
                                     sc * NT:(sc + 1) * NT],
                      r=[("od", "x")], w=[("osb", j)])
            for dd in range(8):
                pk = cx.PSK[2 + dd % 2]
                ps = cx.ps[2 + dd % 2]
                ysl = slice((dd % 2) * NT, (dd % 2 + 1) * NT)
                for k in range(8):
                    p.do("tensor", lambda e, dd=dd, k=k, ps=ps: e.matmul(
                        ps[:, 0:NT], lhsT=wo[:, k * 1024 + dd * 128:
                                             k * 1024 + (dd + 1) * 128],
                        rhs=osb[:, k * NT:(k + 1) * NT],
                        start=(k == 0), stop=(k == 7)),
                        r=[("wo", k), ("osb", k)], w=[pk])
                p.dma("sync", xt[:, ysl],
                      x_in[dd * 128:(dd + 1) * 128, sc * NT:(sc + 1) * NT],
                      r=[(xk, "dram")], w=[("xt", dd % 2)])
                p.do("vector", lambda e, ps=ps, ysl=ysl: e.tensor_add(
                    yt[:, ysl], ps[:, 0:NT], xt[:, ysl]),
                    r=[pk, ("xt", dd % 2)], w=[("yt", dd % 2)])
                p.dma("scalar", x_out[dd * 128:(dd + 1) * 128,
                                      sc * NT:(sc + 1) * NT],
                      yt[:, ysl], r=[("yt", dd % 2)], w=[(xok, "dram")])
        p.barrier()


def emit_l1attn(p, cx, x_in, xk, x_out, xok):
    nc = cx.nc
    NT, TT, T = cx.NT, cx.TT, cx.T
    NSC = TT // NT
    sb = cx.sbt
    N = 16
    with contextlib.ExitStack() as stk:
        xc = stk.enter_context(sb("m1xc", [128, 8 * NT], F32))
        lnb = stk.enter_context(sb("m1ln", [128, 8 * NT], BF16))
        lnf = stk.enter_context(sb("m1lnf", [128, NT], BF16))
        win = stk.enter_context(sb("m1win", [128, 8 * 16 * 128], BF16))
        wx = stk.enter_context(sb("m1wx", [128, 8 * 9 * 128], BF16))
        wdt = stk.enter_context(sb("m1wdt", [128, 8 * 8 * 128], BF16))
        bdt = stk.enter_context(sb("m1bdt", [128, 8], F32))
        xsb = stk.enter_context(sb("m1xs", [128, 8 * NT], BF16))
        zst = stk.enter_context(sb("m1zst", [128, 8 * NT], BF16))
        dst_ = stk.enter_context(sb("m1dst", [128, 8 * NT], BF16))
        drb = stk.enter_context(sb("m1dr", [128, 8 * NT], BF16))
        tE = stk.enter_context(sb("m1te", [128, 2 * NT], F32))
        bcs = stk.enter_context(sb("m1bc", [32, TT], F32))
        _alloc_ln_scratch(cx, stk, NT)
        p.dma("gpsimd", win.rearrange("p (k n) -> p k n", n=2048),
              cx.dr["L1_attn_W_in"].rearrange("(k p) n -> p k n", p=128),
              w=[("win", k) for k in range(8)])
        p.dma("gpsimd", wx.rearrange("p (k n) -> p k n", n=1152),
              cx.dr["L1_attn_W_xp"].rearrange("(k p) n -> p k n", p=128),
              w=[("wx", k) for k in range(8)])
        p.dma("gpsimd", wdt.rearrange("p (k n) -> p k n", n=1024),
              cx.dr["L1_attn_W_dt"].rearrange("(k p) n -> p k n", p=128),
              w=[("wdt", k) for k in range(8)])
        p.dma("sync", bdt[:], cx.dr["L1_attn_b_dtc"], w=[("bdt",)])
        for sc in range(NSC):
            load_x_chunk(p, cx, x_in, xk, sc, xc, "xc", NT)
            emit_ln_chunk(p, cx, "xc", xc, lnb, "ln", NT)
            for j in range(8):
                p.dma("scalar", cx.dr["s_ln"][j * 128:(j + 1) * 128,
                                              sc * NT:(sc + 1) * NT],
                      lnb[:, j * NT:(j + 1) * NT],
                      r=[("ln", j)], w=[("lndram", sc, j)])
            for mi in range(16):
                pk = cx.PSK[4 + mi % 2]
                ps = cx.ps[4 + mi % 2]
                for k in range(8):
                    p.do("tensor", lambda e, mi=mi, k=k, ps=ps: e.matmul(
                        ps[:, 0:NT],
                        lhsT=win[:, k * 2048 + mi * 128:
                                 k * 2048 + (mi + 1) * 128],
                        rhs=lnb[:, k * NT:(k + 1) * NT],
                        start=(k == 0), stop=(k == 7)),
                        r=[("win", k), ("ln", k)], w=[pk])
                if mi < 8:
                    p.do("vector", lambda e, mi=mi, ps=ps: e.tensor_copy(
                        out=xsb[:, mi * NT:(mi + 1) * NT], in_=ps[:, 0:NT]),
                        r=[pk], w=[("xs", mi)])
                    if mi == 7:
                        p.dma("scalar",
                              cx.dr["s_p1"].rearrange(
                                  "(j p) t -> p j t", p=128)[
                                  :, 0:8, sc * NT:(sc + 1) * NT],
                              xsb.rearrange("p (j t) -> p j t", t=NT),
                              r=[("xs", q) for q in range(8)],
                              w=[("xsd", sc, mi)])
                else:
                    p.do("scalar", lambda e, mi=mi, ps=ps: e.activation(
                        out=zst[:, (mi - 8) * NT:(mi - 7) * NT],
                        in_=ps[:, 0:NT], func=AF.Silu),
                        r=[pk], w=[("zst", mi - 8)])
                    if mi == 15:
                        p.dma("scalar",
                              cx.dr["s_p2"].rearrange(
                                  "(j p) t -> p j t", p=128)[
                                  :, 0:8, sc * NT:(sc + 1) * NT],
                              zst.rearrange("p (j t) -> p j t", t=NT),
                              r=[("zst", q) for q in range(8)],
                              w=[("zd", sc, mi)])
            for mi in range(9):
                pk = cx.PSK[4 + mi % 2]
                ps = cx.ps[4 + mi % 2]
                for k in range(8):
                    p.do("tensor", lambda e, mi=mi, k=k, ps=ps: e.matmul(
                        ps[:, 0:NT],
                        lhsT=wx[:, k * 1152 + mi * 128:
                                k * 1152 + (mi + 1) * 128],
                        rhs=xsb[:, k * NT:(k + 1) * NT],
                        start=(k == 0), stop=(k == 7)),
                        r=[("wx", k), ("xs", k)], w=[pk])
                if mi < 8:
                    p.do("vector", lambda e, mi=mi, ps=ps: e.tensor_copy(
                        out=drb[:, mi * NT:(mi + 1) * NT], in_=ps[:, 0:NT]),
                        r=[pk], w=[("dr", mi)])
                else:
                    p.do("vector", lambda e, sc=sc, ps=ps: e.tensor_copy(
                        out=bcs[0:32, sc * NT:(sc + 1) * NT],
                        in_=ps[0:32, 0:NT]), r=[pk], w=[("bcs",)])
            for mi in range(8):
                pk = cx.PSK[4 + mi % 2]
                ps = cx.ps[4 + mi % 2]
                for k in range(8):
                    p.do("tensor", lambda e, mi=mi, k=k, ps=ps: e.matmul(
                        ps[:, 0:NT],
                        lhsT=wdt[:, k * 1024 + mi * 128:
                                 k * 1024 + (mi + 1) * 128],
                        rhs=drb[:, k * NT:(k + 1) * NT],
                        start=(k == 0), stop=(k == 7)),
                        r=[("wdt", k), ("dr", k)], w=[pk])
                p.do("vector", lambda e, mi=mi, ps=ps: e.tensor_scalar(
                    out=tE[:, 0:NT], in0=ps[:, 0:NT],
                    scalar1=bdt[:, mi:mi + 1], scalar2=None, op0=ALU.add),
                    r=[pk, ("bdt",)], w=[("spx",)])
                p.do("scalar", lambda e: e.activation(
                    out=tE[:, NT:2 * NT], in_=tE[:, 0:NT], func=AF.Abs),
                    r=[("spx",)], w=[("spa",)])
                p.do("scalar", lambda e: e.activation(
                    out=tE[:, NT:2 * NT], in_=tE[:, NT:2 * NT], func=AF.Exp,
                    scale=-1.0), r=[("spa",)], w=[("spa",)])
                p.do("scalar", lambda e: e.activation(
                    out=tE[:, NT:2 * NT], in_=tE[:, NT:2 * NT], func=AF.Ln,
                    bias=cx.cval(1.0)), r=[("spa",)], w=[("spa",)])
                p.do("vector", lambda e: e.tensor_scalar(
                    out=tE[:, 0:NT], in0=tE[:, 0:NT], scalar1=0.0,
                    scalar2=None, op0=ALU.max),
                    r=[("spx",)], w=[("spx",)])
                p.do("vector", lambda e: e.tensor_add(
                    tE[:, 0:NT], tE[:, 0:NT], tE[:, NT:2 * NT]),
                    r=[("spx",), ("spa",)], w=[("spx",)])
                p.do("vector", lambda e, mi=mi: e.tensor_copy(
                    out=dst_[:, mi * NT:(mi + 1) * NT], in_=tE[:, 0:NT]),
                    r=[("spx",)], w=[("dlt", mi)])
                if mi == 7:
                    p.dma("scalar",
                          cx.dr["s_p3"].rearrange("(j p) t -> p j t", p=128)[
                              :, 0:8, sc * NT:(sc + 1) * NT],
                          dst_.rearrange("p (j t) -> p j t", t=NT),
                          r=[("dlt", q) for q in range(8)],
                          w=[("dltd", sc, mi)])
        p.dma("scalar", cx.dr["s_bc"], bcs[0:32, :], r=[("bcs",)],
              w=[("bcd",)])
    p.barrier()
    with contextlib.ExitStack() as stk:
        dlt = stk.enter_context(sb("m1dl", [128, 4 * T], BF16))
        xss = stk.enter_context(sb("m1xss", [128, 4 * T], BF16))
        dxt = stk.enter_context(sb("m1dxt", [128, 4 * T], BF16))
        yal = stk.enter_context(sb("m1yal", [128, 4 * T], F32))
        Acol = stk.enter_context(sb("m1ac", [128, 8 * 16], F32))
        brow = stk.enter_context(sb("m1br", [1, 4 * T], F32))
        crow = stk.enter_context(sb("m1cr", [1, 4 * T], F32))
        Bb = stk.enter_context(sb("m1bb", [128, T], F32))
        Cb = stk.enter_context(sb("m1cb", [128, T], F32))
        dAa = stk.enter_context(sb("m1da", [128, 512], F32))
        dBx = stk.enter_context(sb("m1db", [128, 512], F32))
        hsc = stk.enter_context(sb("m1h", [128, 512 + 1], F32))
        zsl = stk.enter_context(sb("m1z", [128, T], BF16))
        dpc = stk.enter_context(sb("m1dp", [128, 8], F32))
        p.dma("sync", Acol[:], cx.dr["L1_attn_A_logc"], w=[("alog",)])
        p.do("scalar", lambda e: e.activation(
            out=Acol[:], in_=Acol[:], func=AF.Exp), r=[("alog",)],
            w=[("alog",)])
        p.do("vector", lambda e: e.tensor_scalar(
            out=Acol[:], in0=Acol[:], scalar1=-1.0, scalar2=None,
            op0=ALU.mult), r=[("alog",)], w=[("alog",)])
        p.dma("sync", dpc[:], cx.dr["L1_attn_D_paramc"], w=[("dpc",)])
        for b in range(2):
            for hf in range(2):
                p.dma("sync", dlt.rearrange("p (j t) -> p j t", t=T),
                      cx.dr["s_p3"].rearrange("(j p) t -> p j t", p=128)[
                          :, hf * 4:hf * 4 + 4, b * T:(b + 1) * T],
                      r=[("dltd", "x")], w=[("dltS", jj) for jj in range(4)])
                p.dma("sync", xss.rearrange("p (j t) -> p j t", t=T),
                      cx.dr["s_p1"].rearrange("(j p) t -> p j t", p=128)[
                          :, hf * 4:hf * 4 + 4, b * T:(b + 1) * T],
                      r=[("xsd", "x")], w=[("xssS", jj) for jj in range(4)])
                for jj in range(4):
                    p.do("vector", lambda e, jj=jj: e.tensor_mul(
                        dxt[:, jj * T:(jj + 1) * T],
                        dlt[:, jj * T:(jj + 1) * T],
                        xss[:, jj * T:(jj + 1) * T]),
                        r=[("dltS", jj), ("xssS", jj)], w=[("dxt", jj)])
                    p.do("vector", lambda e, jj=jj: e.memset(
                        yal[:, jj * T:(jj + 1) * T], 0.0), w=[("yal", jj)])
                bcg = cx.dr["s_bc"].rearrange("(g n) t -> g n t", n=4)
                for n in range(N):
                    if n % 4 == 0:
                        p.dma("sync",
                              brow.rearrange("p (n t) -> p n t", t=T),
                              bcg[n // 4:n // 4 + 1, :, b * T:(b + 1) * T],
                              r=[("bcd",)], w=[("brow",)])
                        p.dma("sync",
                              crow.rearrange("p (n t) -> p n t", t=T),
                              bcg[4 + n // 4:5 + n // 4, :,
                                  b * T:(b + 1) * T],
                              r=[("bcd",)], w=[("crow",)])
                    for c in range(T // 512):
                        p.do("tensor", lambda e, c=c, n=n: e.matmul(
                            cx.ps[0][:, 0:512], lhsT=cx.ones_row_f[:],
                            rhs=brow[0:1, (n % 4) * T + c * 512:
                                     (n % 4) * T + (c + 1) * 512],
                            start=True, stop=True), r=[("brow",)],
                            w=[cx.PSK[0]])
                        p.do("vector", lambda e, c=c: e.tensor_copy(
                            out=Bb[:, c * 512:(c + 1) * 512],
                            in_=cx.ps[0][:, 0:512]), r=[cx.PSK[0]],
                            w=[("Bb",)])
                        p.do("tensor", lambda e, c=c, n=n: e.matmul(
                            cx.ps[1][:, 0:512], lhsT=cx.ones_row_f[:],
                            rhs=crow[0:1, (n % 4) * T + c * 512:
                                     (n % 4) * T + (c + 1) * 512],
                            start=True, stop=True), r=[("crow",)],
                            w=[cx.PSK[1]])
                        p.do("vector", lambda e, c=c: e.tensor_copy(
                            out=Cb[:, c * 512:(c + 1) * 512],
                            in_=cx.ps[1][:, 0:512]), r=[cx.PSK[1]],
                            w=[("Cb",)])
                    for jj in range(4):
                        j = hf * 4 + jj
                        for c in range(T // 512):
                            csl = slice(c * 512, (c + 1) * 512)
                            tsl = slice(jj * T + c * 512,
                                        jj * T + (c + 1) * 512)
                            p.do("vector", lambda e, tsl=tsl, j=j, n=n:
                                 e.tensor_scalar(
                                     out=dAa[:], in0=dlt[:, tsl],
                                     scalar1=Acol[:, j * 16 + n:
                                                  j * 16 + n + 1],
                                     scalar2=None, op0=ALU.mult),
                                 r=[("dltS", jj), ("alog",)], w=[("dAa",)])
                            p.do("scalar", lambda e: e.activation(
                                out=dAa[:], in_=dAa[:], func=AF.Exp),
                                r=[("dAa",)], w=[("dAa",)])
                            p.do("vector", lambda e, tsl=tsl, csl=csl:
                                 e.tensor_mul(
                                     dBx[:], dxt[:, tsl], Bb[:, csl]),
                                 r=[("dxt", jj), ("Bb",)], w=[("dBx",)])
                            init = 0.0 if c == 0 else hsc[:, 512:513]
                            p.do("vector", lambda e, init=init:
                                 e.tensor_tensor_scan(
                                     out=hsc[:, 0:512], data0=dAa[:],
                                     data1=dBx[:], initial=init,
                                     op0=ALU.mult, op1=ALU.add),
                                 r=[("dAa",), ("dBx",), ("hcar",)],
                                 w=[("hsc",)])
                            if c < T // 512 - 1:
                                p.do("vector", lambda e: e.tensor_copy(
                                    out=hsc[:, 512:513],
                                    in_=hsc[:, 511:512]),
                                    r=[("hsc",)], w=[("hcar",)])
                            p.do("vector", lambda e, csl=csl: e.tensor_mul(
                                dBx[:], hsc[:, 0:512], Cb[:, csl]),
                                r=[("hsc",), ("Cb",)], w=[("dBx",)])
                            p.do("vector", lambda e, tsl=tsl: e.tensor_add(
                                yal[:, tsl], yal[:, tsl], dBx[:]),
                                r=[("dBx",), ("yal", jj)], w=[("yal", jj)])
                for jj in range(4):
                    j = hf * 4 + jj
                    p.do("vector", lambda e, jj=jj, j=j:
                         e.scalar_tensor_tensor(
                             out=yal[:, jj * T:(jj + 1) * T],
                             in0=xss[:, jj * T:(jj + 1) * T],
                             scalar=dpc[:, j:j + 1],
                             in1=yal[:, jj * T:(jj + 1) * T],
                             op0=ALU.mult, op1=ALU.add),
                         r=[("xssS", jj), ("dpc",), ("yal", jj)],
                         w=[("yal", jj)])
                    p.dma("sync", zsl[:, 0:T],
                          cx.dr["s_p2"][j * 128:(j + 1) * 128,
                                        b * T:(b + 1) * T],
                          r=[("zd", "x")], w=[("zsl",)])
                    p.do("vector", lambda e, jj=jj: e.tensor_mul(
                        dlt[:, jj * T:(jj + 1) * T],
                        yal[:, jj * T:(jj + 1) * T], zsl[:, 0:T]),
                        r=[("yal", jj), ("zsl",)], w=[("dltS", jj)])
                    p.dma("scalar",
                          cx.dr["s_p3"][j * 128:(j + 1) * 128,
                                        b * T:(b + 1) * T],
                          dlt[:, jj * T:(jj + 1) * T],
                          r=[("dltS", jj)], w=[("y2d", "x")])
    p.barrier()
    with contextlib.ExitStack() as stk:
        wout = stk.enter_context(sb("m1wo", [128, 8 * 1024], BF16))
        y2t = stk.enter_context(sb("m1y2t", [128, 8 * NT], BF16))
        lnoutf = stk.enter_context(sb("m1lno", [128, 8 * NT], BF16))
        resc = stk.enter_context(sb("m1res", [128, 8 * NT], F32))
        lnaf = stk.enter_context(sb("m1laf", [128, 8 * NT], BF16))
        gcol = stk.enter_context(sb("m1gc", [128, 8], F32))
        bcol = stk.enter_context(sb("m1bcl", [128, 8], F32))
        xt2 = stk.enter_context(sb("m1xt2", [128, 8 * NT], F32))
        _alloc_ln_scratch(cx, stk, NT)
        p.dma("sync", gcol[:], cx.dr["L1_attn_ln_gc"], w=[("c", "aff")])
        p.dma("sync", bcol[:], cx.dr["L1_attn_ln_bc"], w=[("c", "aff")])
        for k in range(8):
            p.dma("gpsimd", wout[:, k * 1024:(k + 1) * 1024],
                  cx.dr["L1_attn_W_out"][k * 128:(k + 1) * 128, :],
                  w=[("wout", k)])
        for sc in range(NSC):
            p.dma("sync", y2t.rearrange("p (j t) -> p j t", t=NT),
                  cx.dr["s_p3"].rearrange("(j p) t -> p j t", p=128)[
                      :, 0:8, sc * NT:(sc + 1) * NT],
                  r=[("y2d", "x")], w=[("y2t", j) for j in range(8)])
            p.dma("sync", lnoutf.rearrange("p (j t) -> p j t", t=NT),
                  cx.dr["s_ln"].rearrange("(j p) t -> p j t", p=128)[
                      :, 0:8, sc * NT:(sc + 1) * NT],
                  r=[("lndram", "x")], w=[("lnof", j) for j in range(8)])
            for dd in range(8):
                pk = cx.PSK[4 + dd % 2]
                ps = cx.ps[4 + dd % 2]
                for k in range(8):
                    p.do("tensor", lambda e, dd=dd, k=k, ps=ps: e.matmul(
                        ps[:, 0:NT],
                        lhsT=wout[:, k * 1024 + dd * 128:
                                  k * 1024 + (dd + 1) * 128],
                        rhs=y2t[:, k * NT:(k + 1) * NT],
                        start=(k == 0), stop=(k == 7)),
                        r=[("wout", k), ("y2t", k)], w=[pk])
                p.do("vector", lambda e, dd=dd, ps=ps: e.tensor_add(
                    resc[:, dd * NT:(dd + 1) * NT], ps[:, 0:NT],
                    lnoutf[:, dd * NT:(dd + 1) * NT]),
                    r=[pk, ("lnof", dd)], w=[("resc", dd)])
            emit_ln_chunk(p, cx, "resc", resc, lnaf, "lnaf", NT,
                          affine=(gcol, bcol))
            p.dma("sync", xt2.rearrange("p (j t) -> p j t", t=NT),
                  x_in.rearrange("(j p) t -> p j t", p=128)[
                      :, :, sc * NT:(sc + 1) * NT],
                  r=[(xk, "dram")], w=[("xt2", j) for j in range(8)])
            for dd in range(8):
                p.do("vector", lambda e, dd=dd: e.tensor_add(
                    resc[:, dd * NT:(dd + 1) * NT],
                    lnaf[:, dd * NT:(dd + 1) * NT],
                    xt2[:, dd * NT:(dd + 1) * NT]),
                    r=[("lnaf", dd), ("xt2", dd)], w=[("resc", dd)])
            p.dma("scalar",
                  x_out.rearrange("(j p) t -> p j t", p=128)[
                      :, :, sc * NT:(sc + 1) * NT],
                  resc.rearrange("p (j t) -> p j t", t=NT),
                  r=[("resc", dd) for dd in range(8)], w=[(xok, "dram")])
        p.barrier()




def emit_delta_core(p, cx, kind):
    nc = cx.nc
    T, TT = cx.T, cx.TT
    NCHK = T // 128
    sb = cx.sbt
    l0 = (kind == "l0")
    DV = 128 if l0 else 64
    pre = "L0_attn_" if l0 else "L3_attn_"
    gcolT, bcolT, ecolT, ncolT = cx.d_cols
    with contextlib.ExitStack() as stk:
        qr = stk.enter_context(sb("c_qr", [64, T], BF16))
        kr = stk.enter_context(sb("c_kr", [64, T], BF16))
        vr = stk.enter_context(sb("c_vr", [DV, T], BF16))
        qcv = stk.enter_context(sb("c_qcv", [64, T], F32))
        kcv = stk.enter_context(sb("c_kcv", [64, T], F32))
        vcv = stk.enter_context(sb("c_vcv", [DV, T], F32))
        qs = stk.enter_context(sb("c_qs", [64, T], BF16))
        ks = stk.enter_context(sb("c_ks", [64, T], BF16))
        vs = stk.enter_context(sb("c_vs", [DV, T], BF16))
        q2 = stk.enter_context(sb("c_q2", [64, T], BF16))
        nr = stk.enter_context(sb("c_nr", [1, T], F32))
        cqc = stk.enter_context(sb("c_cqc", [64, 64], F32))
        ckc = stk.enter_context(sb("c_ckc", [64, 64], F32))
        cvc = stk.enter_context(sb("c_cvc", [DV, 64], F32))
        grow = stk.enter_context(sb("c_grow", [1, T], F32))
        Es = stk.enter_context(sb("c_E", [128, 128], F32))
        Ei = stk.enter_context(sb("c_Ei", [128, 128], F32))
        mneg = stk.enter_context(sb("c_mneg", [128, 128], F32))
        m01s = stk.enter_context(sb("c_m01s", [128, 128], F32))
        m01i = stk.enter_context(sb("c_m01i", [128, 128], F32))
        mbd = stk.enter_context(sb("c_mbd", [128, 128], F32))
        mr = stk.enter_context(sb("c_mr", [128, 128], F32))
        tf = stk.enter_context(sb("c_tf", [128, 128], F32))
        MT0 = stk.enter_context(sb("c_MT0", [128, 128], F32))
        BDt = stk.enter_context(sb("c_BDt", [128, 128], F32))
        Rt = stk.enter_context(sb("c_Rt", [128, 128], F32))
        PB = [stk.enter_context(sb(f"c_PB{j}", [128, 128], F32))
              for j in range(3)]
        TBt = stk.enter_context(sb("c_TBt", [128, 128], F32))
        TBs = stk.enter_context(sb("c_TBs", [128, 128], F32))
        Nt = stk.enter_context(sb("c_Nt", [128, 128], F32))
        N2 = stk.enter_context(sb("c_N2", [128, 128], F32))
        N4 = stk.enter_context(sb("c_N4", [128, 128], F32))
        Mjs = stk.enter_context(sb("c_Mjs", [128, 128], F32))
        Wf = stk.enter_context(sb("c_W", [128, DV], F32))
        Ub = stk.enter_context(sb("c_Ub", [128, DV], BF16))
        U2b = stk.enter_context(sb("c_U2", [128, DV], BF16))
        bec = stk.enter_context(sb("c_bec", [128, 2], F32))
        Vt32 = stk.enter_context(sb("c_Vt", [128, DV], F32))
        Pt = None
        NNb = stk.enter_context(sb("c_NN", [128, 128], BF16))
        Ksb = stk.enter_context(sb("c_Ksb", [128, 64], BF16))
        S_f = stk.enter_context(sb("c_Sf", [64, DV], F32))
        S_b = stk.enter_context(sb("c_Sb", [64, DV], BF16))
        t_o = stk.enter_context(sb("c_to", [128, DV], F32))
        ot = stk.enter_context(sb("c_ot", [128, DV], F32))
        o2s = stk.enter_context(sb("c_o2s", [128, DV], BF16))
        racc = stk.enter_context(sb("c_racc", [128, 2], F32))
        wnb = stk.enter_context(sb("c_wnb", [128, DV], F32))
        gt = stk.enter_context(sb("c_gt", [128, (T // 128) * DV], BF16))
        o5b = stk.enter_context(sb("c_o5", [128, DV], BF16))
        obx = stk.enter_context(sb("c_obx", [DV, T], BF16))
        bc64 = stk.enter_context(sb("c_bc64", [64, 1], F32))
        nw1 = stk.enter_context(sb("c_nw1", [1, DV], F32))
        p.dma("sync", mneg[:], cx.dr["c_mneg"], w=[("mneg",)])
        p.dma("sync", m01s[:], cx.dr["c_m01s"], w=[("m01s",)])
        p.dma("sync", m01i[:], cx.dr["c_m01i"], w=[("m01i",)])
        p.dma("sync", mbd[:], cx.dr["c_mbd"], w=[("mbd",)])
        p.dma("sync", mr[:], cx.dr["c_mr"], w=[("mr",)])
        p.dma("sync", cqc[:], cx.dr[pre + ("conv_q_wc" if l0 else "cq_wc")],
              w=[("cqc",)])
        p.dma("sync", ckc[:], cx.dr[pre + ("conv_k_wc" if l0 else "ck_wc")],
              w=[("ckc",)])
        p.dma("sync", cvc[:], cx.dr[pre + ("conv_v_wc" if l0 else "cv_wc")],
              w=[("cvc",)])
        p.dma("sync", nw1[:], cx.dr[pre + ("o_norm_w1" if l0 else "nw1")],
              w=[("nw1",)])
        cbq = cbk = cbv = None
        if not l0:
            cbq = stk.enter_context(sb("c_cbq", [64, 16], F32))
            cbk = stk.enter_context(sb("c_cbk", [64, 16], F32))
            cbv = stk.enter_context(sb("c_cbv", [64, 16], F32))
            p.dma("sync", cbq[:], cx.dr["L3_attn_cq_bc"], w=[("cbq",)])
            p.dma("sync", cbk[:], cx.dr["L3_attn_ck_bc"], w=[("cbk",)])
            p.dma("sync", cbv[:], cx.dr["L3_attn_cv_bc"], w=[("cbv",)])
        p.do("tensor", lambda e: e.matmul(
            cx.ps[5][:, 0:DV], lhsT=cx.ones_row_f[:], rhs=nw1[0:1, 0:DV],
            start=True, stop=True), r=[("nw1",)], w=[cx.PSK[5]])
        p.do("vector", lambda e: e.tensor_copy(out=wnb[:], in_=cx.ps[5][:, 0:DV]),
             r=[cx.PSK[5]], w=[("wnb",)])
        rmse = 1e-6 if l0 else 1e-5

        def conv(dst, dstk, raw, rawk, P_, wc, wck, hh, bias, biask):
            c0 = hh * 4
            if bias is None:
                p.do("vector", lambda e: e.tensor_scalar(
                    out=dst[0:P_, :], in0=raw[0:P_, :],
                    scalar1=wc[0:P_, c0 + 3:c0 + 4], scalar2=None,
                    op0=ALU.mult), r=[rawk, wck], w=[dstk])
            else:
                p.do("vector", lambda e: e.tensor_scalar(
                    out=dst[0:P_, :], in0=raw[0:P_, :],
                    scalar1=wc[0:P_, c0 + 3:c0 + 4],
                    scalar2=bias[0:P_, hh:hh + 1],
                    op0=ALU.mult, op1=ALU.add), r=[rawk, wck, biask],
                    w=[dstk])
            for jt in range(3):
                s = 3 - jt
                p.do("vector", lambda e, jt=jt, s=s: e.scalar_tensor_tensor(
                    out=dst[0:P_, s:T], in0=raw[0:P_, 0:T - s],
                    scalar=wc[0:P_, c0 + jt:c0 + jt + 1],
                    in1=dst[0:P_, s:T], op0=ALU.mult, op1=ALU.add),
                    r=[rawk, wck, dstk], w=[dstk])
            p.do("scalar", lambda e: e.activation(
                out=dst[0:P_, :], in_=dst[0:P_, :], func=AF.Silu),
                r=[dstk], w=[dstk])

        def l2n(dst, dstk, src, srck, scale):
            p.do("scalar", lambda e: e.activation(
                out=q2[:], in_=src[0:64, :], func=AF.Square),
                r=[srck], w=[("q2",)])
            for c in range(T // 512):
                p.do("tensor", lambda e, c=c: e.matmul(
                    cx.ps[0][0:1, 0:512], lhsT=cx.ones_bf[0:64, :],
                    rhs=q2[:, c * 512:(c + 1) * 512],
                    start=True, stop=True), r=[("q2",)], w=[cx.PSK[0]])
                p.do("scalar", lambda e, c=c: e.activation(
                    out=nr[0:1, c * 512:(c + 1) * 512],
                    in_=cx.ps[0][0:1, 0:512], func=AF.Sqrt,
                    bias=cx.cval(1e-6, 1)), r=[cx.PSK[0]], w=[("nr",)])
            p.do("vector", lambda e: e.reciprocal(
                out=nr[0:1, 0:T], in_=nr[0:1, 0:T]), r=[("nr",)], w=[("nr",)])
            for c in range(T // 512):
                p.do("tensor", lambda e, c=c: e.matmul(
                    cx.ps[0][0:64, 0:512], lhsT=cx.ones_row_f[0:1, 0:64],
                    rhs=nr[0:1, c * 512:(c + 1) * 512],
                    start=True, stop=True), r=[("nr",)], w=[cx.PSK[0]])
                p.do("vector", lambda e, c=c: e.scalar_tensor_tensor(
                    out=dst[0:64, c * 512:(c + 1) * 512],
                    in0=src[0:64, c * 512:(c + 1) * 512], scalar=scale,
                    in1=cx.ps[0][0:64, 0:512], op0=ALU.mult, op1=ALU.mult),
                    r=[srck, cx.PSK[0]], w=[dstk])

        for b in range(2):
            for h in range(16):
                p.dma("sync", qr[:], cx.dr["s_p1"][h * 64:(h + 1) * 64,
                                                   b * T:(b + 1) * T],
                      r=[("s_p1", "x")], w=[("qr",)])
                p.dma("sync", kr[:], cx.dr["s_p2"][h * 64:(h + 1) * 64,
                                                   b * T:(b + 1) * T],
                      r=[("s_p2", "x")], w=[("kr",)])
                p.dma("sync", vr[:], cx.dr["s_p3"][h * DV:(h + 1) * DV,
                                                   b * T:(b + 1) * T],
                      r=[("s_p3", "x")], w=[("vr",)])
                if l0:
                    p.dma("sync", grow[:], cx.dr["s_g"][h:h + 1,
                                                        b * T:(b + 1) * T],
                          r=[("gd",)], w=[("grow",)])
                p.dma("sync", gt.rearrange("p (c v) -> p c v", v=DV),
                      cx.dr["s_gtm"].rearrange("(c p) v -> p c v", p=128)[
                          :, b * NCHK:(b + 1) * NCHK, h * DV:(h + 1) * DV],
                      r=[("gtm", "x")], w=[("gt",)])
                conv(qcv, ("qcv",), qr, ("qr",), 64, cqc, ("cqc",), h,
                     cbq, ("cbq",))
                conv(kcv, ("kcv",), kr, ("kr",), 64, ckc, ("ckc",), h,
                     cbk, ("cbk",))
                conv(vcv, ("vcv",), vr, ("vr",), DV, cvc, ("cvc",), h,
                     cbv, ("cbv",))
                l2n(qs, ("qs",), qcv, ("qcv",), 0.125)
                l2n(ks, ("ks",), kcv, ("kcv",), 1.0)
                p.do("vector", lambda e: e.tensor_copy(
                    out=vs[:], in_=vcv[:]), r=[("vcv",)], w=[("vs",)])
                p.do("vector", lambda e: e.memset(S_f[:], 0.0), w=[("Sf",)])
                p.do("vector", lambda e: e.memset(S_b[:], 0.0), w=[("Sb",)])
                cgb = b * NCHK
                for c in range(NCHK):
                    csl = slice(c * 128, (c + 1) * 128)
                    gsl = gcolT[:, (cgb + c) * 16 + h:(cgb + c) * 16 + h + 1]
                    bsl = bcolT[:, (cgb + c) * 16 + h:(cgb + c) * 16 + h + 1]
                    esl = ecolT[:, (cgb + c) * 16 + h:(cgb + c) * 16 + h + 1]
                    nsl = ncolT[:, (cgb + c) * 16 + h:(cgb + c) * 16 + h + 1]
                    if l0:
                        p.do("tensor", lambda e, csl=csl: e.matmul(
                            cx.ps[0][:, 0:128], lhsT=cx.ones_row_f[:],
                            rhs=grow[0:1, csl], start=True, stop=True),
                            r=[("grow",)], w=[cx.PSK[0]])
                        p.do("vector", lambda e, gsl=gsl: (
                            e.scalar_tensor_tensor(
                                out=Es[:], in0=cx.ps[0][:, 0:128],
                                scalar=gsl, in1=mneg[:],
                                op0=ALU.subtract, op1=ALU.add)),
                            r=[cx.PSK[0], ("gcolT",), ("mneg",)], w=[("Es",)])
                        p.do("scalar", lambda e: e.activation(
                            out=Es[:], in_=Es[:], func=AF.Exp),
                            r=[("Es",)], w=[("Es",)])
                        p.do("vector", lambda e: e.tensor_add(
                            Ei[:], Es[:], cx.id_f[:]),
                            r=[("Es",), ("c", "idf")], w=[("Ei",)])
                        p.do("scalar", lambda e: e.activation(
                            out=bc64[:], in_=cx.ps[0][0:64, 127:128],
                            func=AF.Exp), r=[cx.PSK[0]], w=[("bc64",)])
                    p.do("tensor", lambda e, csl=csl: e.matmul(
                        cx.ps[1][:, 0:128], lhsT=ks[:, csl], rhs=ks[:, csl],
                        start=True, stop=True), r=[("ks",)], w=[cx.PSK[1]])
                    if l0:
                        p.do("vector", lambda e: e.tensor_mul(
                            tf[:], cx.ps[1][:, 0:128], Es[:]),
                            r=[cx.PSK[1], ("Es",)], w=[("tf",)])
                        p.do("vector", lambda e, bsl=bsl: e.tensor_scalar(
                            out=MT0[:], in0=tf[:], scalar1=bsl,
                            scalar2=None, op0=ALU.mult),
                            r=[("tf",), ("bcolT",)], w=[("MT", 0)])
                    else:
                        p.do("vector", lambda e, bsl=bsl: (
                            e.scalar_tensor_tensor(
                                out=MT0[:], in0=cx.ps[1][:, 0:128],
                                scalar=bsl, in1=m01s[:],
                                op0=ALU.mult, op1=ALU.mult)),
                            r=[cx.PSK[1], ("bcolT",), ("m01s",)],
                            w=[("MT", 0)])
                    p.do("tensor", lambda e, csl=csl: e.transpose(
                        cx.psbf[:, 0:128][0:128, 0:DV], vs[:, csl],
                        cx.id_bf[0:DV, 0:DV]), r=[("vs",), ("c", "idb")],
                        w=[("psbf", 0)])
                    p.do("vector", lambda e: e.tensor_copy(
                        out=Vt32[:, 0:DV], in_=cx.psbf[:, 0:128][0:128, 0:DV]),
                        r=[("psbf", 0)], w=[("Vt",)])
                    p.do("tensor", lambda e, csl=csl: e.matmul(
                        cx.ps[2][:, 0:DV], lhsT=ks[:, csl], rhs=S_b[:],
                        start=True, stop=True), r=[("ks",), ("Sb",)],
                        w=[cx.PSK[2]])
                    if l0:
                        p.do("vector", lambda e, nsl=nsl: (
                            e.scalar_tensor_tensor(
                                out=Wf[:], in0=cx.ps[2][:, 0:DV],
                                scalar=nsl, in1=Vt32[:, 0:DV],
                                op0=ALU.mult, op1=ALU.add)),
                            r=[cx.PSK[2], ("ncolT",), ("Vt",)], w=[("Wf",)])
                    else:
                        p.do("vector", lambda e: e.scalar_tensor_tensor(
                            out=Wf[:], in0=cx.ps[2][:, 0:DV], scalar=-1.0,
                            in1=Vt32[:, 0:DV], op0=ALU.mult, op1=ALU.add),
                            r=[cx.PSK[2], ("Vt",)], w=[("Wf",)])
                    def tpose(dst, dstk, srck, src_t):
                        p.do("tensor", lambda e, src_t=src_t: e.transpose(
                            cx.ps[0][:, 0:128], src_t[:], cx.id_f[:]),
                            r=[srck, ("c", "idf")], w=[cx.PSK[0]])
                        p.do("vector", lambda e, dst=dst: e.tensor_copy(
                            out=dst[:], in_=cx.ps[0][:, 0:128]),
                            r=[cx.PSK[0]], w=[dstk])

                    def mmat(dst, dstk, lh, lhk, rh, rhk, addid=None):
                        p.do("tensor", lambda e, lh=lh, rh=rh: e.matmul(
                            cx.ps[3][:, 0:128], lhsT=lh[:], rhs=rh[:],
                            start=True, stop=True),
                            r=[lhk, rhk], w=[cx.PSK[3]])
                        if addid is None:
                            p.do("vector", lambda e, dst=dst: e.tensor_copy(
                                out=dst[:], in_=cx.ps[0 + 3][:, 0:128]),
                                r=[cx.PSK[3]], w=[dstk])
                        else:
                            p.do("vector", lambda e, dst=dst, a=addid:
                                 e.tensor_add(dst[:], a[:],
                                              cx.ps[3][:, 0:128]),
                                 r=[cx.PSK[3], ("xid",) if a is cx.id_f
                                    else ("tmpk",)], w=[dstk])
                    p.do("vector", lambda e: e.tensor_mul(
                        BDt[:], MT0[:], mbd[:]),
                        r=[("MT", 0), ("mbd",)], w=[("BDt",)])
                    p.do("vector", lambda e: e.tensor_mul(
                        Rt[:], MT0[:], mr[:]),
                        r=[("MT", 0), ("mr",)], w=[("Rt",)])
                    srcs = [(BDt, ("BDt",))]
                    for j in range(3):
                        sm, smk = srcs[-1]
                        tpose(Mjs, ("Mjs",), smk, sm)
                        mmat(PB[j], ("PB", j), Mjs, ("Mjs",), sm, smk)
                        srcs.append((PB[j], ("PB", j)))
                    p.do("vector", lambda e: e.tensor_sub(
                        TBt[:], cx.id_f[:], BDt[:]),
                        r=[("c", "idf"), ("BDt",)], w=[("TBt",)])
                    for j in range(3):
                        tpose(Mjs, ("Mjs",), ("PB", j), PB[j])
                        p.do("tensor", lambda e: e.matmul(
                            cx.ps[3][:, 0:128], lhsT=Mjs[:], rhs=TBt[:],
                            start=True, stop=True),
                            r=[("Mjs",), ("TBt",)], w=[cx.PSK[3]])
                        p.do("vector", lambda e: e.tensor_add(
                            TBt[:], TBt[:], cx.ps[3][:, 0:128]),
                            r=[("TBt",), cx.PSK[3]], w=[("TBt",)])
                    tpose(Mjs, ("Mjs",), ("Rt",), Rt)
                    mmat(Nt, ("Nt",), Mjs, ("Mjs",), TBt, ("TBt",))
                    tpose(Mjs, ("Mjs",), ("Nt",), Nt)
                    mmat(N2, ("N2",), Mjs, ("Mjs",), Nt, ("Nt",))
                    tpose(Mjs, ("Mjs",), ("N2",), N2)
                    mmat(N4, ("N4",), Mjs, ("Mjs",), N2, ("N2",))
                    p.do("tensor", lambda e: e.matmul(
                        cx.ps[3][:, 0:DV], lhsT=TBt[:], rhs=Wf[:],
                        start=True, stop=True),
                        r=[("TBt",), ("Wf",)], w=[cx.PSK[3]])
                    p.do("vector", lambda e: e.tensor_copy(
                        out=Wf[:], in_=cx.ps[3][:, 0:DV]),
                        r=[cx.PSK[3]], w=[("Wf",)])
                    for (mat, mk, sub) in ((N4, ("N4",), False),
                                           (N2, ("N2",), False),
                                           (Nt, ("Nt",), True)):
                        p.do("tensor", lambda e, mat=mat: e.matmul(
                            cx.ps[3][:, 0:DV], lhsT=mat[:], rhs=Wf[:],
                            start=True, stop=True),
                            r=[mk, ("Wf",)], w=[cx.PSK[3]])
                        if not sub:
                            p.do("vector", lambda e: e.tensor_add(
                                Wf[:], Wf[:], cx.ps[3][:, 0:DV]),
                                r=[("Wf",), cx.PSK[3]], w=[("Wf",)])
                        else:
                            p.do("vector", lambda e: e.tensor_sub(
                                Wf[:], Wf[:], cx.ps[3][:, 0:DV]),
                                r=[("Wf",), cx.PSK[3]], w=[("Wf",)])
                    p.do("vector", lambda e, bsl=bsl: e.tensor_scalar(
                        out=Ub[:], in0=Wf[:], scalar1=bsl, scalar2=None,
                        op0=ALU.mult), r=[("Wf",), ("bcolT",)], w=[("Ub",)])
                    if l0:
                        p.do("vector", lambda e, bsl=bsl: e.tensor_mul(
                            bec[:, 0:1], bsl, Ei[:, 127:128]),
                            r=[("bcolT",), ("Ei",)], w=[("becK",)])
                        p.do("vector", lambda e: e.tensor_scalar(
                            out=U2b[:], in0=Wf[:], scalar1=bec[:, 0:1],
                            scalar2=None, op0=ALU.mult),
                            r=[("Wf",), ("becK",)], w=[("U2b",)])
                    p.do("tensor", lambda e, csl=csl: e.matmul(
                        cx.ps[4][:, 0:128], lhsT=ks[:, csl], rhs=qs[:, csl],
                        start=True, stop=True), r=[("ks",), ("qs",)],
                        w=[cx.PSK[4]])
                    p.do("vector", lambda e: e.tensor_mul(
                        NNb[:], cx.ps[4][:, 0:128],
                        Ei[:] if l0 else m01i[:]),
                        r=[cx.PSK[4], ("Ei",) if l0 else ("m01i",)],
                        w=[("NNb",)])
                    if l0:
                        p.do("tensor", lambda e: e.matmul(
                            cx.ps[5][:, 0:DV], lhsT=NNb[:], rhs=Ub[:],
                            start=True, stop=True),
                            r=[("NNb",), ("Ub",)], w=[cx.PSK[5]])
                        p.do("tensor", lambda e, csl=csl: e.matmul(
                            cx.ps[6][:, 0:DV], lhsT=qs[:, csl], rhs=S_b[:],
                            start=True, stop=True),
                            r=[("qs",), ("Sb",)], w=[cx.PSK[6]])
                        p.do("vector", lambda e, esl=esl: e.tensor_scalar(
                            out=t_o[:], in0=cx.ps[6][:, 0:DV], scalar1=esl,
                            scalar2=None, op0=ALU.mult),
                            r=[cx.PSK[6], ("ecolT",)], w=[("t_o",)])
                        p.do("vector", lambda e: e.tensor_add(
                            ot[:], t_o[:], cx.ps[5][:, 0:DV]),
                            r=[("t_o",), cx.PSK[5]], w=[("ot",)])
                    else:
                        p.do("tensor", lambda e: e.matmul(
                            cx.ps[5][:, 0:DV], lhsT=NNb[:], rhs=Ub[:],
                            start=True, stop=False),
                            r=[("NNb",), ("Ub",)], w=[cx.PSK[5]])
                        p.do("tensor", lambda e, csl=csl: e.matmul(
                            cx.ps[5][:, 0:DV], lhsT=qs[:, csl], rhs=S_b[:],
                            start=False, stop=True),
                            r=[("qs",), ("Sb",)], w=[cx.PSK[5]])
                        p.do("vector", lambda e: e.tensor_copy(
                            out=ot[:], in_=cx.ps[5][:, 0:DV]),
                            r=[cx.PSK[5]], w=[("ot",)])
                    p.do("scalar", lambda e: e.activation(
                        out=o2s[:], in_=ot[:], func=AF.Square,
                        accum_out=racc[:, 0:1]), r=[("ot",)], w=[("racc",)])
                    p.do("vector", lambda e: e.tensor_scalar(
                        out=racc[:, 1:2], in0=racc[:, 0:1],
                        scalar1=1.0 / DV, scalar2=None, op0=ALU.mult),
                        r=[("racc",)], w=[("racc2",)])
                    p.do("scalar", lambda e: e.activation(
                        out=racc[:, 1:2], in_=racc[:, 1:2], func=AF.Sqrt,
                        bias=cx.cval(rmse)), r=[("racc2",)], w=[("racc2",)])
                    p.do("vector", lambda e: e.reciprocal(
                        out=racc[:, 1:2], in_=racc[:, 1:2]),
                        r=[("racc2",)], w=[("racc2",)])
                    p.do("vector", lambda e: e.tensor_scalar(
                        out=ot[:], in0=ot[:], scalar1=racc[:, 1:2],
                        scalar2=None, op0=ALU.mult),
                        r=[("ot",), ("racc2",)], w=[("ot",)])
                    p.do("vector", lambda e: e.tensor_mul(
                        ot[:], ot[:], wnb[:]), r=[("ot",), ("wnb",)],
                        w=[("ot",)])
                    p.do("vector", lambda e, c=c: e.tensor_mul(
                        o5b[:], ot[:], gt[:, c * DV:(c + 1) * DV]),
                        r=[("ot",), ("gt",)], w=[("o5b",)])
                    p.do("tensor", lambda e: e.transpose(
                        cx.psbf[:, 384:512][0:DV, 0:128], o5b[:, 0:DV],
                        cx.id_bf[:]), r=[("o5b",), ("c", "idb")],
                        w=[("psbf", 3)])
                    p.do("vector", lambda e, c=c: e.tensor_copy(
                        out=obx[:, c * 128:(c + 1) * 128],
                        in_=cx.psbf[:, 384:512][0:DV, 0:128]),
                        r=[("psbf", 3)], w=[("obx", c)])
                    p.do("tensor", lambda e, csl=csl: e.transpose(
                        cx.psbf[:, 256:384][0:128, 0:64], ks[:, csl],
                        cx.id_bf[0:64, 0:64]), r=[("ks",), ("c", "idb")],
                        w=[("psbf", 2)])
                    p.do("vector", lambda e: e.tensor_copy(
                        out=Ksb[:, 0:64], in_=cx.psbf[:, 256:384][0:128, 0:64]),
                        r=[("psbf", 2)], w=[("Ksb",)])
                    p.do("tensor", lambda e: e.matmul(
                        cx.ps[1][0:64, 0:DV], lhsT=Ksb[:, 0:64],
                        rhs=U2b[:] if l0 else Ub[:],
                        start=True, stop=True),
                        r=[("Ksb",), ("U2b",) if l0 else ("Ub",)],
                        w=[cx.PSK[1]])
                    if l0:
                        p.do("vector", lambda e: e.scalar_tensor_tensor(
                            out=S_f[:], in0=S_f[:], scalar=bc64[:, 0:1],
                            in1=cx.ps[1][0:64, 0:DV], op0=ALU.mult,
                            op1=ALU.add),
                            r=[("Sf",), ("bc64",), cx.PSK[1]], w=[("Sf",)])
                    else:
                        p.do("vector", lambda e: e.tensor_add(
                            S_f[:], S_f[:], cx.ps[1][0:64, 0:DV]),
                            r=[("Sf",), cx.PSK[1]], w=[("Sf",)])
                    p.do("vector", lambda e: e.tensor_copy(
                        out=S_b[:], in_=S_f[:]), r=[("Sf",)], w=[("Sb",)])
                p.dma("scalar",
                      cx.dr["s_ofm"][h * DV:(h + 1) * DV, b * T:(b + 1) * T],
                      obx[:], r=[("obx", c) for c in range(NCHK)],
                      w=[("od", "x")])


def emit_delta(p, cx, x_in, xk, x_out, xok, kind):
    nc = cx.nc
    NT, TT, T = cx.NT, cx.TT, cx.T
    NSC = TT // NT
    NCHK = T // 128
    sb = cx.sbt
    l0 = (kind == "l0")
    DV = 128 if l0 else 64
    pre = "L0_attn_" if l0 else "L3_attn_"
    NCHK_ = T // 128
    ostk = contextlib.ExitStack()
    gcolT = ostk.enter_context(sb("d_gct", [128, NCHK_ * 2 * 16], F32))
    bcolT = ostk.enter_context(sb("d_bct", [128, NCHK_ * 2 * 16], F32))
    ecolT = ostk.enter_context(sb("d_ect", [128, NCHK_ * 2 * 16], F32))
    ncolT = ostk.enter_context(sb("d_nct", [128, NCHK_ * 2 * 16], F32))
    cx.d_cols = (gcolT, bcolT, ecolT, ncolT)
    with contextlib.ExitStack() as stk:
        xc = stk.enter_context(sb("d_xc", [128, 8 * NT], F32))
        lnb = stk.enter_context(sb("d_ln", [128, 8 * NT], BF16))
        hb = stk.enter_context(sb("d_hb", [128, 8 * NT], BF16))
        wqb = stk.enter_context(sb("d_wq", [128, 8 * 1024], BF16))
        wkb = stk.enter_context(sb("d_wk", [128, 8 * 1024], BF16))
        wvb = stk.enter_context(sb("d_wv", [128, 8 * DV * 16], BF16))
        wgb = stk.enter_context(sb("d_wg", [128, 8 * DV * 16], BF16))
        wbb = stk.enter_context(sb("d_wb", [128, 8 * 16], BF16))
        wab = None
        ev = stk.enter_context(sb("d_ev", [128, 8 * NT], BF16))
        tE = stk.enter_context(sb("d_te", [128, 2 * NT], F32))
        beta_sb = stk.enter_context(sb("d_bs", [16, NT], F32))
        nw_col = stk.enter_context(sb("d_nw", [128, 8], F32))
        g_sb = None
        _alloc_ln_scratch(cx, stk, NT)
        if l0:
            wab = stk.enter_context(sb("d_wa", [128, 8 * 16], BF16))
            g_sb = stk.enter_context(sb("d_gs", [16, NT], F32))
            al1 = stk.enter_context(sb("d_al", [16, 2], F32))
            p.dma("sync", al1[:, 0:1], cx.dr["L0_attn_A_log1"],
                  w=[("al1",)])
            p.dma("sync", al1[:, 1:2], cx.dr["L0_attn_dt_bias1"],
                  w=[("dtb",)])
            p.do("scalar", lambda e: e.activation(
                out=al1[:, 0:1], in_=al1[:, 0:1], func=AF.Exp),
                r=[("al1",)], w=[("al1",)])
            p.do("vector", lambda e: e.tensor_scalar(
                out=al1[:, 0:1], in0=al1[:, 0:1], scalar1=-1.0, scalar2=None,
                op0=ALU.mult), r=[("al1",)], w=[("al1",)])
            p.dma("sync", nw_col[:], cx.dr["L0_attn_norm_wc"],
                  w=[("nwc",)])
        wnames = {"q": "q_w" if l0 else "Wq", "k": "k_w" if l0 else "Wk",
                  "v": "v_w" if l0 else "Wv", "g": "g_w" if l0 else "Wg",
                  "b": "b_w" if l0 else "Wb"}
        for k in range(8):
            p.dma("gpsimd", wqb[:, k * 1024:(k + 1) * 1024],
                  cx.dr[pre + wnames["q"]][k * 128:(k + 1) * 128, :],
                  w=[("wq", k)])
            p.dma("gpsimd", wkb[:, k * 1024:(k + 1) * 1024],
                  cx.dr[pre + wnames["k"]][k * 128:(k + 1) * 128, :],
                  w=[("wk", k)])
            p.dma("gpsimd", wvb[:, k * DV * 16:(k + 1) * DV * 16],
                  cx.dr[pre + wnames["v"]][k * 128:(k + 1) * 128, :],
                  w=[("wv", k)])
            p.dma("gpsimd", wgb[:, k * DV * 16:(k + 1) * DV * 16],
                  cx.dr[pre + wnames["g"]][k * 128:(k + 1) * 128, :],
                  w=[("wg", k)])
            p.dma("gpsimd", wbb[:, k * 16:(k + 1) * 16],
                  cx.dr[pre + wnames["b"]][k * 128:(k + 1) * 128, :],
                  w=[("wb", k)])
            if l0:
                p.dma("gpsimd", wab[:, k * 16:(k + 1) * 16],
                      cx.dr["L0_attn_a_w"][k * 128:(k + 1) * 128, :],
                      w=[("wa", k)])
        for sc in range(NSC):
            load_x_chunk(p, cx, x_in, xk, sc, xc, "xc", NT)
            emit_ln_chunk(p, cx, "xc", xc, lnb, "ln", NT)
            for j in range(8):
                p.dma("scalar", cx.dr["s_ln"][j * 128:(j + 1) * 128,
                                              sc * NT:(sc + 1) * NT],
                      lnb[:, j * NT:(j + 1) * NT],
                      r=[("ln", j)], w=[("lndram", sc, j)])
            if l0:
                for j in range(8):
                    p.do("vector", lambda e, j=j: e.tensor_copy(
                        out=xc[:, j * NT:(j + 1) * NT],
                        in_=lnb[:, j * NT:(j + 1) * NT]),
                        r=[("ln", j)], w=[("xc", j)])
                emit_rms_chunk(p, cx, "xc", xc, hb, "hb", NT, nw_col, 1e-6)
                srck = "hb"
                srcb = hb
            else:
                srck = "ln"
                srcb = lnb
            for (wname, wbuf, dst) in (("wq", wqb, "s_p1"),
                                       ("wk", wkb, "s_p2")):
                for mi in range(8):
                    pk = cx.PSK[4 + mi % 2]
                    ps = cx.ps[4 + mi % 2]
                    for k in range(8):
                        p.do("tensor", lambda e, mi=mi, k=k, ps=ps,
                             wbuf=wbuf: e.matmul(
                                 ps[:, 0:NT],
                                 lhsT=wbuf[:, k * 1024 + mi * 128:
                                           k * 1024 + (mi + 1) * 128],
                                 rhs=srcb[:, k * NT:(k + 1) * NT],
                                 start=(k == 0), stop=(k == 7)),
                             r=[(wname, k), (srck, k)], w=[pk])
                    p.do("vector", lambda e, ps=ps, mi=mi: e.tensor_copy(
                        out=ev[:, mi * NT:(mi + 1) * NT], in_=ps[:, 0:NT]),
                        r=[pk], w=[("ev", mi)])
                p.dma("scalar",
                      cx.dr[dst].rearrange("(j p) t -> p j t", p=128)[
                          :, 0:8, sc * NT:(sc + 1) * NT],
                      ev.rearrange("p (j t) -> p j t", t=NT)[:, 0:8, :],
                      r=[("ev", mi) for mi in range(8)], w=[(dst, "x")])
            NV = DV * 16 // 128
            for mi in range(NV):
                pk = cx.PSK[4 + mi % 2]
                ps = cx.ps[4 + mi % 2]
                for k in range(8):
                    p.do("tensor", lambda e, mi=mi, k=k, ps=ps: e.matmul(
                        ps[:, 0:NT],
                        lhsT=wvb[:, k * DV * 16 + mi * 128:
                                 k * DV * 16 + (mi + 1) * 128],
                        rhs=srcb[:, k * NT:(k + 1) * NT],
                        start=(k == 0), stop=(k == 7)),
                        r=[("wv", k), (srck, k)], w=[pk])
                p.do("vector", lambda e, ps=ps, mi=mi: e.tensor_copy(
                    out=ev[:, (mi % 8) * NT:(mi % 8 + 1) * NT],
                    in_=ps[:, 0:NT]),
                    r=[pk], w=[("ev", mi % 8)])
                if mi % 8 == 7:
                    mb = mi - 7
                    p.dma("scalar",
                          cx.dr["s_p3"].rearrange("(j p) t -> p j t", p=128)[
                              :, mb:mb + 8, sc * NT:(sc + 1) * NT],
                          ev.rearrange("p (j t) -> p j t", t=NT),
                          r=[("ev", q) for q in range(8)],
                          w=[("s_p3", "x")])
            NN_ = DV * 16 // 512
            for tt in range(NT // 128):
                for nn in range(NN_):
                    pk = cx.PSK[4 + nn % 2]
                    ps = cx.ps[4 + nn % 2]
                    for k in range(8):
                        p.do("tensor", lambda e, tt=tt, nn=nn, k=k, ps=ps:
                             e.matmul(
                                 ps[:, 0:512],
                                 lhsT=srcb[:, k * NT + tt * 128:
                                           k * NT + (tt + 1) * 128],
                                 rhs=wgb[:, k * DV * 16 + nn * 512:
                                         k * DV * 16 + (nn + 1) * 512],
                                 start=(k == 0), stop=(k == 7)),
                             r=[(srck, k), ("wg", k)], w=[pk])
                    p.do("scalar", lambda e, ps=ps, nn=nn: e.activation(
                        out=ev[:, nn * 512:(nn + 1) * 512], in_=ps[:, 0:512],
                        func=AF.Silu), r=[pk], w=[("ev", nn)])
                p.dma("scalar",
                      cx.dr["s_gtm"][sc * NT + tt * 128:
                                     sc * NT + (tt + 1) * 128,
                                     0:NN_ * 512],
                      ev[:, 0:NN_ * 512],
                      r=[("ev", nn) for nn in range(NN_)], w=[("gtm", "x")])
            for k in range(8):
                p.do("tensor", lambda e, k=k: e.matmul(
                    cx.ps[0][0:16, 0:NT], lhsT=wbb[:, k * 16:(k + 1) * 16],
                    rhs=srcb[:, k * NT:(k + 1) * NT],
                    start=(k == 0), stop=(k == 7)),
                    r=[("wb", k), (srck, k)], w=[cx.PSK[0]])
            p.do("scalar", lambda e, sc=sc: e.activation(
                out=beta_sb[0:16, 0:NT],
                in_=cx.ps[0][0:16, 0:NT], func=AF.Sigmoid),
                r=[cx.PSK[0]], w=[("beta",)])
            p.dma("scalar", cx.dr["s_beta"][0:16, sc * NT:(sc + 1) * NT],
                  beta_sb[0:16, 0:NT], r=[("beta",)], w=[("betad",)])
            if l0:
                for k in range(8):
                    p.do("tensor", lambda e, k=k: e.matmul(
                        cx.ps[1][0:16, 0:NT],
                        lhsT=wab[:, k * 16:(k + 1) * 16],
                        rhs=srcb[:, k * NT:(k + 1) * NT],
                        start=(k == 0), stop=(k == 7)),
                        r=[("wa", k), (srck, k)], w=[cx.PSK[1]])
                p.do("vector", lambda e: e.tensor_scalar(
                    out=tE[0:16, 0:NT], in0=cx.ps[1][0:16, 0:NT],
                    scalar1=al1[0:16, 1:2], scalar2=None, op0=ALU.add),
                    r=[cx.PSK[1], ("dtb",)], w=[("spx",)])
                p.do("scalar", lambda e: e.activation(
                    out=tE[0:16, NT:2 * NT], in_=tE[0:16, 0:NT],
                    func=AF.Abs), r=[("spx",)], w=[("spa",)])
                p.do("scalar", lambda e: e.activation(
                    out=tE[0:16, NT:2 * NT], in_=tE[0:16, NT:2 * NT],
                    func=AF.Exp, scale=-1.0), r=[("spa",)], w=[("spa",)])
                p.do("scalar", lambda e: e.activation(
                    out=tE[0:16, NT:2 * NT], in_=tE[0:16, NT:2 * NT],
                    func=AF.Ln, bias=cx.cval(1.0, 16)),
                    r=[("spa",)], w=[("spa",)])
                p.do("vector", lambda e: e.tensor_scalar(
                    out=tE[0:16, 0:NT], in0=tE[0:16, 0:NT], scalar1=0.0,
                    scalar2=None, op0=ALU.max), r=[("spx",)], w=[("spx",)])
                p.do("vector", lambda e: e.tensor_add(
                    tE[0:16, 0:NT], tE[0:16, 0:NT], tE[0:16, NT:2 * NT]),
                    r=[("spx",), ("spa",)], w=[("spx",)])
                p.do("vector", lambda e, sc=sc: e.tensor_scalar(
                    out=g_sb[0:16, 0:NT],
                    in0=tE[0:16, 0:NT], scalar1=al1[0:16, 0:1],
                    scalar2=None, op0=ALU.mult),
                    r=[("spx",), ("al1",)], w=[("gsb",)])
                p.dma("scalar",
                      cx.dr["s_graw"][0:16, sc * NT:(sc + 1) * NT],
                      g_sb[0:16, 0:NT], r=[("gsb",)], w=[("grawd",)])
    p.barrier()
    with contextlib.ExitStack() as stk:
        beta2 = stk.enter_context(sb("d_bs2", [16, TT], F32))
        p.dma("sync", beta2[0:16, :], cx.dr["s_beta"][0:16, :],
              w=[("beta",)])
        if l0:
            g2 = stk.enter_context(sb("d_gs2", [16, TT], F32))
            csA = stk.enter_context(sb("d_csA", [16, TT], F32))
            p.dma("sync", g2[0:16, :], cx.dr["s_graw"][0:16, :],
                  w=[("gsb",)])
            cum = g2
            cur_key = ("gsb",)
            bufs = [csA, g2]
            for si, dlt_ in enumerate((1, 2, 4, 8, 16, 32, 64)):
                dst = bufs[si % 2]
                dk_ = ("cs", si)
                gv_in = cum[0:16, :].rearrange("p (c j) -> p c j", j=128)
                gv_out = dst[0:16, :].rearrange("p (c j) -> p c j", j=128)
                p.do("vector", lambda e, gi=gv_in, go=gv_out, d=dlt_:
                     e.tensor_add(go[:, :, d:], gi[:, :, d:],
                                  gi[:, :, 0:128 - d]),
                     r=[cur_key], w=[dk_])
                p.do("vector", lambda e, gi=gv_in, go=gv_out, d=dlt_:
                     e.tensor_copy(out=go[:, :, 0:d], in_=gi[:, :, 0:d]),
                     r=[cur_key], w=[dk_])
                cum = dst
                cur_key = dk_
            for cg in range(NCHK * 2):
                p.do("tensor", lambda e, cg=cg, cum=cum: e.transpose(
                    cx.ps[0][:, 0:16], cum[0:16, cg * 128:(cg + 1) * 128],
                    cx.id_f[0:16, 0:16]), r=[cur_key], w=[cx.PSK[0]])
                p.do("vector", lambda e, cg=cg: e.tensor_copy(
                    out=gcolT[:, cg * 16:(cg + 1) * 16],
                    in_=cx.ps[0][:, 0:16]), r=[cx.PSK[0]], w=[("gcolT",)])
            p.do("scalar", lambda e: e.activation(
                out=ecolT[:], in_=gcolT[:], func=AF.Exp),
                r=[("gcolT",)], w=[("ecolT",)])
            p.do("vector", lambda e: e.tensor_scalar(
                out=ncolT[:], in0=ecolT[:], scalar1=-1.0, scalar2=None,
                op0=ALU.mult), r=[("ecolT",)], w=[("ncolT",)])
            p.dma("scalar", cx.dr["s_g"], cum[0:16, :], r=[cur_key],
                  w=[("gd",)])
        for cg in range(NCHK * 2):
            p.do("tensor", lambda e, cg=cg: e.transpose(
                cx.ps[1][:, 0:16], beta2[0:16, cg * 128:(cg + 1) * 128],
                cx.id_f[0:16, 0:16]), r=[("beta",)], w=[cx.PSK[1]])
            p.do("vector", lambda e, cg=cg: e.tensor_copy(
                out=bcolT[:, cg * 16:(cg + 1) * 16],
                in_=cx.ps[1][:, 0:16]), r=[cx.PSK[1]], w=[("bcolT",)])
    p.barrier()
    emit_delta_core(p, cx, kind)
    p.barrier()
    with contextlib.ExitStack() as stk:
        KT = DV * 16 // 128
        wo = stk.enter_context(sb("d_wo", [128, KT * 1024], BF16))
        osb = stk.enter_context(sb("d_osb", [128, KT * NT], BF16))
        xt = stk.enter_context(sb("d_xt", [128, 8 * NT], F32))
        yt = stk.enter_context(sb("d_yt", [128, 8 * NT], F32))
        lnt = stk.enter_context(sb("d_lnt", [128, 8 * NT], BF16))
        won = "o_w" if l0 else "Wo"
        for k in range(KT):
            p.dma("gpsimd", wo[:, k * 1024:(k + 1) * 1024],
                  cx.dr[pre + won][k * 128:(k + 1) * 128, :],
                  w=[("wo", k)])
        for sc in range(NSC):
            p.dma("sync", osb.rearrange("p (j t) -> p j t", t=NT),
                  cx.dr["s_ofm"].rearrange("(j p) t -> p j t", p=128)[
                      :, 0:KT, sc * NT:(sc + 1) * NT],
                  r=[("od", "x")], w=[("osb", j) for j in range(KT)])
            if l0:
                p.dma("sync", lnt.rearrange("p (j t) -> p j t", t=NT),
                      cx.dr["s_ln"].rearrange("(j p) t -> p j t", p=128)[
                          :, :, sc * NT:(sc + 1) * NT],
                      r=[("lndram", "x")],
                      w=[("lnt", j) for j in range(8)])
            else:
                p.dma("sync", xt.rearrange("p (j t) -> p j t", t=NT),
                      x_in.rearrange("(j p) t -> p j t", p=128)[
                          :, :, sc * NT:(sc + 1) * NT],
                      r=[(xk, "dram")], w=[("xt", j) for j in range(8)])
            for dd in range(8):
                pk = cx.PSK[2 + dd % 2]
                ps = cx.ps[2 + dd % 2]
                ysl = slice(dd * NT, (dd + 1) * NT)
                for k in range(KT):
                    p.do("tensor", lambda e, dd=dd, k=k, ps=ps: e.matmul(
                        ps[:, 0:NT], lhsT=wo[:, k * 1024 + dd * 128:
                                             k * 1024 + (dd + 1) * 128],
                        rhs=osb[:, k * NT:(k + 1) * NT],
                        start=(k == 0), stop=(k == KT - 1)),
                        r=[("wo", k), ("osb", k)], w=[pk])
                if l0:
                    p.do("vector", lambda e, ps=ps, ysl=ysl, dd=dd:
                         e.tensor_add(
                             yt[:, ysl], ps[:, 0:NT],
                             lnt[:, dd * NT:(dd + 1) * NT]),
                         r=[pk, ("lnt", dd)], w=[("yt", dd)])
                else:
                    p.do("vector", lambda e, ps=ps, ysl=ysl: e.tensor_add(
                        yt[:, ysl], ps[:, 0:NT], xt[:, ysl]),
                        r=[pk, ("xt", dd)], w=[("yt", dd)])
            dstd = cx.dr["s_x1"] if l0 else x_out
            p.dma("scalar",
                  dstd.rearrange("(j p) t -> p j t", p=128)[
                      :, :, sc * NT:(sc + 1) * NT],
                  yt.rearrange("p (j t) -> p j t", t=NT),
                  r=[("yt", dd) for dd in range(8)],
                  w=[("x1d", "x") if l0 else (xok, "dram")])
        p.barrier()
    if not l0:
        ostk.close()
        return
    ostk.close()
    with contextlib.ExitStack() as stk:
        HF = 2752
        HTS = [128] * 21 + [64]
        wga = stk.enter_context(sb("d_wga", [128, 8 * HF], BF16))
        wup = stk.enter_context(sb("d_wup", [128, 8 * HF], BF16))
        wdn = stk.enter_context(sb("d_wdn", [128, 22 * 1024], BF16))
        pn_col = stk.enter_context(sb("d_pn", [128, 8], F32))
        xd = stk.enter_context(sb("d_xc2", [128, 8 * NT], F32))
        h2b = stk.enter_context(sb("d_h2", [128, 8 * NT], BF16))
        hbuf = stk.enter_context(sb("d_hbuf", [128, 22 * NT], BF16))
        yt3 = stk.enter_context(sb("d_yt2", [128, 8 * NT], F32))
        _alloc_ln_scratch(cx, stk, NT)
        p.dma("sync", pn_col[:], cx.dr["L0_attn_post_normc"], w=[("pnc",)])
        p.dma("gpsimd", wga.rearrange("p (k n) -> p k n", n=HF),
              cx.dr["L0_attn_gate_w"].rearrange("(k p) n -> p k n", p=128),
              w=[("wga", k) for k in range(8)])
        p.dma("gpsimd", wup.rearrange("p (k n) -> p k n", n=HF),
              cx.dr["L0_attn_up_w"].rearrange("(k p) n -> p k n", p=128),
              w=[("wup", k) for k in range(8)])
        for k in range(22):
            ks_ = HTS[k]
            p.dma("gpsimd", wdn[:, k * 1024:k * 1024 + 1024][0:ks_, :],
                  cx.dr["L0_attn_down_w"][k * 128:k * 128 + ks_, :],
                  w=[("wdn", k)])
        for sc in range(NSC):
            p.dma("sync", xd.rearrange("p (j t) -> p j t", t=NT),
                  cx.dr["s_x1"].rearrange("(j p) t -> p j t", p=128)[
                      :, :, sc * NT:(sc + 1) * NT],
                  r=[("x1d", "x")], w=[("xc", j) for j in range(8)])
            emit_rms_chunk(p, cx, "xc", xd, h2b, "h2", NT, pn_col, 1e-6)
            for hh in range(22):
                hs_ = HTS[hh]
                m0 = hh * 128
                pkA, psA = cx.PSK[2], cx.ps[2]
                pkB, psB = cx.PSK[3], cx.ps[3]
                for k in range(8):
                    p.do("tensor", lambda e, hh=hh, k=k, hs_=hs_, m0=m0:
                         e.matmul(
                             psA[:, 0:NT][0:hs_, :],
                             lhsT=wga[:, k * HF + m0:k * HF + m0 + hs_],
                             rhs=h2b[:, k * NT:(k + 1) * NT],
                             start=(k == 0), stop=(k == 7)),
                         r=[("wga", k), ("h2", k)], w=[pkA])
                for k in range(8):
                    p.do("tensor", lambda e, hh=hh, k=k, hs_=hs_, m0=m0:
                         e.matmul(
                             psB[:, 0:NT][0:hs_, :],
                             lhsT=wup[:, k * HF + m0:k * HF + m0 + hs_],
                             rhs=h2b[:, k * NT:(k + 1) * NT],
                             start=(k == 0), stop=(k == 7)),
                         r=[("wup", k), ("h2", k)], w=[pkB])
                p.do("scalar", lambda e, hh=hh, hs_=hs_: e.activation(
                    out=hbuf[:, hh * NT:(hh + 1) * NT][0:hs_, :],
                    in_=psA[:, 0:NT][0:hs_, :], func=AF.Silu),
                    r=[pkA], w=[("hsl", hh)])
                p.do("vector", lambda e, hh=hh, hs_=hs_: e.tensor_mul(
                    hbuf[:, hh * NT:(hh + 1) * NT][0:hs_, :],
                    hbuf[:, hh * NT:(hh + 1) * NT][0:hs_, :],
                    psB[:, 0:NT][0:hs_, :]),
                    r=[("hsl", hh), pkB], w=[("hb2", hh)])
            p.dma("sync", yt3.rearrange("p (j t) -> p j t", t=NT),
                  x_in.rearrange("(j p) t -> p j t", p=128)[
                      :, :, sc * NT:(sc + 1) * NT],
                  r=[(xk, "dram")], w=[("yt", j) for j in range(8)])
            for dd in range(8):
                pk = cx.PSK[4 + dd % 2]
                ps = cx.ps[4 + dd % 2]
                ysl = slice(dd * NT, (dd + 1) * NT)
                for k in range(22):
                    ks_ = HTS[k]
                    p.do("tensor", lambda e, dd=dd, k=k, ks_=ks_, ps=ps:
                         e.matmul(
                             ps[:, 0:NT],
                             lhsT=wdn[:, k * 1024 + dd * 128:
                                      k * 1024 + (dd + 1) * 128][0:ks_, :],
                             rhs=hbuf[:, k * NT:(k + 1) * NT][0:ks_, :],
                             start=(k == 0), stop=(k == 21)),
                         r=[("wdn", k), ("hb2", k)], w=[pk])
                p.do("vector", lambda e, dd=dd, ps=ps, ysl=ysl: e.tensor_add(
                    yt3[:, ysl], yt3[:, ysl], ps[:, 0:NT]),
                    r=[pk, ("yt", dd)], w=[("yt", dd)])
                p.do("vector", lambda e, ysl=ysl, dd=dd: e.tensor_add(
                    yt3[:, ysl], yt3[:, ysl], xd[:, dd * NT:(dd + 1) * NT]),
                    r=[("yt", dd), ("xc", dd)], w=[("yt", dd)])
            p.dma("scalar",
                  x_out.rearrange("(j p) t -> p j t", p=128)[
                      :, :, sc * NT:(sc + 1) * NT],
                  yt3.rearrange("p (j t) -> p j t", t=NT),
                  r=[("yt", dd) for dd in range(8)], w=[(xok, "dram")])
        p.barrier()



STAGES_ALL = ["l0f", "l0a", "l0f", "l1f", "l1a", "l1f",
              "l2f", "l2a", "l2f", "l3f", "l3a", "l3f"]
_STAGE_RANGE = None


def build_program(T, stages):
    nc = bass.Bass("TRN2", target_bir_lowering=False)
    cx = Ctx()
    cx.nc = nc
    cx.T = T
    cx.TT = 2 * T
    cx.NT = 512
    cx.NCH = cx.TT // cx.NT
    cx.dr = {}

    def din(name, shape):
        cx.dr[name] = nc.dram_tensor(name, shape, F32, kind="ExternalInput").ap()

    def dscr(name, shape, dtype=F32):
        cx.dr[name] = nc.dram_tensor(name, shape, dtype, kind="Internal").ap()

    din("x", [D, cx.TT])
    din("c_ident", [128, 128])
    cx.dr["out"] = nc.dram_tensor("out", [D, cx.TT], F32,
                                  kind="ExternalOutput").ap()
    dscr("s_xa", [D, cx.TT])
    dscr("s_xb", [D, cx.TT])
    if "l1f" in stages:
        din("L1_ffn_W1T", [1024, 4096])
        din("L1_ffn_W2T", [4096, 1024])
        din("L1_ffn_b1c", [128, 32])
        din("L1_ffn_b2c", [128, 8])
        dscr("s_wq1", [1024, 4096], BF16)
        dscr("s_wq2", [4096, 1024], BF16)
    if {"l0f", "l2f", "l3f"} & set(stages):
        dscr("s_ln", [D, cx.TT], BF16)
        dscr("s_y", [D, cx.TT], F32)
        dscr("s_ceT", [8, cx.TT], F32)
    if "l0f" in stages:
        din("L0_ffn_W1", [8, 1024, 4096])
        din("L0_ffn_W2", [8, 4096, 1024])
        din("L0_ffn_Wr", [1024, 8])
        din("L0_ffn_b1c", [128, 8 * 32])
        din("L0_ffn_b2t", [8, 1024])
    if "l3f" in stages:
        din("L3_ffn_routed_up", [8, 1024, 4096])
        din("L3_ffn_routed_down", [8, 4096, 1024])
        din("L3_ffn_shared_up", [1, 1024, 4096])
        din("L3_ffn_shared_down", [1, 4096, 1024])
        din("L3_ffn_router_w", [1024, 8])
    if {"l2a", "l1a", "l0a", "l3a"} & set(stages):
        dscr("s_p1", [2048, cx.TT], BF16)
        dscr("s_p2", [2048, cx.TT], BF16)
        dscr("s_p3", [2048, cx.TT], BF16)
        dscr("s_ofm", [2048, cx.TT], BF16)
        if "s_ln" not in cx.dr:
            dscr("s_ln", [D, cx.TT], BF16)
    if "l1a" in stages:
        din("L1_attn_W_in", [1024, 2048])
        din("L1_attn_W_xp", [1024, 1152])
        din("L1_attn_W_dt", [1024, 1024])
        din("L1_attn_b_dtc", [128, 8])
        din("L1_attn_A_logc", [128, 128])
        din("L1_attn_D_paramc", [128, 8])
        din("L1_attn_W_out", [1024, 1024])
        din("L1_attn_ln_gc", [128, 8])
        din("L1_attn_ln_bc", [128, 8])
        dscr("s_bc", [32, cx.TT])
    if {"l0a", "l3a"} & set(stages):
        din("c_mneg", [128, 128])
        din("c_m01s", [128, 128])
        din("c_m01i", [128, 128])
        din("c_mbd", [128, 128])
        din("c_mr", [128, 128])
        dscr("s_gtm", [cx.TT, 2048], BF16)
        dscr("s_beta", [16, cx.TT])
        dscr("s_graw", [16, cx.TT])
    if "l0a" in stages:
        for nm, shp in (("q_w", [1024, 1024]), ("k_w", [1024, 1024]),
                        ("v_w", [1024, 2048]), ("g_w", [1024, 2048]),
                        ("b_w", [1024, 16]), ("a_w", [1024, 16]),
                        ("A_log1", [16, 1]), ("dt_bias1", [16, 1]),
                        ("conv_q_wc", [64, 64]), ("conv_k_wc", [64, 64]),
                        ("conv_v_wc", [128, 64]), ("norm_wc", [128, 8]),
                        ("o_norm_w1", [1, 128]), ("o_w", [2048, 1024]),
                        ("post_normc", [128, 8]), ("gate_w", [1024, 2752]),
                        ("up_w", [1024, 2752]), ("down_w", [2752, 1024])):
            din("L0_attn_" + nm, shp)
        dscr("s_g", [16, cx.TT])
        dscr("s_x1", [D, cx.TT])
    if "l3a" in stages:
        for nm, shp in (("Wq", [1024, 1024]), ("Wk", [1024, 1024]),
                        ("Wv", [1024, 1024]), ("Wg", [1024, 1024]),
                        ("Wb", [1024, 16]), ("Wo", [1024, 1024]),
                        ("cq_wc", [64, 64]), ("ck_wc", [64, 64]),
                        ("cv_wc", [64, 64]), ("cq_bc", [64, 16]),
                        ("ck_bc", [64, 16]), ("cv_bc", [64, 16]),
                        ("nw1", [1, 64])):
            din("L3_attn_" + nm, shp)
    if "l2a" in stages:
        din("L2_attn_w_qkv", [1024, 3072])
        din("L2_attn_w_o", [1024, 1024])
        din("L2_attn_log_xi2", [1, 16])
        din("c_swp", [128, 128])
        din("c_ropec", [128, cx.TT])
        din("c_ropes", [128, cx.TT])
        din("c_kposc", [128, 16])
        din("c_dm2t", [128, 2048])
        din("c_ebits", [1, 16])
        dscr("s_vtm", [cx.TT, 1024], BF16)
    if "l2f" in stages:
        din("L2_ffn_ew1T", [6, 1024, 1024])
        din("L2_ffn_ew2T", [6, 1024, 1024])
        din("L2_ffn_ew3T", [6, 1024, 1024])
        din("L2_ffn_fc1T", [1024, 1024])
        din("L2_ffn_fc2T", [1024, 1024])
        din("L2_ffn_fc3T", [1024, 1024])
        din("L2_ffn_gate_wT", [1024, 8])
        din("L2_ffn_gate_b2", [1, 6])

    p = P()
    with contextlib.ExitStack() as stk:
        alloc_psum_pool(cx, stk)
        emit_shared_consts(cx, p, stk)
        bufs = ["s_xa", "s_xb"]
        cur, curk = cx.dr["x"], "x"
        seen_l1f = False
        for i, s in enumerate(stages):
            last = (i == len(stages) - 1)
            nxt = cx.dr["out"] if last else cx.dr[bufs[i % 2]]
            nxtk = "out" if last else bufs[i % 2]
            if s == "l1f":
                emit_l1ffn(p, cx, cur, curk, nxt, nxtk, first=not seen_l1f)
                seen_l1f = True
            elif s in ("l0f", "l2f", "l3f"):
                emit_moe(p, cx, cur, curk, nxt, nxtk, s[:2])
            elif s == "l2a":
                emit_l2attn(p, cx, cur, curk, nxt, nxtk)
            elif s == "l1a":
                emit_l1attn(p, cx, cur, curk, nxt, nxtk)
            elif s in ("l0a", "l3a"):
                emit_delta(p, cx, cur, curk, nxt, nxtk, s[:2])
            else:
                raise NotImplementedError(s)
            cur, curk = nxt, nxtk
        p.barrier()
    p.emit(nc)
    return nc, cx


def _host_consts(T=2048, stages=()):
    c = {"c_ident": np.eye(128, dtype=np.float32)}
    if {"l0a", "l3a"} & set(stages):
        r = np.arange(128)[:, None]
        s = np.arange(128)[None, :]
        c["c_mneg"] = np.where(r < s, 0.0, NEG).astype(np.float32)
        c["c_m01s"] = (r < s).astype(np.float32)
        c["c_m01i"] = (r <= s).astype(np.float32)
        c["c_mbd"] = (((r // 16) == (s // 16)) & (r < s)).astype(np.float32)
        c["c_mr"] = ((r // 16) < (s // 16)).astype(np.float32)
    if "l2a" in stages:
        swp = np.zeros((128, 128), np.float32)
        for i in range(64):
            swp[2 * i, 2 * i + 1] = 1.0
            swp[2 * i + 1, 2 * i] = 1.0
        c["c_swp"] = swp
        p = np.arange(128)
        i_of_p = (p % 64) // 2
        invf = 10000.0 ** (-(np.arange(0, 64, 2, dtype=np.float64) / 64))
        t = np.arange(T, dtype=np.float64)
        theta = t[None, :] * invf[i_of_p][:, None]
        cos = np.cos(theta).astype(np.float32)
        sin = np.sin(theta).astype(np.float32)
        sgn = np.where(p % 2 == 0, -1.0, 1.0).astype(np.float32)[:, None]
        c["c_ropec"] = np.ascontiguousarray(np.tile(cos, (1, 2)))
        c["c_ropes"] = np.ascontiguousarray(np.tile(sin * sgn, (1, 2)))
        kposc = (np.arange(16)[None, :] * 128 +
                 np.arange(128)[:, None]).astype(np.float32)
        c["c_kposc"] = np.ascontiguousarray(kposc)
        dm2t = np.zeros((128, 4, 512), np.float32)
        for v in range(4):
            pp = np.arange(128)[:, None]
            jj = np.arange(512)[None, :]
            dm2t[:, v, :] = np.where(128 * v + pp <= jj, 0.0, NEG)
        c["c_dm2t"] = np.ascontiguousarray(dm2t.reshape(128, 2048))
        c["c_ebits"] = (-(np.arange(16) % 2)).astype(np.float32).reshape(1, 16)
    return c


def _prep_weights(inputs, stages):
    A = lambda v: np.ascontiguousarray(np.asarray(v))
    w = {}
    if "l0f" in stages:
        w["L0_ffn_W1"] = A(inputs["L0_ffn_W1"])
        w["L0_ffn_W2"] = A(inputs["L0_ffn_W2"])
        w["L0_ffn_Wr"] = A(inputs["L0_ffn_Wr"])
        w["L0_ffn_b1c"] = A(np.asarray(inputs["L0_ffn_b1"]).reshape(
            8, 32, 128).transpose(2, 0, 1).reshape(128, 256))
        w["L0_ffn_b2t"] = A(inputs["L0_ffn_b2"])
    if "l3f" in stages:
        for k in ("routed_up", "routed_down", "shared_up", "shared_down",
                  "router_w"):
            w["L3_ffn_" + k] = A(inputs["L3_ffn_" + k])
    def ccol(v, P):
        v = np.asarray(v)
        nt = v.shape[0] // P
        return A(v.reshape(nt, P, v.shape[1]).transpose(1, 0, 2).reshape(
            P, nt * v.shape[1]))

    if "l0a" in stages:
        for nm in ("q_w", "k_w", "v_w", "g_w", "b_w", "a_w", "o_w",
                   "gate_w", "up_w", "down_w"):
            w["L0_attn_" + nm] = A(inputs["L0_attn_" + nm])
        w["L0_attn_A_log1"] = A(np.asarray(
            inputs["L0_attn_A_log"]).reshape(16, 1))
        w["L0_attn_dt_bias1"] = A(np.asarray(
            inputs["L0_attn_dt_bias"]).reshape(16, 1))
        w["L0_attn_conv_q_wc"] = ccol(inputs["L0_attn_conv_q_w"], 64)
        w["L0_attn_conv_k_wc"] = ccol(inputs["L0_attn_conv_k_w"], 64)
        w["L0_attn_conv_v_wc"] = ccol(inputs["L0_attn_conv_v_w"], 128)
        w["L0_attn_norm_wc"] = A(np.asarray(
            inputs["L0_attn_norm_w"]).reshape(8, 128).T)
        w["L0_attn_post_normc"] = A(np.asarray(
            inputs["L0_attn_post_norm_w"]).reshape(8, 128).T)
        w["L0_attn_o_norm_w1"] = A(np.asarray(
            inputs["L0_attn_o_norm_w"]).reshape(1, 128))
    if "l3a" in stages:
        for nm in ("Wq", "Wk", "Wv", "Wg", "Wb", "Wo"):
            w["L3_attn_" + nm] = A(inputs["L3_attn_" + nm])
        for nm in ("cq_w", "ck_w", "cv_w"):
            w["L3_attn_" + nm + "c"] = ccol(inputs["L3_attn_" + nm], 64)
        for nm in ("cq_b", "ck_b", "cv_b"):
            w["L3_attn_" + nm + "c"] = A(np.asarray(
                inputs["L3_attn_" + nm]).reshape(16, 64).T)
        w["L3_attn_nw1"] = A(np.asarray(inputs["L3_attn_nw"]).reshape(1, 64))
    if "l1a" in stages:
        w["L1_attn_W_in"] = A(inputs["L1_attn_W_in"])
        wx = np.asarray(inputs["L1_attn_W_x"])
        w["L1_attn_W_xp"] = A(np.concatenate(
            [wx, np.zeros((1024, 1152 - wx.shape[1]), np.float32)], axis=1))
        w["L1_attn_W_dt"] = A(inputs["L1_attn_W_dt"])
        w["L1_attn_b_dtc"] = A(
            np.asarray(inputs["L1_attn_b_dt"]).reshape(8, 128).T)
        w["L1_attn_A_logc"] = A(np.asarray(inputs["L1_attn_A_log"]).reshape(
            8, 128, 16).transpose(1, 0, 2).reshape(128, 128))
        w["L1_attn_D_paramc"] = A(
            np.asarray(inputs["L1_attn_D_param"]).reshape(8, 128).T)
        w["L1_attn_W_out"] = A(inputs["L1_attn_W_out"])
        w["L1_attn_ln_gc"] = A(
            np.asarray(inputs["L1_attn_ln_g"]).reshape(8, 128).T)
        w["L1_attn_ln_bc"] = A(
            np.asarray(inputs["L1_attn_ln_b"]).reshape(8, 128).T)
    if "l2a" in stages:
        w["L2_attn_w_qkv"] = A(inputs["L2_attn_w_qkv"])
        w["L2_attn_w_o"] = A(inputs["L2_attn_w_o"])
        w["L2_attn_log_xi2"] = A(
            np.asarray(inputs["L2_attn_log_xi"]).reshape(1, 16))
    if "l2f" in stages:
        for k in ("ew1", "ew2", "ew3"):
            w["L2_ffn_" + k + "T"] = A(
                np.asarray(inputs["L2_ffn_" + k]).transpose(0, 2, 1))
        for k in ("fc1", "fc2", "fc3"):
            w["L2_ffn_" + k + "T"] = A(np.asarray(inputs["L2_ffn_" + k]).T)
        w["L2_ffn_gate_wT"] = A(
            np.concatenate([np.asarray(inputs["L2_ffn_gate_w"]).T,
                            np.zeros((1024, 2), np.float32)], axis=1))
        w["L2_ffn_gate_b2"] = A(
            np.asarray(inputs["L2_ffn_gate_b"]).reshape(1, 6))
    if "l1f" in stages:
        w["L1_ffn_W1T"] = np.ascontiguousarray(inputs["L1_ffn_W1"].T)
        w["L1_ffn_W2T"] = np.ascontiguousarray(inputs["L1_ffn_W2"].T)
        w["L1_ffn_b1c"] = np.ascontiguousarray(
            np.asarray(inputs["L1_ffn_b1"]).reshape(32, 128).T)
        w["L1_ffn_b2c"] = np.ascontiguousarray(
            np.asarray(inputs["L1_ffn_b2"]).reshape(8, 128).T)
    return w


def kernel(**inputs):
    T = 2048
    x = np.asarray(inputs["L0_attn_hidden_states"])
    stages = STAGES_ALL if _STAGE_RANGE is None else \
        STAGES_ALL[_STAGE_RANGE[0]:_STAGE_RANGE[1]]
    nc, cx = build_program(T, stages)
    w = _prep_weights(inputs, stages)
    w.update(_host_consts(T, stages))
    in_maps = []
    for c in range(8):
        m = dict(w)
        xc = x[2 * c:2 * c + 2]
        m["x"] = np.ascontiguousarray(xc.transpose(2, 0, 1).reshape(D, 2 * T))
        in_maps.append(m)
    res = run_bass_kernel_spmd(nc, in_maps, core_ids=list(range(8)))
    outs = []
    for c in range(8):
        o = res.results[c]["out"]
        outs.append(o.reshape(D, 2, T).transpose(1, 2, 0))
    return np.concatenate(outs, axis=0).astype(np.float32)
